# version 11
# baseline (speedup 1.0000x reference)
import math
import numpy as np
import ml_dtypes
from contextlib import ExitStack

import concourse.bass as bass
import concourse.mybir as mybir
from concourse.bass_utils import run_bass_kernel_spmd

F32 = mybir.dt.float32
BF16 = mybir.dt.bfloat16
OP = mybir.AluOpType
AF = mybir.ActivationFunctionType
AX = mybir.AxisListType

B, T, D = 8, 2048, 4096
NH, NKV, DH = 32, 8, 128
NT = T // 128
NC_ = D // 128
INV_SQRT_DH = 1.0 / math.sqrt(DH)
EXP_SHIFT = -12.0
LN_EPS = 1e-5
RMS_EPS = 1.1920929e-07
TAU_INV = 10.0

_DEV = False


def _rope_tables():
    idx = np.arange(0, DH, 2, dtype=np.float64) / DH
    freq = 1.0 / (2.0 * math.pi) * (500000.0 ** (-idx))
    factor, lo, hi, L0 = 32.0, 1.0, 4.0, 8192.0
    freq_low, freq_high = lo / L0, hi / L0
    freq_scaled = np.where(freq < freq_low, freq / factor, freq)
    smooth = np.clip((L0 * freq - lo) / (hi - lo), 0.0, 1.0)
    freq_smooth = (1.0 - smooth) * (freq / factor) + smooth * freq
    is_mid = (freq >= freq_low) & (freq <= freq_high)
    freq = np.where(is_mid, freq_smooth, freq_scaled)
    phase = 2.0 * math.pi * np.arange(T, dtype=np.float64)[:, None] * freq[None, :]
    emb = np.concatenate([phase, phase], axis=-1)
    cos = np.ascontiguousarray(np.cos(emb).astype(np.float32).T)
    sin = np.ascontiguousarray(np.sin(emb).astype(np.float32).T)
    return cos, sin


def _rot_lhsT():
    r = np.zeros((DH, DH), dtype=np.float32)
    for m in range(64):
        r[m + 64, m] = -1.0
    for m in range(64, 128):
        r[m - 64, m] = 1.0
    return r.astype(ml_dtypes.bfloat16)


def _tri_mask():
    m = (np.arange(128)[:, None] <= np.arange(128)[None, :]).astype(np.float32)
    return m.astype(ml_dtypes.bfloat16)


class Ctr:

    def __init__(self, stack, nc, name):
        self.h = stack.enter_context(nc.semaphore(name))
        self.v = 0

    def inc(self, instr, n=1):
        instr.then_inc(self.h, n)
        self.v += n
        return self.v


def _cells(off):
    if off < 512:
        return [(off, 512), (512, 1024)]
    return [(off, 1024)]


def build_nc():
    nc = bass.Bass("TRN2", target_bir_lowering=False, debug=False)

    x_d = nc.dram_tensor("x", [T, D], F32, kind="ExternalInput").ap()
    wq_d = nc.dram_tensor("wq", [D, D], F32, kind="ExternalInput").ap()
    wk_d = nc.dram_tensor("wk", [D, NKV * DH], F32, kind="ExternalInput").ap()
    wv_d = nc.dram_tensor("wv", [D, NKV * DH], F32, kind="ExternalInput").ap()
    wo_d = nc.dram_tensor("wo", [D, D], F32, kind="ExternalInput").ap()
    wb_d = nc.dram_tensor("wb", [1024, 1024], F32, kind="ExternalInput").ap()
    wrppT_d = nc.dram_tensor("wrppT", [D, 24], F32, kind="ExternalInput").ap()
    base24_d = nc.dram_tensor("base24", [1, 24], F32, kind="ExternalInput").ap()
    delta16_d = nc.dram_tensor("delta16", [1, 16], F32, kind="ExternalInput").ap()
    gate_d = nc.dram_tensor("gate", [1, 1], F32, kind="ExternalInput").ap()
    cos_d = nc.dram_tensor("cos_fm", [DH, T], F32, kind="ExternalInput").ap()
    sin_d = nc.dram_tensor("sin_fm", [DH, T], F32, kind="ExternalInput").ap()
    ident_d = nc.dram_tensor("ident", [128, 128], BF16, kind="ExternalInput").ap()
    rmat_d = nc.dram_tensor("rmat", [128, 128], BF16, kind="ExternalInput").ap()
    tri_d = nc.dram_tensor("tri", [128, 128], BF16, kind="ExternalInput").ap()

    out_d = nc.dram_tensor("out", [T, D], F32, kind="ExternalOutput").ap()

    scratch_kind = "ExternalOutput" if _DEV else "Internal"
    qs_d = nc.dram_tensor("qs", [NH * DH, T], BF16, kind=scratch_kind).ap()
    ks_d = nc.dram_tensor("ks", [NKV * DH, T], BF16, kind=scratch_kind).ap()
    vs_d = nc.dram_tensor("vs", [T, NKV * DH], BF16, kind=scratch_kind).ap()
    ss_d = nc.dram_tensor("ss", [T, D], F32, kind=scratch_kind).ap()
    xns_d = nc.dram_tensor("xns", [T, D], BF16, kind=scratch_kind).ap()
    gs_d2 = nc.dram_tensor("gsp", [T, D], F32, kind=scratch_kind).ap()
    ys_d = nc.dram_tensor("ys", [T, 1024], BF16, kind=scratch_kind).ap()

    with ExitStack() as st:
        e = st.enter_context

        env = {}
        env.update(x_d=x_d, wq_d=wq_d, wk_d=wk_d, wv_d=wv_d, wo_d=wo_d, wb_d=wb_d,
                  wrppT_d=wrppT_d, base24_d=base24_d, delta16_d=delta16_d,
                  gate_d=gate_d, cos_d=cos_d, sin_d=sin_d, ident_d=ident_d,
                  rmat_d=rmat_d, tri_d=tri_d, out_d=out_d, qs_d=qs_d, ks_d=ks_d,
                  vs_d=vs_d, ss_d=ss_d, xns_d=xns_d, gs_d2=gs_d2, ys_d=ys_d)

        env["ident"] = e(nc.sbuf_tensor("identb", [128, 128], BF16))
        env["rmat"] = e(nc.sbuf_tensor("rmatb", [128, 128], BF16))
        env["tri"] = e(nc.sbuf_tensor("trib", [128, 128], BF16))
        env["wrppT"] = e(nc.sbuf_tensor("wrppTb", [128, NC_ * 24], BF16))
        env["base_bc"] = e(nc.sbuf_tensor("base_bc", [128, 24], F32))
        env["i1mg"] = e(nc.sbuf_tensor("i1mg", [128, 16], F32))
        env["gcol"] = e(nc.sbuf_tensor("gcol", [128, 1], F32))
        env["ones_bf"] = e(nc.sbuf_tensor("ones_bf", [128, 1], BF16))
        env["ones_f1"] = e(nc.sbuf_tensor("ones_f1", [1, 128], F32))
        env["shiftc"] = e(nc.sbuf_tensor("shiftc", [128, 1], F32))
        env["m_st"] = e(nc.sbuf_tensor("m_st", [128, NT], F32))
        env["rstd_st"] = e(nc.sbuf_tensor("rstd_st", [128, NT], F32))
        env["rstd2_st"] = e(nc.sbuf_tensor("rstd2_st", [128, NT], F32))
        env["tmp_st"] = e(nc.sbuf_tensor("tmp_st", [128, 8 * NT], F32))
        env["lbuf"] = e(nc.sbuf_tensor("lbuf", [128, NT * 24], F32))
        env["hres"] = e(nc.sbuf_tensor("hres", [128, NT * 16], F32))
        env["hpre"] = e(nc.sbuf_tensor("hpre", [128, NT * 4], F32))
        env["hpost"] = e(nc.sbuf_tensor("hpost", [128, NT * 4], F32))

        for nm in ("ds", "gs", "sp", "pe", "ve", "ac", "po", "os_", "sf",
                   "x0", "x1", "w0", "w1", "k0", "k1", "q0", "q1",
                   "l0", "l1", "o0", "o1", "sg", "s0", "s1", "g0", "g1", "sv"):
            env[nm] = Ctr(st, nc, nm)

        with nc.sbuf_tensor("xn_fm", [128, NC_ * T], BF16) as xn_fm, \
             nc.sbuf_tensor("cosb", [128, T], F32) as cosb, \
             nc.sbuf_tensor("sinb", [128, T], F32) as sinb, \
             nc.sbuf_tensor("psink", [128, NT * 16], F32) as psink, \
             nc.sbuf_tensor("mxs", [128, NT], F32) as mxs, \
             nc.sbuf_tensor("avs", [128, NT * 4], F32) as avs, \
             nc.sbuf_tensor("bvs", [128, NT * 4], F32) as bvs, \
             nc.sbuf_tensor("tts", [128, NT * 16], F32) as tts:
            env["xn_fm"] = xn_fm
            env["cosb"] = cosb
            env["sinb"] = sinb
            env.update(psink=psink, mxs=mxs, avs=avs, bvs=bvs, tts=tts)
            _p0(nc, env)
            _p1(nc, env)
            _p1b_setup(nc, env)
            _p2(nc, env)
            _p3(nc, env)
            _p1b_final(nc, env)
        _p35(nc, env)
        with nc.sbuf_tensor("ctx", [128, NH * T], BF16) as ctx:
            env["ctx"] = ctx
            _p4(nc, env)
            _p5(nc, env)

        with nc.Block() as blk:
            @blk.sync
            def _(sync):
                for nm in ("o0", "o1", "sp", "ds", "gs", "x0", "x1", "w0", "w1",
                           "k0", "k1", "q0", "q1", "l0", "l1", "s0", "s1", "g0", "g1"):
                    c = env[nm]
                    sync.wait_ge(c.h, c.v)

    return nc


def _p0(nc, env):
    g = env.__getitem__
    ds, gs, pe, ve, ac = g("ds"), g("gs"), g("pe"), g("ve"), g("ac")

    with nc.psum_tensor("p0ps", [128, 512], F32) as p0ps, \
         nc.sbuf_tensor("g11", [1, 2], F32) as g11, \
         nc.sbuf_tensor("gate_sb", [1, 1], F32) as gate_sb, \
         nc.sbuf_tensor("b24row", [1, 24], F32) as b24row, \
         nc.sbuf_tensor("d16row", [1, 16], F32) as d16row, \
         nc.sbuf_tensor("omg", [128, 1], F32) as omg, \
         nc.Block() as blk:

        @blk.sync
        def _(sync):
            for dst, src in ((g("cosb")[:], g("cos_d")), (g("sinb")[:], g("sin_d")),
                             (g("ident")[:], g("ident_d")), (g("rmat")[:], g("rmat_d")),
                             (g("tri")[:], g("tri_d")), (gate_sb[:], g("gate_d")),
                             (b24row[:], g("base24_d")), (d16row[:], g("delta16_d"))):
                sync.dma_start(dst, src).then_inc(ds.h, 16)
            ds.v += 8 * 16
            sync.wait_ge(ds.h, ds.v)

        @blk.gpsimd
        def _(p):
            p.dma_start(g("wrppT")[:].rearrange("p (c e) -> p c e", e=24),
                        g("wrppT_d").rearrange("(c p) e -> p c e", p=128)).then_inc(gs.h, 16)
            gs.v += 16
            p.wait_ge(gs.h, gs.v)

        @blk.vector
        def _(v):
            v.memset(g("ones_bf")[:], 1.0)
            v.memset(g("shiftc")[:], EXP_SHIFT)
            ve.inc(v.memset(g("ones_f1")[:], 1.0))

        @blk.scalar
        def _(s):
            s.wait_ge(ds.h, ds.v)
            s.activation(g11[:, 0:1], gate_sb[:], AF.Sigmoid)
            ac.inc(s.activation(g11[:, 1:2], gate_sb[:], AF.Sigmoid, scale=-1.0))

        @blk.tensor
        def _(t):
            t.wait_ge(ac.h, ac.v)
            t.wait_ge(ve.h, ve.v)
            t.matmul(p0ps[:, 0:24], g("ones_f1")[:], b24row[:], start=True, stop=True)
            t.matmul(p0ps[:, 24:40], g("ones_f1")[:], d16row[:], start=True, stop=True)
            t.matmul(p0ps[:, 40:41], g("ones_f1")[:], g11[:, 0:1], start=True, stop=True)
            pe.inc(t.matmul(p0ps[:, 41:42], g("ones_f1")[:], g11[:, 1:2],
                            start=True, stop=True))

        @blk.vector
        def _(v):
            sf = g("sf")
            v.wait_ge(pe.h, pe.v)
            v.tensor_copy(g("base_bc")[:], p0ps[:, 0:24])
            v.tensor_copy(g("gcol")[:], p0ps[:, 40:41])
            sf.inc(v.tensor_copy(omg[:], p0ps[:, 41:42]))
            v.wait_ge(sf.h, sf.v)
            ve.inc(v.tensor_scalar(g("i1mg")[:], p0ps[:, 24:40], omg[:], None, OP.mult))


def _p1(nc, env):
    g = env.__getitem__
    x_d, xn_fm, wrppT = g("x_d"), g("xn_fm"), g("wrppT")
    ident, base_bc = g("ident"), g("base_bc")
    m_st, rstd_st, rstd2_st, tmp_st, lbuf = (g("m_st"), g("rstd_st"), g("rstd2_st"),
                                             g("tmp_st"), g("lbuf"))
    ds, pe, ve, ac, sf = g("ds"), g("pe"), g("ve"), g("ac"), g("sf")
    xl = (g("x0"), g("x1"))
    xsp = (g("s0"), g("s1"))
    xns_d = g("xns_d")

    VE_T, AC_T, PE_T = 19, 18, 33

    with nc.sbuf_tensor("xb", [128, 2 * D], F32) as xb, \
         nc.sbuf_tensor("xnb", [128, 2 * D], BF16) as xnb, \
         nc.sbuf_tensor("sqdump", [128, 2048], BF16) as sqdump, \
         nc.psum_tensor("ptr", [128, 2 * 1024], BF16) as ptr, \
         nc.psum_tensor("plog", [128, 512], F32) as plog, \
         nc.Block() as blk:

        ds0, ve0, ac0, pe0 = ds.v, ve.v, ac.v, pe.v
        xl0 = (xl[0].v, xl[1].v)
        xsp0 = (xsp[0].v, xsp[1].v)

        def xl_tgt(i):
            return xl0[i % 2] + (i // 2 + 1) * 16

        def ve_evac(i, c):
            return ve0 + i * VE_T + 2 + (c + 1) // 2

        def ac_evac(i, c):
            return ac0 + i * AC_T + 2 + c // 2 + 1

        @blk.sync
        def _(sync):
            for i in range(NT):
                if i >= 2:
                    sync.wait_ge(ve.h, ve0 + (i - 2) * VE_T + 2)
                    sync.wait_ge(ac.h, ac0 + (i - 2) * AC_T + 1)
                sync.dma_start(xb[:, (i % 2) * D:(i % 2 + 1) * D],
                               x_d.rearrange("(n p) d -> n p d", p=128)[i]
                               ).then_inc(xl[i % 2].h, 16)
                xl[i % 2].v += 16

        @blk.vector
        def _(v):
            for i in range(NT):
                xt = xb[:, (i % 2) * D:(i % 2 + 1) * D]
                c0 = 8 * i
                sums = tmp_st[:, c0:c0 + 1]
                varp = tmp_st[:, c0 + 1:c0 + 2]
                varpe = tmp_st[:, c0 + 2:c0 + 3]
                t1 = tmp_st[:, c0 + 3:c0 + 4]
                sumsq = tmp_st[:, c0 + 4:c0 + 5]
                ue = tmp_st[:, c0 + 5:c0 + 6]
                def _e(instr):
                    sf.inc(instr)
                    v.wait_ge(sf.h, sf.v)
                v.wait_ge(xl[i % 2].h, xl_tgt(i))
                _e(v.reduce_sum(sums, xt, axis=AX.X))
                v.wait_ge(ac.h, ac0 + i * AC_T + 1)
                _e(v.tensor_scalar(m_st[:, i:i + 1], sums, 1.0 / D, None, OP.mult))
                _e(v.tensor_scalar(t1, m_st[:, i:i + 1], m_st[:, i:i + 1], -1.0,
                                   OP.mult, OP.mult))
                _e(v.scalar_tensor_tensor(varp, tmp_st[:, c0 + 6:c0 + 7], 1.0 / D, t1,
                                          OP.mult, OP.add))
                _e(v.scalar_tensor_tensor(varp, sumsq, 1.0 / D, varp, OP.mult, OP.add))
                _e(v.tensor_scalar(varpe, varp, LN_EPS, None, OP.add))
                _e(v.reciprocal(t1, varpe))
                _e(v.scalar_tensor_tensor(ue, varp, 1.0, t1, OP.mult, OP.mult))
                ve.inc(v.tensor_scalar(ue, ue, RMS_EPS, None, OP.add))
                v.wait_ge(ac.h, ac0 + i * AC_T + 2)
                _e(v.reciprocal(rstd_st[:, i:i + 1], t1))
                _e(v.reciprocal(rstd2_st[:, i:i + 1], ue))
                if i >= 2:
                    v.wait_ge(pe.h, pe0 + (i - 2) * PE_T + NC_)
                    v.wait_ge(xsp[i % 2].h, xsp0[i % 2] + ((i - 2) // 2 + 1) * 16)
                ve.inc(v.tensor_scalar(xnb[:, (i % 2) * D:(i % 2 + 1) * D], xt,
                                       m_st[:, i:i + 1], rstd_st[:, i:i + 1],
                                       OP.subtract, OP.mult))
                for c in range(1, NC_, 2):
                    v.wait_ge(pe.h, pe0 + i * PE_T + c + 1)
                    ve.inc(v.tensor_copy(
                        xn_fm[:, c * T + i * 128: c * T + (i + 1) * 128],
                        ptr[:, (c % 2) * 1024:(c % 2) * 1024 + 128]))
                v.wait_ge(pe.h, pe0 + (i + 1) * PE_T)
                ve.inc(v.scalar_tensor_tensor(lbuf[:, i * 24:(i + 1) * 24], plog[:, 0:24],
                                              rstd2_st[:, i:i + 1], base_bc[:],
                                              OP.mult, OP.add))

        @blk.scalar
        def _(s):
            for i in range(NT):
                xt = xb[:, (i % 2) * D:(i % 2 + 1) * D]
                c0 = 8 * i
                s.wait_ge(xl[i % 2].h, xl_tgt(i))
                s.activation(sqdump[:], xt[:, 0:2048], AF.Square,
                             accum_out=tmp_st[:, c0 + 6:c0 + 7])
                ac.inc(s.activation(sqdump[:], xt[:, 2048:4096], AF.Square,
                                    accum_out=tmp_st[:, c0 + 4:c0 + 5]))
                s.wait_ge(ve.h, ve0 + i * VE_T + 1)
                s.activation(tmp_st[:, c0 + 3:c0 + 4], tmp_st[:, c0 + 2:c0 + 3], AF.Sqrt)
                ac.inc(s.activation(tmp_st[:, c0 + 5:c0 + 6],
                                    tmp_st[:, c0 + 5:c0 + 6], AF.Sqrt))
                for c in range(0, NC_, 2):
                    s.wait_ge(pe.h, pe0 + i * PE_T + c + 1)
                    ac.inc(s.copy(
                        xn_fm[:, c * T + i * 128: c * T + (i + 1) * 128],
                        ptr[:, (c % 2) * 1024:(c % 2) * 1024 + 128]))
                s.dma_start(xns_d.rearrange("(n p) d -> n p d", p=128)[i],
                            xnb[:, (i % 2) * D:(i % 2 + 1) * D]).then_inc(xsp[i % 2].h, 16)
                xsp[i % 2].v += 16

        @blk.tensor
        def _(t):
            for i in range(NT):
                t.wait_ge(ve.h, ve0 + i * VE_T + 2)
                for c in range(NC_):
                    if c >= 2:
                        pc = c - 2
                        if pc % 2 == 1:
                            t.wait_ge(ve.h, ve_evac(i, pc))
                        else:
                            t.wait_ge(ac.h, ac_evac(i, pc))
                    elif i > 0:
                        pc = NC_ - 2 + c
                        if pc % 2 == 1:
                            t.wait_ge(ve.h, ve_evac(i - 1, pc))
                        else:
                            t.wait_ge(ac.h, ac_evac(i - 1, pc))
                    pe.inc(t.transpose(ptr[:, (c % 2) * 1024:(c % 2) * 1024 + 128],
                                       xnb[:, (i % 2) * D + c * 128:
                                           (i % 2) * D + (c + 1) * 128],
                                       ident[:]))
                t.wait_ge(ve.h, ve0 + i * VE_T + 18)
                t.wait_ge(ac.h, ac0 + i * AC_T + 18)
                if i > 0:
                    t.wait_ge(ve.h, ve0 + i * VE_T)
                mm = None
                for c in range(NC_):
                    mm = t.matmul(plog[:, 0:24],
                                  xn_fm[:, c * T + i * 128: c * T + (i + 1) * 128],
                                  wrppT[:, c * 24:(c + 1) * 24],
                                  start=(c == 0), stop=(c == NC_ - 1))
                pe.inc(mm)

        assert ve.v == ve0 + NT * VE_T, (ve.v, ve0)
        assert ac.v == ac0 + NT * AC_T
        assert pe.v == pe0 + NT * PE_T


def _sink_views(env):
    G = NT
    psink, tts = env["psink"], env["tts"]
    avs, bvs = env["avs"], env["bvs"]
    return dict(
        pv4=psink[:].rearrange("p (g i j) -> p g i j", i=4, j=4),
        ttv=tts[:].rearrange("p (g i j) -> p g i j", i=4, j=4),
        ttv_t=tts[:].rearrange("p (g i j) -> p g j i", i=4, j=4),
        bRep=bvs[:].rearrange("p (g j) -> p g j", j=4).unsqueeze(2)
                   .broadcast_to([128, G, 4, 4]),
        aRep=avs[:].rearrange("p (g i) -> p g i", i=4).unsqueeze(3)
                   .broadcast_to([128, G, 4, 4]),
        av_v=avs[:].rearrange("p (g i) -> p g i", i=4),
        bv_v=bvs[:].rearrange("p (g j) -> p g j", j=4),
    )


def _p1b_setup(nc, env):
    g = env.__getitem__
    lbuf = g("lbuf")
    ve, ac, sf, po, sv = g("ve"), g("ac"), g("sf"), g("po"), g("sv")
    psink, mxs, bvs = g("psink"), g("mxs"), g("bvs")
    V = _sink_views(env)
    G = NT

    with nc.sbuf_tensor("z10", [128, NT * 16], F32) as z10, \
         nc.Block() as blk:
        z10v = z10[:].rearrange("p (g e) -> p g e", e=16)
        res_view = lbuf[:].rearrange("p (g e) -> p g e", e=24)[:, :, 0:16]

        @blk.vector
        def _(v):
            sf.inc(v.tensor_scalar(z10v, res_view, TAU_INV, None, OP.mult))
            v.wait_ge(sf.h, sf.v)
            sf.inc(v.reduce_max(mxs[:], z10v, axis=AX.X))
            v.wait_ge(sf.h, sf.v)
            ve.inc(v.tensor_tensor(
                z10v, z10v,
                mxs[:].rearrange("p (g o) -> p g o", o=1).broadcast_to([128, G, 16]),
                OP.subtract))

        @blk.scalar
        def _(s):
            s.wait_ge(ve.h, ve.v)
            ac.inc(s.activation(psink[:], z10[:], AF.Exp))

        @blk.gpsimd
        def _(p):
            sg = g("sg")
            sg.inc(p.memset(bvs[:], 1.0))
            p.wait_ge(sg.h, sg.v)
            p.wait_ge(ac.h, ac.v)
            po.inc(p.engine_nop())

    env["sk_po0"] = po.v
    env["sk_sv0"] = sv.v
    env["sk_dve_it"] = 0
    env["sk_pool_it"] = 0


def _sink_pool_emit(p, env, n_iters):
    g = env.__getitem__
    po, sv = g("po"), g("sv")
    V = _sink_views(env)
    po0, sv0 = env["sk_po0"], env["sk_sv0"]
    for _ in range(n_iters):
        it = env["sk_pool_it"]
        if it >= 50:
            return
        env["sk_pool_it"] = it + 1
        if it > 0:
            p.wait_ge(sv.h, sv0 + 2 * it)
        po.inc(p.tensor_tensor(V["ttv"], V["pv4"], V["bRep"], OP.mult))
        p.wait_ge(sv.h, sv0 + 2 * it + 1)
        po.inc(p.tensor_tensor(V["ttv"], V["pv4"], V["aRep"], OP.mult))


def _sink_dve_emit(v, env, n_iters):
    g = env.__getitem__
    po, sv, sf = g("po"), g("sv"), g("sf")
    avs, bvs = g("avs"), g("bvs")
    V = _sink_views(env)
    po0, sv0 = env["sk_po0"], env["sk_sv0"]
    for _ in range(n_iters):
        it = env["sk_dve_it"]
        if it >= 50:
            return
        env["sk_dve_it"] = it + 1
        v.wait_ge(po.h, po0 + 2 * it + 1)
        sf.inc(v.reduce_sum(V["av_v"], V["ttv"], axis=AX.X))
        v.wait_ge(sf.h, sf.v)
        sv.inc(v.reciprocal(avs[:], avs[:]))
        v.wait_ge(po.h, po0 + 2 * it + 2)
        sf.inc(v.reduce_sum(V["bv_v"], V["ttv_t"], axis=AX.X))
        v.wait_ge(sf.h, sf.v)
        sv.inc(v.reciprocal(bvs[:], bvs[:]))


def _p1b_final(nc, env):
    g = env.__getitem__
    lbuf, hres, hpre, hpost = g("lbuf"), g("hres"), g("hpre"), g("hpost")
    i1mg, gcol = g("i1mg"), g("gcol")
    ve, ac, po, sv, sf = g("ve"), g("ac"), g("po"), g("sv"), g("sf")
    tts = g("tts")
    V = _sink_views(env)
    G = NT
    assert env["sk_dve_it"] == 50 and env["sk_pool_it"] == 50, (
        env["sk_dve_it"], env["sk_pool_it"])

    with nc.sbuf_tensor("sm", [128, G * 2], F32) as sm, \
         nc.sbuf_tensor("pexp", [128, G * 8], F32) as pexp, \
         nc.Block() as blk:

        ppv = lbuf[:].rearrange("p (g e) -> p g e", e=24)[:, :, 16:24] \
                     .rearrange("p g (h f) -> p g h f", h=2, f=4)
        pexpv = pexp[:].rearrange("p (g h f) -> p g h f", h=2, f=4)
        smv = sm[:].rearrange("p (g h) -> p g h", h=2)

        @blk.vector
        def _(v):
            v.wait_ge(po.h, po.v)
            v.wait_ge(sv.h, sv.v)
            sf.inc(v.tensor_tensor(V["ttv"], V["ttv"], V["bRep"], OP.mult))
            v.wait_ge(sf.h, sf.v)
            ve.inc(v.scalar_tensor_tensor(
                hres[:].rearrange("p (g q) -> p g q", q=16),
                tts[:].rearrange("p (g q) -> p g q", q=16), gcol[:],
                i1mg[:].unsqueeze(1).broadcast_to([128, G, 16]),
                OP.mult, OP.add))
            sf.inc(v.reduce_max(smv, ppv, axis=AX.X))
            v.wait_ge(sf.h, sf.v)
            ve.inc(v.tensor_tensor(pexpv, ppv,
                                   smv.unsqueeze(3).broadcast_to([128, G, 2, 4]),
                                   OP.subtract))

        @blk.scalar
        def _(s):
            s.wait_ge(ve.h, ve.v)
            ac.inc(s.activation(pexp[:], pexp[:], AF.Exp))

        @blk.vector
        def _(v):
            v.wait_ge(ac.h, ac.v)
            sf.inc(v.reduce_sum(smv, pexpv, axis=AX.X))
            v.wait_ge(sf.h, sf.v)
            sf.inc(v.reciprocal(sm[:], sm[:]))
            v.wait_ge(sf.h, sf.v)
            sf.inc(v.tensor_tensor(pexpv, pexpv,
                                   smv.unsqueeze(3).broadcast_to([128, G, 2, 4]),
                                   OP.mult))
            v.wait_ge(sf.h, sf.v)
            v.tensor_copy(hpre[:].rearrange("p (g f) -> p g f", f=4), pexpv[:, :, 0, :])
            ve.inc(v.tensor_copy(hpost[:].rearrange("p (g f) -> p g f", f=4),
                                 pexpv[:, :, 1, :]))


def _kv_matmuls(t, pe, psum, wslab, xn_fm):
    mm = None
    for c in range(NC_):
        for nb in range(4):
            mm = t.matmul(psum[:, nb * 512:(nb + 1) * 512],
                          wslab[:, c * 128:(c + 1) * 128],
                          xn_fm[:, c * T + nb * 512: c * T + (nb + 1) * 512],
                          start=(c == 0), stop=(c == NC_ - 1))
    return pe.inc(mm)


def _proj_rope_phase(nc, env, n_units, w_d, out_dram, tag, sink_n=0):
    g = env.__getitem__
    xn_fm, cosb, sinb, rmat = g("xn_fm"), g("cosb"), g("sinb"), g("rmat")
    gs, sp, pe, ve, ac = g("gs"), g("sp"), g("pe"), g("ve"), g("ac")
    wl = (g("w0"), g("w1"))

    with nc.sbuf_tensor(f"wslab_{tag}", [128, 2 * NC_ * 128], BF16) as wslab, \
         nc.sbuf_tensor(f"raw_{tag}", [128, T], BF16) as raw, \
         nc.sbuf_tensor(f"c_{tag}", [128, T], F32) as cbuf, \
         nc.sbuf_tensor(f"t2_{tag}", [128, T], F32) as t2buf, \
         nc.sbuf_tensor(f"rb_{tag}", [128, T], BF16) as rb, \
         nc.psum_tensor(f"psM_{tag}", [128, T], F32) as psM, \
         nc.psum_tensor(f"psR_{tag}", [128, T], F32) as psR, \
         nc.Block() as blk:

        gs0, pe0, ve0, ac0, sp0 = gs.v, pe.v, ve.v, ac.v, sp.v
        wl0 = (wl[0].v, wl[1].v)

        @blk.gpsimd
        def _(p):
            for u in range(n_units):
                if u >= 2:
                    p.wait_ge(pe.h, pe0 + (u - 2) * 2 + 1)
                p.dma_start(
                    wslab[:, (u % 2) * NC_ * 128:(u % 2 + 1) * NC_ * 128]
                        .rearrange("p (c o) -> p c o", o=128),
                    w_d.rearrange("(c p) o -> p c o", p=128)[:, :, u * 128:(u + 1) * 128]
                ).then_inc(wl[u % 2].h, 16)
                wl[u % 2].v += 16
                _sink_pool_emit(p, env, sink_n)

        @blk.tensor
        def _(t):
            for u in range(n_units):
                t.wait_ge(wl[u % 2].h, wl0[u % 2] + (u // 2 + 1) * 16)
                if u > 0:
                    t.wait_ge(ve.h, ve0 + (u - 1) * 3 + 1)
                    t.wait_ge(ac.h, ac0 + u)
                _kv_matmuls(t, pe, psM, wslab[:, (u % 2) * NC_ * 128:
                                              (u % 2 + 1) * NC_ * 128], xn_fm)
                t.wait_ge(ac.h, ac0 + u + 1)
                if u > 0:
                    t.wait_ge(ve.h, ve0 + (u - 1) * 3 + 2)
                mm = None
                for nb in range(4):
                    mm = t.matmul(psR[:, nb * 512:(nb + 1) * 512], rmat[:],
                                  raw[:, nb * 512:(nb + 1) * 512], start=True, stop=True)
                pe.inc(mm)

        @blk.scalar
        def _(s):
            for u in range(n_units):
                s.wait_ge(pe.h, pe0 + u * 2 + 1)
                ac.inc(s.copy(raw[:], psM[:]))
                if u > 0:
                    s.wait_ge(ve.h, ve0 + u * 3)
                    s.dma_start(out_dram.rearrange("(a p) t -> a p t", p=128)[u - 1],
                                rb[:]).then_inc(sp.h, 16)
            s.wait_ge(ve.h, ve0 + n_units * 3)
            s.dma_start(out_dram.rearrange("(a p) t -> a p t", p=128)[n_units - 1],
                        rb[:]).then_inc(sp.h, 16)
            sp.v += n_units * 16
            s.wait_ge(sp.h, sp.v)

        @blk.vector
        def _(v):
            for u in range(n_units):
                v.wait_ge(pe.h, pe0 + u * 2 + 1)
                if u > 0:
                    v.wait_ge(sp.h, sp0 + u * 16)
                ve.inc(v.tensor_tensor(cbuf[:], psM[:], cosb[:], OP.mult))
                v.wait_ge(pe.h, pe0 + u * 2 + 2)
                ve.inc(v.tensor_tensor(t2buf[:], psR[:], sinb[:], OP.mult))
                v.wait_ge(ve.h, ve.v)
                ve.inc(v.tensor_tensor(rb[:], cbuf[:], t2buf[:], OP.add))
                _sink_dve_emit(v, env, sink_n)

        assert pe.v == pe0 + n_units * 2
        assert ac.v == ac0 + n_units
        assert ve.v == ve0 + n_units * 3


def _p2(nc, env):
    import os
    g = env.__getitem__
    if not os.environ.get("SKIP_K"):
        _proj_rope_phase(nc, env, NKV, g("wk_d"), g("ks_d"), "k", sink_n=2)
    if os.environ.get("SKIP_V"):
        return

    xn_fm, ident = g("xn_fm"), g("ident")
    wv_d, vs_d = g("wv_d"), g("vs_d")
    gs, sp, pe, ve, ac = g("gs"), g("sp"), g("pe"), g("ve"), g("ac")
    wl = (g("w0"), g("w1"))

    with nc.sbuf_tensor("wvslab", [128, 2 * NC_ * 128], BF16) as wvslab, \
         nc.sbuf_tensor("vfm", [128, T], BF16) as vfm, \
         nc.sbuf_tensor("vstage", [128, NT * 128], BF16) as vstage, \
         nc.psum_tensor("psV", [128, T], F32) as psV, \
         nc.psum_tensor("psT", [128, 2 * 1024], BF16) as psT, \
         nc.Block() as blk:

        gs0, pe0, ve0, ac0, sp0 = gs.v, pe.v, ve.v, ac.v, sp.v
        wl0 = (wl[0].v, wl[1].v)
        PE_V, VE_V = 17, 16

        @blk.gpsimd
        def _(p):
            for vd in range(NKV):
                if vd >= 2:
                    p.wait_ge(pe.h, pe0 + (vd - 2) * PE_V + 1)
                p.dma_start(
                    wvslab[:, (vd % 2) * NC_ * 128:(vd % 2 + 1) * NC_ * 128]
                        .rearrange("p (c o) -> p c o", o=128),
                    wv_d.rearrange("(c p) o -> p c o", p=128)[:, :, vd * 128:(vd + 1) * 128]
                ).then_inc(wl[vd % 2].h, 16)
                wl[vd % 2].v += 16
                _sink_pool_emit(p, env, 2)

        @blk.tensor
        def _(t):
            for vd in range(NKV):
                t.wait_ge(wl[vd % 2].h, wl0[vd % 2] + (vd // 2 + 1) * 16)
                if vd > 0:
                    t.wait_ge(ac.h, ac0 + vd)
                _kv_matmuls(t, pe, psV, wvslab[:, (vd % 2) * NC_ * 128:
                                               (vd % 2 + 1) * NC_ * 128], xn_fm)
                t.wait_ge(ac.h, ac0 + vd + 1)
                for kt in range(NT):
                    if kt >= 2:
                        t.wait_ge(ve.h, ve0 + vd * VE_V + kt - 1)
                    elif vd > 0:
                        t.wait_ge(ve.h, ve0 + vd * VE_V - 2 + kt + 1)
                    pe.inc(t.transpose(psT[:, (kt % 2) * 1024:(kt % 2) * 1024 + 128],
                                       vfm[:, kt * 128:(kt + 1) * 128], ident[:]))

        @blk.scalar
        def _(s):
            for vd in range(NKV):
                s.wait_ge(pe.h, pe0 + vd * PE_V + 1)
                ac.inc(s.copy(vfm[:], psV[:]))
                if vd > 0:
                    s.wait_ge(ve.h, ve0 + vd * VE_V)
                    s.dma_start(
                        vs_d.rearrange("(n p) o -> p n o", p=128)[:, :,
                                                                 (vd - 1) * 128:vd * 128],
                        vstage[:].rearrange("p (n o) -> p n o", o=128)).then_inc(sp.h, 16)
            s.wait_ge(ve.h, ve0 + NKV * VE_V)
            s.dma_start(
                vs_d.rearrange("(n p) o -> p n o", p=128)[:, :, (NKV - 1) * 128:NKV * 128],
                vstage[:].rearrange("p (n o) -> p n o", o=128)).then_inc(sp.h, 16)
            sp.v += NKV * 16
            s.wait_ge(sp.h, sp.v)

        @blk.vector
        def _(v):
            for vd in range(NKV):
                if vd > 0:
                    v.wait_ge(sp.h, sp0 + vd * 16)
                for kt in range(NT):
                    v.wait_ge(pe.h, pe0 + vd * PE_V + 1 + kt + 1)
                    ve.inc(v.tensor_copy(vstage[:, kt * 128:(kt + 1) * 128],
                                         psT[:, (kt % 2) * 1024:(kt % 2) * 1024 + 128]))
                _sink_dve_emit(v, env, 2)

        assert pe.v == pe0 + NKV * PE_V
        assert ac.v == ac0 + NKV
        assert ve.v == ve0 + NKV * VE_V


def _p3(nc, env):
    g = env.__getitem__
    _proj_rope_phase(nc, env, NH, g("wq_d"), g("qs_d"), "q", sink_n=1)


def _p35(nc, env):
    g = env.__getitem__
    xns_d, ys_d, wb_d = g("xns_d"), g("ys_d"), g("wb_d")
    ident, hpre = g("ident"), g("hpre")
    gs, sp, pe, ve, ac, sf = g("gs"), g("sp"), g("pe"), g("ve"), g("ac"), g("sf")
    ll = (g("l0"), g("l1"))

    with nc.sbuf_tensor("wbs", [128, 8 * 1024], BF16) as wbs, \
         nc.sbuf_tensor("xnb35", [128, 2 * D], BF16) as xnb35, \
         nc.sbuf_tensor("xp35", [128, 1024], F32) as xp, \
         nc.sbuf_tensor("xpa35", [128, 1024], F32) as xpa, \
         nc.sbuf_tensor("xpb35", [128, 2 * 1024], BF16) as xpb, \
         nc.sbuf_tensor("xpT35", [128, 2 * 8 * 128], BF16) as xpT, \
         nc.sbuf_tensor("yt35", [128, 2 * 1024], BF16) as yt, \
         nc.psum_tensor("pT35", [128, 2 * 1024], BF16) as pT35, \
         nc.psum_tensor("psY35", [128, 2 * 1024], F32) as psY, \
         nc.Block() as blk:

        gs0, pe0, ve0, ac0, sp0 = gs.v, pe.v, ve.v, ac.v, sp.v
        ll0 = (ll[0].v, ll[1].v)

        def xpb_done(i):
            return ve0 + i * 10 + 1

        def evT(i, cc):
            return ve0 + i * 10 + 2 + cc

        def yt_done(i):
            return ve0 + (i + 1) * 10

        def tr_done(i, cc):
            return pe0 + i * 9 + cc + 1

        def y_done(i):
            return pe0 + (i + 1) * 9

        @blk.gpsimd
        def _(p):
            p.dma_start(wbs[:].rearrange("p (c o) -> p c o", o=1024),
                        wb_d.rearrange("(c p) o -> p c o", p=128)).then_inc(gs.h, 16)
            gs.v += 16
            p.wait_ge(gs.h, gs.v)

        @blk.sync
        def _(sync):
            for i in range(NT):
                if i >= 2:
                    sync.wait_ge(ve.h, xpb_done(i - 2) + 4)
                sync.dma_start(xnb35[:, (i % 2) * D:(i % 2 + 1) * D],
                               xns_d.rearrange("(n p) d -> n p d", p=128)[i]
                               ).then_inc(ll[i % 2].h, 16)
                ll[i % 2].v += 16

        @blk.vector
        def _(v):
            for i in range(NT):
                xt = xnb35[:, (i % 2) * D:(i % 2 + 1) * D]

                def _e(instr):
                    sf.inc(instr)
                    v.wait_ge(sf.h, sf.v)
                v.wait_ge(ll[i % 2].h, ll0[i % 2] + (i // 2 + 1) * 16)
                _e(v.tensor_scalar(xp[:], xt[:, 0:1024], hpre[:, i * 4:i * 4 + 1],
                                   None, OP.mult))
                _e(v.scalar_tensor_tensor(xpa[:], xt[:, 1024:2048],
                                          hpre[:, i * 4 + 1:i * 4 + 2], xp[:],
                                          OP.mult, OP.add))
                _e(v.scalar_tensor_tensor(xp[:], xt[:, 2048:3072],
                                          hpre[:, i * 4 + 2:i * 4 + 3], xpa[:],
                                          OP.mult, OP.add))
                if i >= 2:
                    v.wait_ge(pe.h, tr_done(i - 2, 7))
                ve.inc(v.scalar_tensor_tensor(xpb[:, (i % 2) * 1024:(i % 2 + 1) * 1024],
                                              xt[:, 3072:4096],
                                              hpre[:, i * 4 + 3:i * 4 + 4], xp[:],
                                              OP.mult, OP.add))
                for cc in range(8):
                    v.wait_ge(pe.h, tr_done(i, cc))
                    ve.inc(v.tensor_copy(
                        xpT[:, (i % 2) * 1024 + cc * 128:(i % 2) * 1024 + (cc + 1) * 128],
                        pT35[:, (cc % 2) * 1024:(cc % 2) * 1024 + 128]))
                v.wait_ge(pe.h, y_done(i))
                if i >= 2:
                    v.wait_ge(sp.h, sp0 + (i - 1) * 16)
                ve.inc(v.tensor_copy(yt[:, (i % 2) * 1024:(i % 2 + 1) * 1024], psY[
                    :, (i % 2) * 1024:(i % 2 + 1) * 1024]))

        @blk.tensor
        def _(t):
            for i in range(NT):
                t.wait_ge(ve.h, xpb_done(i))
                if i == 0:
                    t.wait_ge(gs.h, gs.v)
                for cc in range(8):
                    if cc >= 2:
                        t.wait_ge(ve.h, evT(i, cc - 2))
                    elif i > 0:
                        t.wait_ge(ve.h, evT(i - 1, 6 + cc))
                    pe.inc(t.transpose(pT35[:, (cc % 2) * 1024:(cc % 2) * 1024 + 128],
                                       xpb[:, (i % 2) * 1024 + cc * 128:
                                           (i % 2) * 1024 + (cc + 1) * 128],
                                       ident[:]))
                t.wait_ge(ve.h, evT(i, 7))
                if i > 0:
                    t.wait_ge(ve.h, yt_done(i - 1))
                mm = None
                for cc in range(8):
                    for nb in range(2):
                        mm = t.matmul(
                            psY[:, (i % 2) * 1024 + nb * 512:(i % 2) * 1024 + (nb + 1) * 512],
                            xpT[:, (i % 2) * 1024 + cc * 128:(i % 2) * 1024 + (cc + 1) * 128],
                            wbs[:, cc * 1024 + nb * 512: cc * 1024 + (nb + 1) * 512],
                            start=(cc == 0), stop=(cc == 7))
                pe.inc(mm)

        @blk.scalar
        def _(s):
            for i in range(NT):
                s.wait_ge(ve.h, yt_done(i))
                s.dma_start(ys_d.rearrange("(n p) c -> n p c", p=128)[i],
                            yt[:, (i % 2) * 1024:(i % 2 + 1) * 1024]).then_inc(sp.h, 16)
            sp.v += NT * 16
            s.wait_ge(sp.h, sp.v)

        assert pe.v == pe0 + NT * 9
        assert ve.v == ve0 + NT * 10


def _p4(nc, env):
    g = env.__getitem__
    qs_d, ks_d, vs_d = g("qs_d"), g("ks_d"), g("vs_d")
    xns_d, ys_d, gs_d2 = g("xns_d"), g("ys_d"), g("gs_d2")
    ctx, tri, ones_bf, ones_f1 = g("ctx"), g("tri"), g("ones_bf"), g("ones_f1")
    shiftc = g("shiftc")
    hres, hpost = g("hres"), g("hpost")
    pe, ve, ac, po, sf = g("pe"), g("ve"), g("ac"), g("po"), g("sf")
    kl = (g("k0"), g("k1"))
    ql = (g("q0"), g("q1"))
    ol = (g("o0"), g("o1"))
    yl = (g("l0"), g("l1"))
    gsp = (g("g0"), g("g1"))

    steps = [(hh, qc) for hh in range(NH) for qc in range(4)]
    NS = len(steps)

    with nc.sbuf_tensor("ksl", [128, 2 * T], BF16) as ksl, \
         nc.sbuf_tensor("vsl", [128, 2 * NT * 128], BF16) as vsl, \
         nc.sbuf_tensor("qh", [128, 2 * 512], BF16) as qh, \
         nc.sbuf_tensor("pbuf", [128, 6 * 512], BF16) as pbuf, \
         nc.sbuf_tensor("invs", [1, 2 * 512], F32) as invs, \
         nc.sbuf_tensor("invbc", [128, 2 * 512], F32) as invbc, \
         nc.sbuf_tensor("xnb4", [128, 2 * 2048], BF16) as xnb4, \
         nc.sbuf_tensor("yt4", [128, 2 * 1024], BF16) as yt4, \
         nc.sbuf_tensor("gpre", [128, D], F32) as gpre, \
         nc.psum_tensor("scp", [128, 3 * 512], F32) as scp, \
         nc.psum_tensor("sbp", [128, 2 * 512], F32) as sbp, \
         nc.psum_tensor("ctxPS", [128, 2 * 512], F32) as ctxPS, \
         nc.Block() as blk:

        pe0, ve0, ac0, po0 = pe.v, ve.v, ac.v, po.v
        kl0 = (kl[0].v, kl[1].v)
        ql0 = (ql[0].v, ql[1].v)
        ol0 = (ol[0].v, ol[1].v)
        yl0 = (yl[0].v, yl[1].v)
        gsp0 = (gsp[0].v, gsp[1].v)
        gv0 = 0

        def kv_tgt(kvh):
            return kl0[kvh % 2] + (kvh // 2 + 1) * 32

        def q_tgt(s_i):
            return ql0[s_i % 2] + (s_i // 2 + 1) * 16

        def yt_tgt(i):
            return yl0[i % 2] + (i // 2 + 1) * 16

        def xa_tgt(i):
            return ol0[0] + (i + 1) * 16

        def xb_tgt(i):
            return ol0[1] + (i + 1) * 16

        def gspill_tgt(i):
            return gsp0[i % 2] + (i // 2 + 1) * 16

        pe_c, ac_c, ve_c, po_c = pe0, ac0, ve0, po0
        scores_done, exp_done, mask_done = {}, {}, {}
        svkt_done, pv_done, bcast_done = {}, {}, {}
        recip_done, evac_done = {}, {}
        g_t0, g_t1, g_t2 = {}, {}, {}

        def g_slots(s_i):
            if s_i % 8 in (0, 2, 4) and s_i // 8 < NT:
                return (s_i // 8, s_i % 8 // 2)
            return None

        for s_i, (hh, qc) in enumerate(steps):
            Q0 = qc * 512
            ktmax = 4 * (qc + 1)
            for kt in range(ktmax):
                ac_c += 1
                exp_done[(s_i, kt)] = ac_c
                if kt * 128 >= Q0:
                    ve_c += 1
                    mask_done[(s_i, kt)] = ve_c
            for kt in range(ktmax):
                pe_c += 1
                scores_done[(s_i, kt)] = pe_c
                if s_i > 0 and kt == 1:
                    pe_c += 1
                    bcast_done[s_i - 1] = pe_c
                if kt >= 2:
                    pe_c += 2
                    svkt_done[(s_i, kt - 2)] = pe_c
            for kt in (ktmax - 2, ktmax - 1):
                pe_c += 2
                svkt_done[(s_i, kt)] = pe_c
            pv_done[s_i] = pe_c
            ve_c += 1
            recip_done[s_i] = ve_c
            ve_c += 2
            evac_done[s_i] = ve_c
            gsl = g_slots(s_i)
            if gsl is not None:
                i, piece = gsl
                ve_c += 1
                (g_t0 if piece == 0 else g_t1 if piece == 1 else g_t2)[i] = ve_c
                if piece == 2:
                    po_c += 1
        pe_c += 1
        bcast_done[NS - 1] = pe_c

        @blk.sync
        def _(sync):
            for s_i, (hh, qc) in enumerate(steps):
                kvh = hh // 4
                if qc == 0 and hh % 4 == 0:
                    if kvh >= 2:
                        sync.wait_ge(pe.h, pv_done[((kvh - 2) * 4 + 3) * 4 + 3])
                    sync.dma_start(ksl[:, (kvh % 2) * T:(kvh % 2 + 1) * T],
                                   ks_d.rearrange("(a p) t -> a p t", p=128)[kvh]
                                   ).then_inc(kl[kvh % 2].h, 16)
                    sync.dma_start(
                        vsl[:, (kvh % 2) * NT * 128:(kvh % 2 + 1) * NT * 128]
                            .rearrange("p (n o) -> p n o", o=128),
                        vs_d.rearrange("(n p) o -> p n o", p=128)[:, :,
                                                                 kvh * 128:(kvh + 1) * 128]
                    ).then_inc(kl[kvh % 2].h, 16)
                    kl[kvh % 2].v += 32
                if s_i >= 2:
                    sync.wait_ge(pe.h, pv_done[s_i - 2])
                sync.dma_start(qh[:, (s_i % 2) * 512:(s_i % 2 + 1) * 512],
                               qs_d.rearrange("(a p) t -> a p t", p=128)[hh]
                               [:, qc * 512:(qc + 1) * 512]).then_inc(ql[s_i % 2].h, 16)
                ql[s_i % 2].v += 16
                if s_i % 8 == 0 and s_i // 8 < NT:
                    i = s_i // 8
                    if i >= 2:
                        sync.wait_ge(ve.h, g_t0[i - 2])
                    sync.dma_start(yt4[:, (i % 2) * 1024:(i % 2 + 1) * 1024],
                                   ys_d.rearrange("(n p) c -> n p c", p=128)[i]
                                   ).then_inc(yl[i % 2].h, 16)
                    yl[i % 2].v += 16
                    if i >= 1:
                        sync.wait_ge(ve.h, g_t1[i - 1])
                    sync.dma_start(
                        xnb4[:, 0:2048],
                        xns_d.rearrange("(n p) d -> n p d", p=128)[i][:, 0:2048]
                    ).then_inc(ol[0].h, 16)
                    ol[0].v += 16
                    if i >= 1:
                        sync.wait_ge(ve.h, g_t2[i - 1])
                    sync.dma_start(
                        xnb4[:, 2048:4096],
                        xns_d.rearrange("(n p) d -> n p d", p=128)[i][:, 2048:4096]
                    ).then_inc(ol[1].h, 16)
                    ol[1].v += 16

        @blk.tensor
        def _(t):
            def emit_sv(s_i, kt, Q0, ktmax, kvh):
                qlo = max(Q0, kt * 128)
                off = qlo - Q0
                preg_kt = ((s_i % 2) * 3 + (kt % 3)) * 512
                vsl_o = (kvh % 2) * NT * 128
                if (s_i, kt) in mask_done:
                    t.wait_ge(ve.h, mask_done[(s_i, kt)])
                else:
                    t.wait_ge(ac.h, exp_done[(s_i, kt)])
                if kt == 0 and s_i > 1:
                    t.wait_ge(ve.h, evac_done[s_i - 2])
                mm = t.matmul(sbp[0:1, (s_i % 2) * 512 + off:(s_i % 2) * 512 + 512],
                              ones_bf[:], pbuf[:, preg_kt + off: preg_kt + 512],
                              start=(kt == 0), stop=(kt == ktmax - 1))
                pe.inc(mm)
                mm = t.matmul(ctxPS[:, (s_i % 2) * 512 + off:(s_i % 2) * 512 + 512],
                              vsl[:, vsl_o + kt * 128: vsl_o + (kt + 1) * 128],
                              pbuf[:, preg_kt + off: preg_kt + 512],
                              start=(kt == 0), stop=(kt == ktmax - 1))
                pe.inc(mm)

            for s_i, (hh, qc) in enumerate(steps):
                kvh = hh // 4
                Q0 = qc * 512
                ktmax = 4 * (qc + 1)
                ksl_o = (kvh % 2) * T
                t.wait_ge(ql[s_i % 2].h, q_tgt(s_i))
                if qc == 0 and hh % 4 == 0:
                    t.wait_ge(kl[kvh % 2].h, kv_tgt(kvh))
                for kt in range(ktmax):
                    qlo = max(Q0, kt * 128)
                    off = qlo - Q0
                    w = 512 - off
                    if kt >= 3:
                        t.wait_ge(ac.h, exp_done[(s_i, kt - 3)])
                    elif s_i > 0:
                        pmax = 4 * (steps[s_i - 1][1] + 1)
                        t.wait_ge(ac.h, exp_done[(s_i - 1, pmax - 1)])
                    mm = t.matmul(scp[:, (kt % 3) * 512:(kt % 3) * 512 + w],
                                  ksl[:, ksl_o + kt * 128: ksl_o + (kt + 1) * 128],
                                  qh[:, (s_i % 2) * 512 + off:(s_i % 2 + 1) * 512],
                                  start=True, stop=True)
                    pe.inc(mm)
                    if s_i > 0 and kt == 1:
                        ps = s_i - 1
                        t.wait_ge(ve.h, recip_done[ps])
                        mm = t.matmul(sbp[:, (ps % 2) * 512:(ps % 2 + 1) * 512],
                                      ones_f1[:], invs[:, (ps % 2) * 512:(ps % 2 + 1) * 512],
                                      start=True, stop=True)
                        pe.inc(mm)
                    if kt >= 2:
                        emit_sv(s_i, kt - 2, Q0, ktmax, kvh)
                emit_sv(s_i, ktmax - 2, Q0, ktmax, kvh)
                emit_sv(s_i, ktmax - 1, Q0, ktmax, kvh)
            t.wait_ge(ve.h, recip_done[NS - 1])
            mm = t.matmul(sbp[:, ((NS - 1) % 2) * 512:((NS - 1) % 2 + 1) * 512],
                          ones_f1[:], invs[:, ((NS - 1) % 2) * 512:((NS - 1) % 2 + 1) * 512],
                          start=True, stop=True)
            pe.inc(mm)

        @blk.scalar
        def _(s):
            for s_i, (hh, qc) in enumerate(steps):
                Q0 = qc * 512
                ktmax = 4 * (qc + 1)
                for kt in range(ktmax):
                    qlo = max(Q0, kt * 128)
                    off = qlo - Q0
                    w = 512 - off
                    preg_kt = ((s_i % 2) * 3 + (kt % 3)) * 512
                    s.wait_ge(pe.h, scores_done[(s_i, kt)])
                    if kt >= 3:
                        s.wait_ge(pe.h, svkt_done[(s_i, kt - 3)])
                    elif s_i >= 2:
                        s.wait_ge(pe.h, pv_done[s_i - 2])
                    ac.inc(s.activation(
                        pbuf[:, preg_kt + off: preg_kt + off + w],
                        scp[:, (kt % 3) * 512:(kt % 3) * 512 + w],
                        AF.Exp, bias=shiftc[:], scale=INV_SQRT_DH))

        @blk.vector
        def _(v):
            for s_i, (hh, qc) in enumerate(steps):
                Q0 = qc * 512
                ktmax = 4 * (qc + 1)
                for kt in range(4 * qc, ktmax):
                    off = kt * 128 - Q0
                    preg_kt = ((s_i % 2) * 3 + (kt % 3)) * 512
                    v.wait_ge(ac.h, exp_done[(s_i, kt)])
                    ve.inc(v.tensor_tensor(
                        pbuf[:, preg_kt + off: preg_kt + off + 128],
                        pbuf[:, preg_kt + off: preg_kt + off + 128],
                        tri[:], OP.mult))
                v.wait_ge(pe.h, pv_done[s_i])
                ve.inc(v.reciprocal(invs[:, (s_i % 2) * 512:(s_i % 2 + 1) * 512],
                                    sbp[0:1, (s_i % 2) * 512:(s_i % 2 + 1) * 512]))
                v.wait_ge(pe.h, bcast_done[s_i])
                ve.inc(v.tensor_copy(invbc[:, (s_i % 2) * 512:(s_i % 2 + 1) * 512],
                                     sbp[:, (s_i % 2) * 512:(s_i % 2 + 1) * 512]))
                v.wait_ge(ve.h, ve.v)
                ve.inc(v.tensor_tensor(
                    ctx[:, hh * T + qc * 512: hh * T + (qc + 1) * 512],
                    ctxPS[:, (s_i % 2) * 512:(s_i % 2 + 1) * 512],
                    invbc[:, (s_i % 2) * 512:(s_i % 2 + 1) * 512], OP.mult))
                gsl = g_slots(s_i)
                if gsl is not None:
                    i, piece = gsl
                    if piece == 0:
                        if i >= 1:
                            v.wait_ge(gsp[(i - 1) % 2].h, gspill_tgt(i - 1))
                        v.wait_ge(yl[i % 2].h, yt_tgt(i))
                        for istr in range(4):
                            tsp = v.tensor_scalar(
                                gpre[:, istr * 1024:(istr + 1) * 1024],
                                yt4[:, (i % 2) * 1024:(i % 2 + 1) * 1024],
                                hpost[:, i * 4 + istr:i * 4 + istr + 1], None, OP.mult)
                        ve.inc(tsp)
                    elif piece == 1:
                        v.wait_ge(ol[0].h, xa_tgt(i))
                        for j in (0, 1):
                            for istr in range(4):
                                stt = v.scalar_tensor_tensor(
                                    gpre[:, istr * 1024:(istr + 1) * 1024],
                                    xnb4[:, j * 1024:(j + 1) * 1024],
                                    hres[:, i * 16 + istr * 4 + j:
                                         i * 16 + istr * 4 + j + 1],
                                    gpre[:, istr * 1024:(istr + 1) * 1024],
                                    OP.mult, OP.add)
                        ve.inc(stt)
                    else:
                        v.wait_ge(ol[1].h, xb_tgt(i))
                        for j in (2, 3):
                            for istr in range(4):
                                stt = v.scalar_tensor_tensor(
                                    gpre[:, istr * 1024:(istr + 1) * 1024],
                                    xnb4[:, j * 1024:(j + 1) * 1024],
                                    hres[:, i * 16 + istr * 4 + j:
                                         i * 16 + istr * 4 + j + 1],
                                    gpre[:, istr * 1024:(istr + 1) * 1024],
                                    OP.mult, OP.add)
                        ve.inc(stt)

        @blk.gpsimd
        def _(p):
            for i in range(NT):
                p.wait_ge(ve.h, g_t2[i])
                p.dma_start(gs_d2.rearrange("(n p) d -> n p d", p=128)[i],
                            gpre[:]).then_inc(gsp[i % 2].h, 16)
                gsp[i % 2].v += 16
            p.wait_ge(gsp[0].h, gsp[0].v)
            p.wait_ge(gsp[1].h, gsp[1].v)

        assert pe.v == pe_c, (pe.v, pe_c)
        assert ac.v == ac_c
        assert ve.v == ve_c


def _p5(nc, env):
    g = env.__getitem__
    wo_d, ctx, x_d, out_d, gs_d2 = g("wo_d"), g("ctx"), g("x_d"), g("out_d"), g("gs_d2")
    gs, pe, ve, ac, po = g("gs"), g("pe"), g("ve"), g("ac"), g("po")
    wl = (g("w0"), g("w1"))
    ll = (g("l0"), g("l1"))
    ol_ = (g("o0"), g("o1"))

    NOC = 16
    OC = 256

    with nc.sbuf_tensor("woblk", [128, 2 * NC_ * OC], BF16) as woblk, \
         nc.sbuf_tensor("xcols", [128, 2 * OC], F32) as xcols, \
         nc.sbuf_tensor("gcols", [128, 2 * OC], F32) as gcols, \
         nc.sbuf_tensor("gbuf", [128, 2 * OC], F32) as gbuf, \
         nc.sbuf_tensor("tbuf", [128, 2 * OC], F32) as tbuf, \
         nc.sbuf_tensor("obuf", [128, 2 * OC], F32) as obuf, \
         nc.psum_tensor("sPS", [128, 2 * 512], F32) as sPS, \
         nc.Block() as blk:

        pe0, ve0, ac0, po0 = pe.v, ve.v, ac.v, po.v
        wl0 = (wl[0].v, wl[1].v)
        ll0 = (ll[0].v, ll[1].v)
        ol0 = (ol_[0].v, ol_[1].v)
        NK = NOC * NT

        def ld_tgt(k):
            return ll0[k % 2] + (k // 2 + 1) * 32

        def out_tgt(k):
            return ol0[k % 2] + (k // 2 + 1) * 16

        def _emit_wold(p, oc):
            if oc >= 2:
                p.wait_ge(pe.h, pe0 + (oc - 1) * NT)
            p.dma_start(
                woblk[:, (oc % 2) * NC_ * OC:(oc % 2 + 1) * NC_ * OC]
                    .rearrange("p (c o) -> p c o", o=OC),
                wo_d.rearrange("(c p) o -> p c o", p=128)[:, :, oc * OC:(oc + 1) * OC]
            ).then_inc(wl[oc % 2].h, 16)
            wl[oc % 2].v += 16

        @blk.sync
        def _(sync):
            for oc in range(NOC):
                for i in range(NT):
                    k = oc * NT + i
                    if k >= 2:
                        sync.wait_ge(po.h, po0 + k - 1)
                        sync.wait_ge(ac.h, ac0 + k - 1)
                    sync.dma_start(
                        xcols[:, (k % 2) * OC:(k % 2 + 1) * OC],
                        x_d.rearrange("(n p) d -> n p d", p=128)[i][:, oc * OC:(oc + 1) * OC]
                    ).then_inc(ll[k % 2].h, 16)
                    sync.dma_start(
                        gcols[:, (k % 2) * OC:(k % 2 + 1) * OC],
                        gs_d2.rearrange("(n p) d -> n p d", p=128)[i][:, oc * OC:(oc + 1) * OC]
                    ).then_inc(ll[k % 2].h, 16)
                    ll[k % 2].v += 32
                    if k >= 2:
                        kk = k - 2
                        oc2, i2 = kk // NT, kk % NT
                        sync.wait_ge(po.h, po0 + kk + 1)
                        sync.dma_start(
                            out_d.rearrange("(n p) d -> n p d", p=128)[i2]
                            [:, oc2 * OC:(oc2 + 1) * OC],
                            obuf[:, (kk % 2) * OC:(kk % 2 + 1) * OC]
                        ).then_inc(ol_[kk % 2].h, 16)
                        ol_[kk % 2].v += 16
            for kk in (NK - 2, NK - 1):
                oc2, i2 = kk // NT, kk % NT
                sync.wait_ge(po.h, po0 + kk + 1)
                sync.dma_start(
                    out_d.rearrange("(n p) d -> n p d", p=128)[i2]
                    [:, oc2 * OC:(oc2 + 1) * OC],
                    obuf[:, (kk % 2) * OC:(kk % 2 + 1) * OC]
                ).then_inc(ol_[kk % 2].h, 16)
                ol_[kk % 2].v += 16

        @blk.tensor
        def _(t):
            for oc in range(NOC):
                t.wait_ge(wl[oc % 2].h, wl0[oc % 2] + (oc // 2 + 1) * 16)
                for i in range(NT):
                    k = oc * NT + i
                    if k >= 2:
                        t.wait_ge(ve.h, ve0 + k - 1)
                    mm = None
                    for c in range(NC_):
                        mm = t.matmul(sPS[:, (k % 2) * 512:(k % 2) * 512 + OC],
                                      ctx[:, c * T + i * 128: c * T + (i + 1) * 128],
                                      woblk[:, (oc % 2) * NC_ * OC + c * OC:
                                            (oc % 2) * NC_ * OC + (c + 1) * OC],
                                      start=(c == 0), stop=(c == NC_ - 1))
                    pe.inc(mm)

        @blk.scalar
        def _(s):
            for oc in range(NOC):
                for i in range(NT):
                    k = oc * NT + i
                    s.wait_ge(ll[k % 2].h, ld_tgt(k))
                    if k >= 2:
                        s.wait_ge(ve.h, ve0 + k - 1)
                    ac.inc(s.activation(gbuf[:, (k % 2) * OC:(k % 2 + 1) * OC],
                                        gcols[:, (k % 2) * OC:(k % 2 + 1) * OC],
                                        AF.Silu))

        @blk.vector
        def _(v):
            for oc in range(NOC):
                for i in range(NT):
                    k = oc * NT + i
                    v.wait_ge(pe.h, pe0 + k + 1)
                    v.wait_ge(ac.h, ac0 + k + 1)
                    if k >= 2:
                        v.wait_ge(po.h, po0 + k - 1)
                    ve.inc(v.tensor_tensor(tbuf[:, (k % 2) * OC:(k % 2 + 1) * OC],
                                           sPS[:, (k % 2) * 512:(k % 2) * 512 + OC],
                                           gbuf[:, (k % 2) * OC:(k % 2 + 1) * OC],
                                           OP.mult))

        @blk.gpsimd
        def _(p):
            _emit_wold(p, 0)
            _emit_wold(p, 1)
            for oc in range(NOC):
                for i in range(NT):
                    k = oc * NT + i
                    p.wait_ge(ve.h, ve0 + k + 1)
                    p.wait_ge(ll[k % 2].h, ld_tgt(k))
                    if k >= 2:
                        p.wait_ge(ol_[k % 2].h, out_tgt(k - 2))
                    po.inc(p.tensor_tensor(obuf[:, (k % 2) * OC:(k % 2 + 1) * OC],
                                           tbuf[:, (k % 2) * OC:(k % 2 + 1) * OC],
                                           xcols[:, (k % 2) * OC:(k % 2 + 1) * OC],
                                           OP.add))
                if oc + 2 < NOC:
                    _emit_wold(p, oc + 2)

        assert pe.v == pe0 + NK
        assert ve.v == ve0 + NK
        assert ac.v == ac0 + NK
        assert po.v == po0 + NK


def kernel(L0_attn_resid, L0_attn_Wq, L0_attn_Wk, L0_attn_Wv, L0_attn_Wo,
           L0_ffn_W_res, L0_ffn_W_pre, L0_ffn_W_post, L0_ffn_H_res_base,
           L0_ffn_H_pre_base, L0_ffn_H_post_base, L0_ffn_gate, L0_ffn_W_branch):
    x = np.ascontiguousarray(np.asarray(L0_attn_resid, dtype=np.float32))
    cos_fm, sin_fm = _rope_tables()
    wrppT = np.ascontiguousarray(
        np.concatenate([np.asarray(L0_ffn_W_res), np.asarray(L0_ffn_W_pre),
                        np.asarray(L0_ffn_W_post)], axis=0).T.astype(np.float32))
    base24 = np.concatenate([
        np.asarray(L0_ffn_H_res_base, dtype=np.float32).reshape(16),
        np.asarray(L0_ffn_H_pre_base, dtype=np.float32).reshape(4),
        np.asarray(L0_ffn_H_post_base, dtype=np.float32).reshape(4)]).reshape(1, 24)
    delta16 = np.eye(4, dtype=np.float32).reshape(1, 16)
    gate = np.asarray(L0_ffn_gate, dtype=np.float32).reshape(1, 1)

    common = {
        "wq": np.ascontiguousarray(L0_attn_Wq, dtype=np.float32),
        "wk": np.ascontiguousarray(L0_attn_Wk, dtype=np.float32),
        "wv": np.ascontiguousarray(L0_attn_Wv, dtype=np.float32),
        "wo": np.ascontiguousarray(L0_attn_Wo, dtype=np.float32),
        "wb": np.ascontiguousarray(L0_ffn_W_branch, dtype=np.float32),
        "wrppT": wrppT, "base24": base24, "delta16": delta16, "gate": gate,
        "cos_fm": cos_fm, "sin_fm": sin_fm,
        "ident": np.eye(128, dtype=ml_dtypes.bfloat16),
        "rmat": _rot_lhsT(), "tri": _tri_mask(),
    }
    in_maps = [dict(common, x=np.ascontiguousarray(x[b])) for b in range(B)]

    nc = build_nc()
    res = run_bass_kernel_spmd(nc, in_maps, core_ids=list(range(B)))
    return np.stack([np.asarray(r["out"], dtype=np.float32) for r in res.results], axis=0)


# version 12
# speedup vs baseline: 1.0682x; 1.0682x over previous
import math
import numpy as np
import ml_dtypes
from contextlib import ExitStack

import concourse.bass as bass
import concourse.mybir as mybir
from concourse.bass_utils import run_bass_kernel_spmd

F32 = mybir.dt.float32
BF16 = mybir.dt.bfloat16
OP = mybir.AluOpType
AF = mybir.ActivationFunctionType
AX = mybir.AxisListType

B, T, D = 8, 2048, 4096
NH, NKV, DH = 32, 8, 128
NT = T // 128
NC_ = D // 128
INV_SQRT_DH = 1.0 / math.sqrt(DH)
EXP_SHIFT = -12.0
LN_EPS = 1e-5
RMS_EPS = 1.1920929e-07
TAU_INV = 10.0

_DEV = False


def _rope_tables():
    idx = np.arange(0, DH, 2, dtype=np.float64) / DH
    freq = 1.0 / (2.0 * math.pi) * (500000.0 ** (-idx))
    factor, lo, hi, L0 = 32.0, 1.0, 4.0, 8192.0
    freq_low, freq_high = lo / L0, hi / L0
    freq_scaled = np.where(freq < freq_low, freq / factor, freq)
    smooth = np.clip((L0 * freq - lo) / (hi - lo), 0.0, 1.0)
    freq_smooth = (1.0 - smooth) * (freq / factor) + smooth * freq
    is_mid = (freq >= freq_low) & (freq <= freq_high)
    freq = np.where(is_mid, freq_smooth, freq_scaled)
    phase = 2.0 * math.pi * np.arange(T, dtype=np.float64)[:, None] * freq[None, :]
    emb = np.concatenate([phase, phase], axis=-1)
    cos = np.ascontiguousarray(np.cos(emb).astype(np.float32).T)
    sin = np.ascontiguousarray(np.sin(emb).astype(np.float32).T)
    return cos, sin


def _rot_lhsT():
    r = np.zeros((DH, DH), dtype=np.float32)
    for m in range(64):
        r[m + 64, m] = -1.0
    for m in range(64, 128):
        r[m - 64, m] = 1.0
    return r.astype(ml_dtypes.bfloat16)


def _tri_mask():
    m = (np.arange(128)[:, None] <= np.arange(128)[None, :]).astype(np.float32)
    return m.astype(ml_dtypes.bfloat16)


class Ctr:

    def __init__(self, stack, nc, name):
        self.h = stack.enter_context(nc.semaphore(name))
        self.v = 0

    def inc(self, instr, n=1):
        instr.then_inc(self.h, n)
        self.v += n
        return self.v


def _cells(off):
    if off < 512:
        return [(off, 512), (512, 1024)]
    return [(off, 1024)]


def build_nc():
    nc = bass.Bass("TRN2", target_bir_lowering=False, debug=False)

    x_d = nc.dram_tensor("x", [T, D], F32, kind="ExternalInput").ap()
    wq_d = nc.dram_tensor("wq", [D, D], F32, kind="ExternalInput").ap()
    wk_d = nc.dram_tensor("wk", [D, NKV * DH], F32, kind="ExternalInput").ap()
    wv_d = nc.dram_tensor("wv", [D, NKV * DH], F32, kind="ExternalInput").ap()
    wo_d = nc.dram_tensor("wo", [D, D], F32, kind="ExternalInput").ap()
    wb_d = nc.dram_tensor("wb", [1024, 1024], F32, kind="ExternalInput").ap()
    wrppT_d = nc.dram_tensor("wrppT", [D, 24], F32, kind="ExternalInput").ap()
    base24_d = nc.dram_tensor("base24", [1, 24], F32, kind="ExternalInput").ap()
    delta16_d = nc.dram_tensor("delta16", [1, 16], F32, kind="ExternalInput").ap()
    gate_d = nc.dram_tensor("gate", [1, 1], F32, kind="ExternalInput").ap()
    cos_d = nc.dram_tensor("cos_fm", [DH, T], F32, kind="ExternalInput").ap()
    sin_d = nc.dram_tensor("sin_fm", [DH, T], F32, kind="ExternalInput").ap()
    ident_d = nc.dram_tensor("ident", [128, 128], BF16, kind="ExternalInput").ap()
    rmat_d = nc.dram_tensor("rmat", [128, 128], BF16, kind="ExternalInput").ap()
    tri_d = nc.dram_tensor("tri", [128, 128], BF16, kind="ExternalInput").ap()

    out_d = nc.dram_tensor("out", [T, D], F32, kind="ExternalOutput").ap()

    scratch_kind = "ExternalOutput" if _DEV else "Internal"
    qs_d = nc.dram_tensor("qs", [NH * DH, T], BF16, kind=scratch_kind).ap()
    ks_d = nc.dram_tensor("ks", [NKV * DH, T], BF16, kind=scratch_kind).ap()
    vs_d = nc.dram_tensor("vs", [T, NKV * DH], BF16, kind=scratch_kind).ap()
    ss_d = nc.dram_tensor("ss", [T, D], F32, kind=scratch_kind).ap()
    xns_d = nc.dram_tensor("xns", [T, D], BF16, kind=scratch_kind).ap()
    gs_d2 = nc.dram_tensor("gsp", [T, D], F32, kind=scratch_kind).ap()
    ys_d = nc.dram_tensor("ys", [T, 1024], BF16, kind=scratch_kind).ap()

    with ExitStack() as st:
        e = st.enter_context

        env = {}
        env.update(x_d=x_d, wq_d=wq_d, wk_d=wk_d, wv_d=wv_d, wo_d=wo_d, wb_d=wb_d,
                  wrppT_d=wrppT_d, base24_d=base24_d, delta16_d=delta16_d,
                  gate_d=gate_d, cos_d=cos_d, sin_d=sin_d, ident_d=ident_d,
                  rmat_d=rmat_d, tri_d=tri_d, out_d=out_d, qs_d=qs_d, ks_d=ks_d,
                  vs_d=vs_d, ss_d=ss_d, xns_d=xns_d, gs_d2=gs_d2, ys_d=ys_d)

        env["ident"] = e(nc.sbuf_tensor("identb", [128, 128], BF16))
        env["rmat"] = e(nc.sbuf_tensor("rmatb", [128, 128], BF16))
        env["tri"] = e(nc.sbuf_tensor("trib", [128, 128], BF16))
        env["wrppT"] = e(nc.sbuf_tensor("wrppTb", [128, NC_ * 24], BF16))
        env["base_bc"] = e(nc.sbuf_tensor("base_bc", [128, 24], F32))
        env["i1mg"] = e(nc.sbuf_tensor("i1mg", [128, 16], F32))
        env["gcol"] = e(nc.sbuf_tensor("gcol", [128, 1], F32))
        env["ones_bf"] = e(nc.sbuf_tensor("ones_bf", [128, 1], BF16))
        env["ones_f1"] = e(nc.sbuf_tensor("ones_f1", [1, 128], F32))
        env["shiftc"] = e(nc.sbuf_tensor("shiftc", [128, 1], F32))
        env["m_st"] = e(nc.sbuf_tensor("m_st", [128, NT], F32))
        env["rstd_st"] = e(nc.sbuf_tensor("rstd_st", [128, NT], F32))
        env["rstd2_st"] = e(nc.sbuf_tensor("rstd2_st", [128, NT], F32))
        env["tmp_st"] = e(nc.sbuf_tensor("tmp_st", [128, 8 * NT], F32))
        env["lbuf"] = e(nc.sbuf_tensor("lbuf", [128, NT * 24], F32))
        env["hres"] = e(nc.sbuf_tensor("hres", [128, NT * 16], F32))
        env["hpre"] = e(nc.sbuf_tensor("hpre", [128, NT * 4], F32))
        env["hpost"] = e(nc.sbuf_tensor("hpost", [128, NT * 4], F32))

        for nm in ("ds", "gs", "sp", "pe", "ve", "ac", "po", "os_", "sf",
                   "x0", "x1", "w0", "w1", "k0", "k1", "q0", "q1",
                   "l0", "l1", "o0", "o1", "sg", "s0", "s1", "g0", "g1", "sv"):
            env[nm] = Ctr(st, nc, nm)

        with nc.sbuf_tensor("xn_fm", [128, NC_ * T], BF16) as xn_fm, \
             nc.sbuf_tensor("cosb", [128, T], F32) as cosb, \
             nc.sbuf_tensor("sinb", [128, T], F32) as sinb, \
             nc.sbuf_tensor("psink", [128, NT * 16], F32) as psink, \
             nc.sbuf_tensor("mxs", [128, NT], F32) as mxs, \
             nc.sbuf_tensor("avs", [128, NT * 4], F32) as avs, \
             nc.sbuf_tensor("bvs", [128, NT * 4], F32) as bvs, \
             nc.sbuf_tensor("tts", [128, NT * 16], F32) as tts:
            env["xn_fm"] = xn_fm
            env["cosb"] = cosb
            env["sinb"] = sinb
            env.update(psink=psink, mxs=mxs, avs=avs, bvs=bvs, tts=tts)
            _p0(nc, env)
            _p1(nc, env)
            _p1b_setup(nc, env)
            _p1b_iters(nc, env)
            _p2(nc, env)
            _p3(nc, env)
            _p1b_final(nc, env)
        _p35(nc, env)
        with nc.sbuf_tensor("ctx", [128, NH * T], BF16) as ctx:
            env["ctx"] = ctx
            _p4(nc, env)
            _p5(nc, env)

        with nc.Block() as blk:
            @blk.sync
            def _(sync):
                for nm in ("o0", "o1", "sp", "ds", "gs", "x0", "x1", "w0", "w1",
                           "k0", "k1", "q0", "q1", "l0", "l1", "s0", "s1", "g0", "g1"):
                    c = env[nm]
                    sync.wait_ge(c.h, c.v)

    return nc


def _p0(nc, env):
    g = env.__getitem__
    ds, gs, pe, ve, ac = g("ds"), g("gs"), g("pe"), g("ve"), g("ac")

    with nc.psum_tensor("p0ps", [128, 512], F32) as p0ps, \
         nc.sbuf_tensor("g11", [1, 2], F32) as g11, \
         nc.sbuf_tensor("gate_sb", [1, 1], F32) as gate_sb, \
         nc.sbuf_tensor("b24row", [1, 24], F32) as b24row, \
         nc.sbuf_tensor("d16row", [1, 16], F32) as d16row, \
         nc.sbuf_tensor("omg", [128, 1], F32) as omg, \
         nc.Block() as blk:

        @blk.sync
        def _(sync):
            for dst, src in ((g("cosb")[:], g("cos_d")), (g("sinb")[:], g("sin_d")),
                             (g("ident")[:], g("ident_d")), (g("rmat")[:], g("rmat_d")),
                             (g("tri")[:], g("tri_d")), (gate_sb[:], g("gate_d")),
                             (b24row[:], g("base24_d")), (d16row[:], g("delta16_d"))):
                sync.dma_start(dst, src).then_inc(ds.h, 16)
            ds.v += 8 * 16
            sync.wait_ge(ds.h, ds.v)

        @blk.gpsimd
        def _(p):
            p.dma_start(g("wrppT")[:].rearrange("p (c e) -> p c e", e=24),
                        g("wrppT_d").rearrange("(c p) e -> p c e", p=128)).then_inc(gs.h, 16)
            gs.v += 16
            p.wait_ge(gs.h, gs.v)

        @blk.vector
        def _(v):
            v.memset(g("ones_bf")[:], 1.0)
            v.memset(g("shiftc")[:], EXP_SHIFT)
            ve.inc(v.memset(g("ones_f1")[:], 1.0))

        @blk.scalar
        def _(s):
            s.wait_ge(ds.h, ds.v)
            s.activation(g11[:, 0:1], gate_sb[:], AF.Sigmoid)
            ac.inc(s.activation(g11[:, 1:2], gate_sb[:], AF.Sigmoid, scale=-1.0))

        @blk.tensor
        def _(t):
            t.wait_ge(ac.h, ac.v)
            t.wait_ge(ve.h, ve.v)
            t.matmul(p0ps[:, 0:24], g("ones_f1")[:], b24row[:], start=True, stop=True)
            t.matmul(p0ps[:, 24:40], g("ones_f1")[:], d16row[:], start=True, stop=True)
            t.matmul(p0ps[:, 40:41], g("ones_f1")[:], g11[:, 0:1], start=True, stop=True)
            pe.inc(t.matmul(p0ps[:, 41:42], g("ones_f1")[:], g11[:, 1:2],
                            start=True, stop=True))

        @blk.vector
        def _(v):
            sf = g("sf")
            v.wait_ge(pe.h, pe.v)
            v.tensor_copy(g("base_bc")[:], p0ps[:, 0:24])
            v.tensor_copy(g("gcol")[:], p0ps[:, 40:41])
            sf.inc(v.tensor_copy(omg[:], p0ps[:, 41:42]))
            v.wait_ge(sf.h, sf.v)
            ve.inc(v.tensor_scalar(g("i1mg")[:], p0ps[:, 24:40], omg[:], None, OP.mult))


def _p1(nc, env):
    g = env.__getitem__
    x_d, xn_fm, wrppT = g("x_d"), g("xn_fm"), g("wrppT")
    ident, base_bc = g("ident"), g("base_bc")
    m_st, rstd_st, rstd2_st, tmp_st, lbuf = (g("m_st"), g("rstd_st"), g("rstd2_st"),
                                             g("tmp_st"), g("lbuf"))
    ds, pe, ve, ac, sf = g("ds"), g("pe"), g("ve"), g("ac"), g("sf")
    xl = (g("x0"), g("x1"))
    xsp = (g("s0"), g("s1"))
    xns_d = g("xns_d")

    VE_T, AC_T, PE_T = 19, 18, 33

    with nc.sbuf_tensor("xb", [128, 2 * D], F32) as xb, \
         nc.sbuf_tensor("xnb", [128, 2 * D], BF16) as xnb, \
         nc.sbuf_tensor("sqdump", [128, 2048], BF16) as sqdump, \
         nc.psum_tensor("ptr", [128, 2 * 1024], BF16) as ptr, \
         nc.psum_tensor("plog", [128, 512], F32) as plog, \
         nc.Block() as blk:

        ds0, ve0, ac0, pe0 = ds.v, ve.v, ac.v, pe.v
        xl0 = (xl[0].v, xl[1].v)
        xsp0 = (xsp[0].v, xsp[1].v)

        def xl_tgt(i):
            return xl0[i % 2] + (i // 2 + 1) * 16

        def ve_evac(i, c):
            return ve0 + i * VE_T + 2 + (c + 1) // 2

        def ac_evac(i, c):
            return ac0 + i * AC_T + 2 + c // 2 + 1

        @blk.sync
        def _(sync):
            for i in range(NT):
                if i >= 2:
                    sync.wait_ge(ve.h, ve0 + (i - 2) * VE_T + 2)
                    sync.wait_ge(ac.h, ac0 + (i - 2) * AC_T + 1)
                sync.dma_start(xb[:, (i % 2) * D:(i % 2 + 1) * D],
                               x_d.rearrange("(n p) d -> n p d", p=128)[i]
                               ).then_inc(xl[i % 2].h, 16)
                xl[i % 2].v += 16

        @blk.vector
        def _(v):
            for i in range(NT):
                xt = xb[:, (i % 2) * D:(i % 2 + 1) * D]
                c0 = 8 * i
                sums = tmp_st[:, c0:c0 + 1]
                varp = tmp_st[:, c0 + 1:c0 + 2]
                varpe = tmp_st[:, c0 + 2:c0 + 3]
                t1 = tmp_st[:, c0 + 3:c0 + 4]
                sumsq = tmp_st[:, c0 + 4:c0 + 5]
                ue = tmp_st[:, c0 + 5:c0 + 6]
                def _e(instr):
                    sf.inc(instr)
                    v.wait_ge(sf.h, sf.v)
                v.wait_ge(xl[i % 2].h, xl_tgt(i))
                _e(v.reduce_sum(sums, xt, axis=AX.X))
                v.wait_ge(ac.h, ac0 + i * AC_T + 1)
                _e(v.tensor_scalar(m_st[:, i:i + 1], sums, 1.0 / D, None, OP.mult))
                _e(v.tensor_scalar(t1, m_st[:, i:i + 1], m_st[:, i:i + 1], -1.0,
                                   OP.mult, OP.mult))
                _e(v.scalar_tensor_tensor(varp, tmp_st[:, c0 + 6:c0 + 7], 1.0 / D, t1,
                                          OP.mult, OP.add))
                _e(v.scalar_tensor_tensor(varp, sumsq, 1.0 / D, varp, OP.mult, OP.add))
                _e(v.tensor_scalar(varpe, varp, LN_EPS, None, OP.add))
                _e(v.reciprocal(t1, varpe))
                _e(v.scalar_tensor_tensor(ue, varp, 1.0, t1, OP.mult, OP.mult))
                ve.inc(v.tensor_scalar(ue, ue, RMS_EPS, None, OP.add))
                v.wait_ge(ac.h, ac0 + i * AC_T + 2)
                _e(v.reciprocal(rstd_st[:, i:i + 1], t1))
                _e(v.reciprocal(rstd2_st[:, i:i + 1], ue))
                if i >= 2:
                    v.wait_ge(pe.h, pe0 + (i - 2) * PE_T + NC_)
                    v.wait_ge(xsp[i % 2].h, xsp0[i % 2] + ((i - 2) // 2 + 1) * 16)
                ve.inc(v.tensor_scalar(xnb[:, (i % 2) * D:(i % 2 + 1) * D], xt,
                                       m_st[:, i:i + 1], rstd_st[:, i:i + 1],
                                       OP.subtract, OP.mult))
                for c in range(1, NC_, 2):
                    v.wait_ge(pe.h, pe0 + i * PE_T + c + 1)
                    ve.inc(v.tensor_copy(
                        xn_fm[:, c * T + i * 128: c * T + (i + 1) * 128],
                        ptr[:, (c % 2) * 1024:(c % 2) * 1024 + 128]))
                v.wait_ge(pe.h, pe0 + (i + 1) * PE_T)
                ve.inc(v.scalar_tensor_tensor(lbuf[:, i * 24:(i + 1) * 24], plog[:, 0:24],
                                              rstd2_st[:, i:i + 1], base_bc[:],
                                              OP.mult, OP.add))

        @blk.scalar
        def _(s):
            for i in range(NT):
                xt = xb[:, (i % 2) * D:(i % 2 + 1) * D]
                c0 = 8 * i
                s.wait_ge(xl[i % 2].h, xl_tgt(i))
                s.activation(sqdump[:], xt[:, 0:2048], AF.Square,
                             accum_out=tmp_st[:, c0 + 6:c0 + 7])
                ac.inc(s.activation(sqdump[:], xt[:, 2048:4096], AF.Square,
                                    accum_out=tmp_st[:, c0 + 4:c0 + 5]))
                s.wait_ge(ve.h, ve0 + i * VE_T + 1)
                s.activation(tmp_st[:, c0 + 3:c0 + 4], tmp_st[:, c0 + 2:c0 + 3], AF.Sqrt)
                ac.inc(s.activation(tmp_st[:, c0 + 5:c0 + 6],
                                    tmp_st[:, c0 + 5:c0 + 6], AF.Sqrt))
                for c in range(0, NC_, 2):
                    s.wait_ge(pe.h, pe0 + i * PE_T + c + 1)
                    ac.inc(s.copy(
                        xn_fm[:, c * T + i * 128: c * T + (i + 1) * 128],
                        ptr[:, (c % 2) * 1024:(c % 2) * 1024 + 128]))
                s.dma_start(xns_d.rearrange("(n p) d -> n p d", p=128)[i],
                            xnb[:, (i % 2) * D:(i % 2 + 1) * D]).then_inc(xsp[i % 2].h, 16)
                xsp[i % 2].v += 16

        @blk.tensor
        def _(t):
            for i in range(NT):
                t.wait_ge(ve.h, ve0 + i * VE_T + 2)
                for c in range(NC_):
                    if c >= 2:
                        pc = c - 2
                        if pc % 2 == 1:
                            t.wait_ge(ve.h, ve_evac(i, pc))
                        else:
                            t.wait_ge(ac.h, ac_evac(i, pc))
                    elif i > 0:
                        pc = NC_ - 2 + c
                        if pc % 2 == 1:
                            t.wait_ge(ve.h, ve_evac(i - 1, pc))
                        else:
                            t.wait_ge(ac.h, ac_evac(i - 1, pc))
                    pe.inc(t.transpose(ptr[:, (c % 2) * 1024:(c % 2) * 1024 + 128],
                                       xnb[:, (i % 2) * D + c * 128:
                                           (i % 2) * D + (c + 1) * 128],
                                       ident[:]))
                t.wait_ge(ve.h, ve0 + i * VE_T + 18)
                t.wait_ge(ac.h, ac0 + i * AC_T + 18)
                if i > 0:
                    t.wait_ge(ve.h, ve0 + i * VE_T)
                mm = None
                for c in range(NC_):
                    mm = t.matmul(plog[:, 0:24],
                                  xn_fm[:, c * T + i * 128: c * T + (i + 1) * 128],
                                  wrppT[:, c * 24:(c + 1) * 24],
                                  start=(c == 0), stop=(c == NC_ - 1))
                pe.inc(mm)

        assert ve.v == ve0 + NT * VE_T, (ve.v, ve0)
        assert ac.v == ac0 + NT * AC_T
        assert pe.v == pe0 + NT * PE_T


def _sink_views(env):
    G = NT
    psink, tts = env["psink"], env["tts"]
    avs, bvs = env["avs"], env["bvs"]
    return dict(
        pv4=psink[:].rearrange("p (g i j) -> p g i j", i=4, j=4),
        ttv=tts[:].rearrange("p (g i j) -> p g i j", i=4, j=4),
        ttv_t=tts[:].rearrange("p (g i j) -> p g j i", i=4, j=4),
        bRep=bvs[:].rearrange("p (g j) -> p g j", j=4).unsqueeze(2)
                   .broadcast_to([128, G, 4, 4]),
        aRep=avs[:].rearrange("p (g i) -> p g i", i=4).unsqueeze(3)
                   .broadcast_to([128, G, 4, 4]),
        av_v=avs[:].rearrange("p (g i) -> p g i", i=4),
        bv_v=bvs[:].rearrange("p (g j) -> p g j", j=4),
    )


def _p1b_setup(nc, env):
    g = env.__getitem__
    lbuf = g("lbuf")
    ve, ac, sf, po, sv = g("ve"), g("ac"), g("sf"), g("po"), g("sv")
    psink, mxs, bvs = g("psink"), g("mxs"), g("bvs")
    V = _sink_views(env)
    G = NT

    with nc.sbuf_tensor("z10", [128, NT * 16], F32) as z10, \
         nc.Block() as blk:
        z10v = z10[:].rearrange("p (g e) -> p g e", e=16)
        res_view = lbuf[:].rearrange("p (g e) -> p g e", e=24)[:, :, 0:16]

        @blk.vector
        def _(v):
            sf.inc(v.tensor_scalar(z10v, res_view, TAU_INV, None, OP.mult))
            v.wait_ge(sf.h, sf.v)
            sf.inc(v.reduce_max(mxs[:], z10v, axis=AX.X))
            v.wait_ge(sf.h, sf.v)
            ve.inc(v.tensor_tensor(
                z10v, z10v,
                mxs[:].rearrange("p (g o) -> p g o", o=1).broadcast_to([128, G, 16]),
                OP.subtract))

        @blk.scalar
        def _(s):
            s.wait_ge(ve.h, ve.v)
            ac.inc(s.activation(psink[:], z10[:], AF.Exp))

        @blk.gpsimd
        def _(p):
            sg = g("sg")
            sg.inc(p.memset(bvs[:], 1.0))
            p.wait_ge(sg.h, sg.v)
            p.wait_ge(ac.h, ac.v)
            po.inc(p.engine_nop())

    env["sk_po0"] = po.v
    env["sk_sv0"] = sv.v
    env["sk_dve_it"] = 0
    env["sk_pool_it"] = 0


def _sink_pool_emit(p, env, n_iters):
    g = env.__getitem__
    po, sv = g("po"), g("sv")
    V = _sink_views(env)
    po0, sv0 = env["sk_po0"], env["sk_sv0"]
    for _ in range(n_iters):
        it = env["sk_pool_it"]
        if it >= 50:
            return
        env["sk_pool_it"] = it + 1
        if it > 0:
            p.wait_ge(sv.h, sv0 + 2 * it)
        po.inc(p.tensor_tensor(V["ttv"], V["pv4"], V["bRep"], OP.mult))
        p.wait_ge(sv.h, sv0 + 2 * it + 1)
        po.inc(p.tensor_tensor(V["ttv"], V["pv4"], V["aRep"], OP.mult))


def _sink_dve_emit(v, env, n_iters):
    g = env.__getitem__
    po, sv, sf = g("po"), g("sv"), g("sf")
    avs, bvs = g("avs"), g("bvs")
    V = _sink_views(env)
    po0, sv0 = env["sk_po0"], env["sk_sv0"]
    for _ in range(n_iters):
        it = env["sk_dve_it"]
        if it >= 50:
            return
        env["sk_dve_it"] = it + 1
        v.wait_ge(po.h, po0 + 2 * it + 1)
        sf.inc(v.reduce_sum(V["av_v"], V["ttv"], axis=AX.X))
        v.wait_ge(sf.h, sf.v)
        sv.inc(v.reciprocal(avs[:], avs[:]))
        v.wait_ge(po.h, po0 + 2 * it + 2)
        sf.inc(v.reduce_sum(V["bv_v"], V["ttv_t"], axis=AX.X))
        v.wait_ge(sf.h, sf.v)
        sv.inc(v.reciprocal(bvs[:], bvs[:]))


def _p1b_iters(nc, env):
    with nc.Block() as blk:
        @blk.gpsimd
        def _(p):
            _sink_pool_emit(p, env, 50)

        @blk.vector
        def _(v):
            _sink_dve_emit(v, env, 50)


def _p1b_final(nc, env):
    g = env.__getitem__
    lbuf, hres, hpre, hpost = g("lbuf"), g("hres"), g("hpre"), g("hpost")
    i1mg, gcol = g("i1mg"), g("gcol")
    ve, ac, po, sv, sf = g("ve"), g("ac"), g("po"), g("sv"), g("sf")
    tts = g("tts")
    V = _sink_views(env)
    G = NT
    assert env["sk_dve_it"] == 50 and env["sk_pool_it"] == 50, (
        env["sk_dve_it"], env["sk_pool_it"])

    with nc.sbuf_tensor("sm", [128, G * 2], F32) as sm, \
         nc.sbuf_tensor("pexp", [128, G * 8], F32) as pexp, \
         nc.Block() as blk:

        ppv = lbuf[:].rearrange("p (g e) -> p g e", e=24)[:, :, 16:24] \
                     .rearrange("p g (h f) -> p g h f", h=2, f=4)
        pexpv = pexp[:].rearrange("p (g h f) -> p g h f", h=2, f=4)
        smv = sm[:].rearrange("p (g h) -> p g h", h=2)

        @blk.vector
        def _(v):
            v.wait_ge(po.h, po.v)
            v.wait_ge(sv.h, sv.v)
            sf.inc(v.tensor_tensor(V["ttv"], V["ttv"], V["bRep"], OP.mult))
            v.wait_ge(sf.h, sf.v)
            ve.inc(v.scalar_tensor_tensor(
                hres[:].rearrange("p (g q) -> p g q", q=16),
                tts[:].rearrange("p (g q) -> p g q", q=16), gcol[:],
                i1mg[:].unsqueeze(1).broadcast_to([128, G, 16]),
                OP.mult, OP.add))
            sf.inc(v.reduce_max(smv, ppv, axis=AX.X))
            v.wait_ge(sf.h, sf.v)
            ve.inc(v.tensor_tensor(pexpv, ppv,
                                   smv.unsqueeze(3).broadcast_to([128, G, 2, 4]),
                                   OP.subtract))

        @blk.scalar
        def _(s):
            s.wait_ge(ve.h, ve.v)
            ac.inc(s.activation(pexp[:], pexp[:], AF.Exp))

        @blk.vector
        def _(v):
            v.wait_ge(ac.h, ac.v)
            sf.inc(v.reduce_sum(smv, pexpv, axis=AX.X))
            v.wait_ge(sf.h, sf.v)
            sf.inc(v.reciprocal(sm[:], sm[:]))
            v.wait_ge(sf.h, sf.v)
            sf.inc(v.tensor_tensor(pexpv, pexpv,
                                   smv.unsqueeze(3).broadcast_to([128, G, 2, 4]),
                                   OP.mult))
            v.wait_ge(sf.h, sf.v)
            v.tensor_copy(hpre[:].rearrange("p (g f) -> p g f", f=4), pexpv[:, :, 0, :])
            ve.inc(v.tensor_copy(hpost[:].rearrange("p (g f) -> p g f", f=4),
                                 pexpv[:, :, 1, :]))


def _kv_matmuls(t, pe, psum, wslab, xn_fm):
    mm = None
    for c in range(NC_):
        for nb in range(4):
            mm = t.matmul(psum[:, nb * 512:(nb + 1) * 512],
                          wslab[:, c * 128:(c + 1) * 128],
                          xn_fm[:, c * T + nb * 512: c * T + (nb + 1) * 512],
                          start=(c == 0), stop=(c == NC_ - 1))
    return pe.inc(mm)


def _proj_rope_phase(nc, env, n_units, w_d, out_dram, tag):
    g = env.__getitem__
    xn_fm, cosb, sinb, rmat = g("xn_fm"), g("cosb"), g("sinb"), g("rmat")
    gs, sp, pe, ve, ac = g("gs"), g("sp"), g("pe"), g("ve"), g("ac")
    wl = (g("w0"), g("w1"))

    with nc.sbuf_tensor(f"wslab_{tag}", [128, 2 * NC_ * 128], BF16) as wslab, \
         nc.sbuf_tensor(f"raw_{tag}", [128, T], BF16) as raw, \
         nc.sbuf_tensor(f"c_{tag}", [128, T], F32) as cbuf, \
         nc.sbuf_tensor(f"t2_{tag}", [128, T], F32) as t2buf, \
         nc.sbuf_tensor(f"rb_{tag}", [128, T], BF16) as rb, \
         nc.psum_tensor(f"psM_{tag}", [128, T], F32) as psM, \
         nc.psum_tensor(f"psR_{tag}", [128, T], F32) as psR, \
         nc.Block() as blk:

        gs0, pe0, ve0, ac0, sp0 = gs.v, pe.v, ve.v, ac.v, sp.v
        wl0 = (wl[0].v, wl[1].v)

        @blk.gpsimd
        def _(p):
            for u in range(n_units):
                if u >= 2:
                    p.wait_ge(pe.h, pe0 + (u - 2) * 2 + 1)
                p.dma_start(
                    wslab[:, (u % 2) * NC_ * 128:(u % 2 + 1) * NC_ * 128]
                        .rearrange("p (c o) -> p c o", o=128),
                    w_d.rearrange("(c p) o -> p c o", p=128)[:, :, u * 128:(u + 1) * 128]
                ).then_inc(wl[u % 2].h, 16)
                wl[u % 2].v += 16

        @blk.tensor
        def _(t):
            for u in range(n_units):
                t.wait_ge(wl[u % 2].h, wl0[u % 2] + (u // 2 + 1) * 16)
                if u > 0:
                    t.wait_ge(ve.h, ve0 + (u - 1) * 3 + 1)
                    t.wait_ge(ac.h, ac0 + u)
                _kv_matmuls(t, pe, psM, wslab[:, (u % 2) * NC_ * 128:
                                              (u % 2 + 1) * NC_ * 128], xn_fm)
                t.wait_ge(ac.h, ac0 + u + 1)
                if u > 0:
                    t.wait_ge(ve.h, ve0 + (u - 1) * 3 + 2)
                mm = None
                for nb in range(4):
                    mm = t.matmul(psR[:, nb * 512:(nb + 1) * 512], rmat[:],
                                  raw[:, nb * 512:(nb + 1) * 512], start=True, stop=True)
                pe.inc(mm)

        @blk.scalar
        def _(s):
            for u in range(n_units):
                s.wait_ge(pe.h, pe0 + u * 2 + 1)
                ac.inc(s.copy(raw[:], psM[:]))
                if u > 0:
                    s.wait_ge(ve.h, ve0 + u * 3)
                    s.dma_start(out_dram.rearrange("(a p) t -> a p t", p=128)[u - 1],
                                rb[:]).then_inc(sp.h, 16)
            s.wait_ge(ve.h, ve0 + n_units * 3)
            s.dma_start(out_dram.rearrange("(a p) t -> a p t", p=128)[n_units - 1],
                        rb[:]).then_inc(sp.h, 16)
            sp.v += n_units * 16
            s.wait_ge(sp.h, sp.v)

        @blk.vector
        def _(v):
            for u in range(n_units):
                v.wait_ge(pe.h, pe0 + u * 2 + 1)
                if u > 0:
                    v.wait_ge(sp.h, sp0 + u * 16)
                ve.inc(v.tensor_tensor(cbuf[:], psM[:], cosb[:], OP.mult))
                v.wait_ge(pe.h, pe0 + u * 2 + 2)
                ve.inc(v.tensor_tensor(t2buf[:], psR[:], sinb[:], OP.mult))
                v.wait_ge(ve.h, ve.v)
                ve.inc(v.tensor_tensor(rb[:], cbuf[:], t2buf[:], OP.add))

        assert pe.v == pe0 + n_units * 2
        assert ac.v == ac0 + n_units
        assert ve.v == ve0 + n_units * 3


def _p2(nc, env):
    import os
    g = env.__getitem__
    if not os.environ.get("SKIP_K"):
        _proj_rope_phase(nc, env, NKV, g("wk_d"), g("ks_d"), "k")
    if os.environ.get("SKIP_V"):
        return

    xn_fm, ident = g("xn_fm"), g("ident")
    wv_d, vs_d = g("wv_d"), g("vs_d")
    gs, sp, pe, ve, ac = g("gs"), g("sp"), g("pe"), g("ve"), g("ac")
    wl = (g("w0"), g("w1"))

    with nc.sbuf_tensor("wvslab", [128, 2 * NC_ * 128], BF16) as wvslab, \
         nc.sbuf_tensor("vfm", [128, T], BF16) as vfm, \
         nc.sbuf_tensor("vstage", [128, NT * 128], BF16) as vstage, \
         nc.psum_tensor("psV", [128, T], F32) as psV, \
         nc.psum_tensor("psT", [128, 2 * 1024], BF16) as psT, \
         nc.Block() as blk:

        gs0, pe0, ve0, ac0, sp0 = gs.v, pe.v, ve.v, ac.v, sp.v
        wl0 = (wl[0].v, wl[1].v)
        PE_V, VE_V = 17, 16

        @blk.gpsimd
        def _(p):
            for vd in range(NKV):
                if vd >= 2:
                    p.wait_ge(pe.h, pe0 + (vd - 2) * PE_V + 1)
                p.dma_start(
                    wvslab[:, (vd % 2) * NC_ * 128:(vd % 2 + 1) * NC_ * 128]
                        .rearrange("p (c o) -> p c o", o=128),
                    wv_d.rearrange("(c p) o -> p c o", p=128)[:, :, vd * 128:(vd + 1) * 128]
                ).then_inc(wl[vd % 2].h, 16)
                wl[vd % 2].v += 16

        @blk.tensor
        def _(t):
            for vd in range(NKV):
                t.wait_ge(wl[vd % 2].h, wl0[vd % 2] + (vd // 2 + 1) * 16)
                if vd > 0:
                    t.wait_ge(ac.h, ac0 + vd)
                _kv_matmuls(t, pe, psV, wvslab[:, (vd % 2) * NC_ * 128:
                                               (vd % 2 + 1) * NC_ * 128], xn_fm)
                t.wait_ge(ac.h, ac0 + vd + 1)
                for kt in range(NT):
                    if kt >= 2:
                        t.wait_ge(ve.h, ve0 + vd * VE_V + kt - 1)
                    elif vd > 0:
                        t.wait_ge(ve.h, ve0 + vd * VE_V - 2 + kt + 1)
                    pe.inc(t.transpose(psT[:, (kt % 2) * 1024:(kt % 2) * 1024 + 128],
                                       vfm[:, kt * 128:(kt + 1) * 128], ident[:]))

        @blk.scalar
        def _(s):
            for vd in range(NKV):
                s.wait_ge(pe.h, pe0 + vd * PE_V + 1)
                ac.inc(s.copy(vfm[:], psV[:]))
                if vd > 0:
                    s.wait_ge(ve.h, ve0 + vd * VE_V)
                    s.dma_start(
                        vs_d.rearrange("(n p) o -> p n o", p=128)[:, :,
                                                                 (vd - 1) * 128:vd * 128],
                        vstage[:].rearrange("p (n o) -> p n o", o=128)).then_inc(sp.h, 16)
            s.wait_ge(ve.h, ve0 + NKV * VE_V)
            s.dma_start(
                vs_d.rearrange("(n p) o -> p n o", p=128)[:, :, (NKV - 1) * 128:NKV * 128],
                vstage[:].rearrange("p (n o) -> p n o", o=128)).then_inc(sp.h, 16)
            sp.v += NKV * 16
            s.wait_ge(sp.h, sp.v)

        @blk.vector
        def _(v):
            for vd in range(NKV):
                if vd > 0:
                    v.wait_ge(sp.h, sp0 + vd * 16)
                for kt in range(NT):
                    v.wait_ge(pe.h, pe0 + vd * PE_V + 1 + kt + 1)
                    ve.inc(v.tensor_copy(vstage[:, kt * 128:(kt + 1) * 128],
                                         psT[:, (kt % 2) * 1024:(kt % 2) * 1024 + 128]))

        assert pe.v == pe0 + NKV * PE_V
        assert ac.v == ac0 + NKV
        assert ve.v == ve0 + NKV * VE_V


def _p3(nc, env):
    g = env.__getitem__
    _proj_rope_phase(nc, env, NH, g("wq_d"), g("qs_d"), "q")


def _p35(nc, env):
    g = env.__getitem__
    xns_d, ys_d, wb_d = g("xns_d"), g("ys_d"), g("wb_d")
    ident, hpre = g("ident"), g("hpre")
    gs, sp, pe, ve, ac, sf = g("gs"), g("sp"), g("pe"), g("ve"), g("ac"), g("sf")
    ll = (g("l0"), g("l1"))

    with nc.sbuf_tensor("wbs", [128, 8 * 1024], BF16) as wbs, \
         nc.sbuf_tensor("xnb35", [128, 2 * D], BF16) as xnb35, \
         nc.sbuf_tensor("xp35", [128, 1024], F32) as xp, \
         nc.sbuf_tensor("xpa35", [128, 1024], F32) as xpa, \
         nc.sbuf_tensor("xpb35", [128, 2 * 1024], BF16) as xpb, \
         nc.sbuf_tensor("xpT35", [128, 2 * 8 * 128], BF16) as xpT, \
         nc.sbuf_tensor("yt35", [128, 2 * 1024], BF16) as yt, \
         nc.psum_tensor("pT35", [128, 2 * 1024], BF16) as pT35, \
         nc.psum_tensor("psY35", [128, 2 * 1024], F32) as psY, \
         nc.Block() as blk:

        gs0, pe0, ve0, ac0, sp0 = gs.v, pe.v, ve.v, ac.v, sp.v
        ll0 = (ll[0].v, ll[1].v)

        def xpb_done(i):
            return ve0 + i * 10 + 1

        def evT(i, cc):
            return ve0 + i * 10 + 2 + cc

        def yt_done(i):
            return ve0 + (i + 1) * 10

        def tr_done(i, cc):
            return pe0 + i * 9 + cc + 1

        def y_done(i):
            return pe0 + (i + 1) * 9

        @blk.gpsimd
        def _(p):
            p.dma_start(wbs[:].rearrange("p (c o) -> p c o", o=1024),
                        wb_d.rearrange("(c p) o -> p c o", p=128)).then_inc(gs.h, 16)
            gs.v += 16
            p.wait_ge(gs.h, gs.v)

        @blk.sync
        def _(sync):
            for i in range(NT):
                if i >= 2:
                    sync.wait_ge(ve.h, xpb_done(i - 2) + 4)
                sync.dma_start(xnb35[:, (i % 2) * D:(i % 2 + 1) * D],
                               xns_d.rearrange("(n p) d -> n p d", p=128)[i]
                               ).then_inc(ll[i % 2].h, 16)
                ll[i % 2].v += 16

        @blk.vector
        def _(v):
            for i in range(NT):
                xt = xnb35[:, (i % 2) * D:(i % 2 + 1) * D]

                def _e(instr):
                    sf.inc(instr)
                    v.wait_ge(sf.h, sf.v)
                v.wait_ge(ll[i % 2].h, ll0[i % 2] + (i // 2 + 1) * 16)
                _e(v.tensor_scalar(xp[:], xt[:, 0:1024], hpre[:, i * 4:i * 4 + 1],
                                   None, OP.mult))
                _e(v.scalar_tensor_tensor(xpa[:], xt[:, 1024:2048],
                                          hpre[:, i * 4 + 1:i * 4 + 2], xp[:],
                                          OP.mult, OP.add))
                _e(v.scalar_tensor_tensor(xp[:], xt[:, 2048:3072],
                                          hpre[:, i * 4 + 2:i * 4 + 3], xpa[:],
                                          OP.mult, OP.add))
                if i >= 2:
                    v.wait_ge(pe.h, tr_done(i - 2, 7))
                ve.inc(v.scalar_tensor_tensor(xpb[:, (i % 2) * 1024:(i % 2 + 1) * 1024],
                                              xt[:, 3072:4096],
                                              hpre[:, i * 4 + 3:i * 4 + 4], xp[:],
                                              OP.mult, OP.add))
                for cc in range(8):
                    v.wait_ge(pe.h, tr_done(i, cc))
                    ve.inc(v.tensor_copy(
                        xpT[:, (i % 2) * 1024 + cc * 128:(i % 2) * 1024 + (cc + 1) * 128],
                        pT35[:, (cc % 2) * 1024:(cc % 2) * 1024 + 128]))
                v.wait_ge(pe.h, y_done(i))
                if i >= 2:
                    v.wait_ge(sp.h, sp0 + (i - 1) * 16)
                ve.inc(v.tensor_copy(yt[:, (i % 2) * 1024:(i % 2 + 1) * 1024], psY[
                    :, (i % 2) * 1024:(i % 2 + 1) * 1024]))

        @blk.tensor
        def _(t):
            for i in range(NT):
                t.wait_ge(ve.h, xpb_done(i))
                if i == 0:
                    t.wait_ge(gs.h, gs.v)
                for cc in range(8):
                    if cc >= 2:
                        t.wait_ge(ve.h, evT(i, cc - 2))
                    elif i > 0:
                        t.wait_ge(ve.h, evT(i - 1, 6 + cc))
                    pe.inc(t.transpose(pT35[:, (cc % 2) * 1024:(cc % 2) * 1024 + 128],
                                       xpb[:, (i % 2) * 1024 + cc * 128:
                                           (i % 2) * 1024 + (cc + 1) * 128],
                                       ident[:]))
                t.wait_ge(ve.h, evT(i, 7))
                if i > 0:
                    t.wait_ge(ve.h, yt_done(i - 1))
                mm = None
                for cc in range(8):
                    for nb in range(2):
                        mm = t.matmul(
                            psY[:, (i % 2) * 1024 + nb * 512:(i % 2) * 1024 + (nb + 1) * 512],
                            xpT[:, (i % 2) * 1024 + cc * 128:(i % 2) * 1024 + (cc + 1) * 128],
                            wbs[:, cc * 1024 + nb * 512: cc * 1024 + (nb + 1) * 512],
                            start=(cc == 0), stop=(cc == 7))
                pe.inc(mm)

        @blk.scalar
        def _(s):
            for i in range(NT):
                s.wait_ge(ve.h, yt_done(i))
                s.dma_start(ys_d.rearrange("(n p) c -> n p c", p=128)[i],
                            yt[:, (i % 2) * 1024:(i % 2 + 1) * 1024]).then_inc(sp.h, 16)
            sp.v += NT * 16
            s.wait_ge(sp.h, sp.v)

        assert pe.v == pe0 + NT * 9
        assert ve.v == ve0 + NT * 10


def _p4(nc, env):
    g = env.__getitem__
    qs_d, ks_d, vs_d = g("qs_d"), g("ks_d"), g("vs_d")
    xns_d, ys_d, gs_d2 = g("xns_d"), g("ys_d"), g("gs_d2")
    ctx, tri, ones_bf, ones_f1 = g("ctx"), g("tri"), g("ones_bf"), g("ones_f1")
    shiftc = g("shiftc")
    hres, hpost = g("hres"), g("hpost")
    pe, ve, ac, po, sf = g("pe"), g("ve"), g("ac"), g("po"), g("sf")
    kl = (g("k0"), g("k1"))
    ql = (g("q0"), g("q1"))
    ol = (g("o0"), g("o1"))
    yl = (g("l0"), g("l1"))
    gsp = (g("g0"), g("g1"))

    steps = [(hh, qc) for hh in range(NH) for qc in range(4)]
    NS = len(steps)

    with nc.sbuf_tensor("ksl", [128, 2 * T], BF16) as ksl, \
         nc.sbuf_tensor("vsl", [128, 2 * NT * 128], BF16) as vsl, \
         nc.sbuf_tensor("qh", [128, 2 * 512], BF16) as qh, \
         nc.sbuf_tensor("pbuf", [128, 6 * 512], BF16) as pbuf, \
         nc.sbuf_tensor("invs", [1, 2 * 512], F32) as invs, \
         nc.sbuf_tensor("invbc", [128, 2 * 512], F32) as invbc, \
         nc.sbuf_tensor("xnb4", [128, 2 * 2048], BF16) as xnb4, \
         nc.sbuf_tensor("yt4", [128, 2 * 1024], BF16) as yt4, \
         nc.sbuf_tensor("gpre", [128, D], F32) as gpre, \
         nc.psum_tensor("scp", [128, 3 * 512], F32) as scp, \
         nc.psum_tensor("sbp", [128, 2 * 512], F32) as sbp, \
         nc.psum_tensor("ctxPS", [128, 2 * 512], F32) as ctxPS, \
         nc.Block() as blk:

        pe0, ve0, ac0, po0 = pe.v, ve.v, ac.v, po.v
        kl0 = (kl[0].v, kl[1].v)
        ql0 = (ql[0].v, ql[1].v)
        ol0 = (ol[0].v, ol[1].v)
        yl0 = (yl[0].v, yl[1].v)
        gsp0 = (gsp[0].v, gsp[1].v)
        gv0 = 0

        def kv_tgt(kvh):
            return kl0[kvh % 2] + (kvh // 2 + 1) * 32

        def q_tgt(s_i):
            return ql0[s_i % 2] + (s_i // 2 + 1) * 16

        def yt_tgt(i):
            return yl0[i % 2] + (i // 2 + 1) * 16

        def xa_tgt(i):
            return ol0[0] + (i + 1) * 16

        def xb_tgt(i):
            return ol0[1] + (i + 1) * 16

        def gspill_tgt(i):
            return gsp0[i % 2] + (i // 2 + 1) * 16

        pe_c, ac_c, ve_c, po_c = pe0, ac0, ve0, po0
        scores_done, exp_done, mask_done = {}, {}, {}
        svkt_done, pv_done, bcast_done = {}, {}, {}
        recip_done, evac_done = {}, {}
        g_t0, g_t1, g_t2 = {}, {}, {}

        def g_slots(s_i):
            if s_i % 8 in (0, 2, 4) and s_i // 8 < NT:
                return (s_i // 8, s_i % 8 // 2)
            return None

        for s_i, (hh, qc) in enumerate(steps):
            Q0 = qc * 512
            ktmax = 4 * (qc + 1)
            for kt in range(ktmax):
                ac_c += 1
                exp_done[(s_i, kt)] = ac_c
                if kt * 128 >= Q0:
                    ve_c += 1
                    mask_done[(s_i, kt)] = ve_c
            for kt in range(ktmax):
                pe_c += 1
                scores_done[(s_i, kt)] = pe_c
                if s_i > 0 and kt == 1:
                    pe_c += 1
                    bcast_done[s_i - 1] = pe_c
                if kt >= 2:
                    pe_c += 2
                    svkt_done[(s_i, kt - 2)] = pe_c
            for kt in (ktmax - 2, ktmax - 1):
                pe_c += 2
                svkt_done[(s_i, kt)] = pe_c
            pv_done[s_i] = pe_c
            ve_c += 1
            recip_done[s_i] = ve_c
            ve_c += 2
            evac_done[s_i] = ve_c
            gsl = g_slots(s_i)
            if gsl is not None:
                i, piece = gsl
                ve_c += 1
                (g_t0 if piece == 0 else g_t1 if piece == 1 else g_t2)[i] = ve_c
                if piece == 2:
                    po_c += 1
        pe_c += 1
        bcast_done[NS - 1] = pe_c

        @blk.sync
        def _(sync):
            for s_i, (hh, qc) in enumerate(steps):
                kvh = hh // 4
                if qc == 0 and hh % 4 == 0:
                    if kvh >= 2:
                        sync.wait_ge(pe.h, pv_done[((kvh - 2) * 4 + 3) * 4 + 3])
                    sync.dma_start(ksl[:, (kvh % 2) * T:(kvh % 2 + 1) * T],
                                   ks_d.rearrange("(a p) t -> a p t", p=128)[kvh]
                                   ).then_inc(kl[kvh % 2].h, 16)
                    sync.dma_start(
                        vsl[:, (kvh % 2) * NT * 128:(kvh % 2 + 1) * NT * 128]
                            .rearrange("p (n o) -> p n o", o=128),
                        vs_d.rearrange("(n p) o -> p n o", p=128)[:, :,
                                                                 kvh * 128:(kvh + 1) * 128]
                    ).then_inc(kl[kvh % 2].h, 16)
                    kl[kvh % 2].v += 32
                if s_i >= 2:
                    sync.wait_ge(pe.h, pv_done[s_i - 2])
                sync.dma_start(qh[:, (s_i % 2) * 512:(s_i % 2 + 1) * 512],
                               qs_d.rearrange("(a p) t -> a p t", p=128)[hh]
                               [:, qc * 512:(qc + 1) * 512]).then_inc(ql[s_i % 2].h, 16)
                ql[s_i % 2].v += 16
                if s_i % 8 == 0 and s_i // 8 < NT:
                    i = s_i // 8
                    if i >= 2:
                        sync.wait_ge(ve.h, g_t0[i - 2])
                    sync.dma_start(yt4[:, (i % 2) * 1024:(i % 2 + 1) * 1024],
                                   ys_d.rearrange("(n p) c -> n p c", p=128)[i]
                                   ).then_inc(yl[i % 2].h, 16)
                    yl[i % 2].v += 16
                    if i >= 1:
                        sync.wait_ge(ve.h, g_t1[i - 1])
                    sync.dma_start(
                        xnb4[:, 0:2048],
                        xns_d.rearrange("(n p) d -> n p d", p=128)[i][:, 0:2048]
                    ).then_inc(ol[0].h, 16)
                    ol[0].v += 16
                    if i >= 1:
                        sync.wait_ge(ve.h, g_t2[i - 1])
                    sync.dma_start(
                        xnb4[:, 2048:4096],
                        xns_d.rearrange("(n p) d -> n p d", p=128)[i][:, 2048:4096]
                    ).then_inc(ol[1].h, 16)
                    ol[1].v += 16

        @blk.tensor
        def _(t):
            def emit_sv(s_i, kt, Q0, ktmax, kvh):
                qlo = max(Q0, kt * 128)
                off = qlo - Q0
                preg_kt = ((s_i % 2) * 3 + (kt % 3)) * 512
                vsl_o = (kvh % 2) * NT * 128
                if (s_i, kt) in mask_done:
                    t.wait_ge(ve.h, mask_done[(s_i, kt)])
                else:
                    t.wait_ge(ac.h, exp_done[(s_i, kt)])
                if kt == 0 and s_i > 1:
                    t.wait_ge(ve.h, evac_done[s_i - 2])
                mm = t.matmul(sbp[0:1, (s_i % 2) * 512 + off:(s_i % 2) * 512 + 512],
                              ones_bf[:], pbuf[:, preg_kt + off: preg_kt + 512],
                              start=(kt == 0), stop=(kt == ktmax - 1))
                pe.inc(mm)
                mm = t.matmul(ctxPS[:, (s_i % 2) * 512 + off:(s_i % 2) * 512 + 512],
                              vsl[:, vsl_o + kt * 128: vsl_o + (kt + 1) * 128],
                              pbuf[:, preg_kt + off: preg_kt + 512],
                              start=(kt == 0), stop=(kt == ktmax - 1))
                pe.inc(mm)

            for s_i, (hh, qc) in enumerate(steps):
                kvh = hh // 4
                Q0 = qc * 512
                ktmax = 4 * (qc + 1)
                ksl_o = (kvh % 2) * T
                t.wait_ge(ql[s_i % 2].h, q_tgt(s_i))
                if qc == 0 and hh % 4 == 0:
                    t.wait_ge(kl[kvh % 2].h, kv_tgt(kvh))
                for kt in range(ktmax):
                    qlo = max(Q0, kt * 128)
                    off = qlo - Q0
                    w = 512 - off
                    if kt >= 3:
                        t.wait_ge(ac.h, exp_done[(s_i, kt - 3)])
                    elif s_i > 0:
                        pmax = 4 * (steps[s_i - 1][1] + 1)
                        t.wait_ge(ac.h, exp_done[(s_i - 1, pmax - 1)])
                    mm = t.matmul(scp[:, (kt % 3) * 512:(kt % 3) * 512 + w],
                                  ksl[:, ksl_o + kt * 128: ksl_o + (kt + 1) * 128],
                                  qh[:, (s_i % 2) * 512 + off:(s_i % 2 + 1) * 512],
                                  start=True, stop=True)
                    pe.inc(mm)
                    if s_i > 0 and kt == 1:
                        ps = s_i - 1
                        t.wait_ge(ve.h, recip_done[ps])
                        mm = t.matmul(sbp[:, (ps % 2) * 512:(ps % 2 + 1) * 512],
                                      ones_f1[:], invs[:, (ps % 2) * 512:(ps % 2 + 1) * 512],
                                      start=True, stop=True)
                        pe.inc(mm)
                    if kt >= 2:
                        emit_sv(s_i, kt - 2, Q0, ktmax, kvh)
                emit_sv(s_i, ktmax - 2, Q0, ktmax, kvh)
                emit_sv(s_i, ktmax - 1, Q0, ktmax, kvh)
            t.wait_ge(ve.h, recip_done[NS - 1])
            mm = t.matmul(sbp[:, ((NS - 1) % 2) * 512:((NS - 1) % 2 + 1) * 512],
                          ones_f1[:], invs[:, ((NS - 1) % 2) * 512:((NS - 1) % 2 + 1) * 512],
                          start=True, stop=True)
            pe.inc(mm)

        @blk.scalar
        def _(s):
            for s_i, (hh, qc) in enumerate(steps):
                Q0 = qc * 512
                ktmax = 4 * (qc + 1)
                for kt in range(ktmax):
                    qlo = max(Q0, kt * 128)
                    off = qlo - Q0
                    w = 512 - off
                    preg_kt = ((s_i % 2) * 3 + (kt % 3)) * 512
                    s.wait_ge(pe.h, scores_done[(s_i, kt)])
                    if kt >= 3:
                        s.wait_ge(pe.h, svkt_done[(s_i, kt - 3)])
                    elif s_i >= 2:
                        s.wait_ge(pe.h, pv_done[s_i - 2])
                    ac.inc(s.activation(
                        pbuf[:, preg_kt + off: preg_kt + off + w],
                        scp[:, (kt % 3) * 512:(kt % 3) * 512 + w],
                        AF.Exp, bias=shiftc[:], scale=INV_SQRT_DH))

        @blk.vector
        def _(v):
            for s_i, (hh, qc) in enumerate(steps):
                Q0 = qc * 512
                ktmax = 4 * (qc + 1)
                for kt in range(4 * qc, ktmax):
                    off = kt * 128 - Q0
                    preg_kt = ((s_i % 2) * 3 + (kt % 3)) * 512
                    v.wait_ge(ac.h, exp_done[(s_i, kt)])
                    ve.inc(v.tensor_tensor(
                        pbuf[:, preg_kt + off: preg_kt + off + 128],
                        pbuf[:, preg_kt + off: preg_kt + off + 128],
                        tri[:], OP.mult))
                v.wait_ge(pe.h, pv_done[s_i])
                ve.inc(v.reciprocal(invs[:, (s_i % 2) * 512:(s_i % 2 + 1) * 512],
                                    sbp[0:1, (s_i % 2) * 512:(s_i % 2 + 1) * 512]))
                v.wait_ge(pe.h, bcast_done[s_i])
                ve.inc(v.tensor_copy(invbc[:, (s_i % 2) * 512:(s_i % 2 + 1) * 512],
                                     sbp[:, (s_i % 2) * 512:(s_i % 2 + 1) * 512]))
                v.wait_ge(ve.h, ve.v)
                ve.inc(v.tensor_tensor(
                    ctx[:, hh * T + qc * 512: hh * T + (qc + 1) * 512],
                    ctxPS[:, (s_i % 2) * 512:(s_i % 2 + 1) * 512],
                    invbc[:, (s_i % 2) * 512:(s_i % 2 + 1) * 512], OP.mult))
                gsl = g_slots(s_i)
                if gsl is not None:
                    i, piece = gsl
                    if piece == 0:
                        if i >= 1:
                            v.wait_ge(gsp[(i - 1) % 2].h, gspill_tgt(i - 1))
                        v.wait_ge(yl[i % 2].h, yt_tgt(i))
                        for istr in range(4):
                            tsp = v.tensor_scalar(
                                gpre[:, istr * 1024:(istr + 1) * 1024],
                                yt4[:, (i % 2) * 1024:(i % 2 + 1) * 1024],
                                hpost[:, i * 4 + istr:i * 4 + istr + 1], None, OP.mult)
                        ve.inc(tsp)
                    elif piece == 1:
                        v.wait_ge(ol[0].h, xa_tgt(i))
                        for j in (0, 1):
                            for istr in range(4):
                                stt = v.scalar_tensor_tensor(
                                    gpre[:, istr * 1024:(istr + 1) * 1024],
                                    xnb4[:, j * 1024:(j + 1) * 1024],
                                    hres[:, i * 16 + istr * 4 + j:
                                         i * 16 + istr * 4 + j + 1],
                                    gpre[:, istr * 1024:(istr + 1) * 1024],
                                    OP.mult, OP.add)
                        ve.inc(stt)
                    else:
                        v.wait_ge(ol[1].h, xb_tgt(i))
                        for j in (2, 3):
                            for istr in range(4):
                                stt = v.scalar_tensor_tensor(
                                    gpre[:, istr * 1024:(istr + 1) * 1024],
                                    xnb4[:, j * 1024:(j + 1) * 1024],
                                    hres[:, i * 16 + istr * 4 + j:
                                         i * 16 + istr * 4 + j + 1],
                                    gpre[:, istr * 1024:(istr + 1) * 1024],
                                    OP.mult, OP.add)
                        ve.inc(stt)

        @blk.gpsimd
        def _(p):
            for i in range(NT):
                p.wait_ge(ve.h, g_t2[i])
                p.dma_start(gs_d2.rearrange("(n p) d -> n p d", p=128)[i],
                            gpre[:]).then_inc(gsp[i % 2].h, 16)
                gsp[i % 2].v += 16
            p.wait_ge(gsp[0].h, gsp[0].v)
            p.wait_ge(gsp[1].h, gsp[1].v)

        assert pe.v == pe_c, (pe.v, pe_c)
        assert ac.v == ac_c
        assert ve.v == ve_c


def _p5(nc, env):
    g = env.__getitem__
    wo_d, ctx, x_d, out_d, gs_d2 = g("wo_d"), g("ctx"), g("x_d"), g("out_d"), g("gs_d2")
    gs, pe, ve, ac, po = g("gs"), g("pe"), g("ve"), g("ac"), g("po")
    wl = (g("w0"), g("w1"))
    ll = (g("l0"), g("l1"))
    ol_ = (g("o0"), g("o1"))

    NOC = 16
    OC = 256

    with nc.sbuf_tensor("woblk", [128, 2 * NC_ * OC], BF16) as woblk, \
         nc.sbuf_tensor("xcols", [128, 2 * OC], F32) as xcols, \
         nc.sbuf_tensor("gcols", [128, 2 * OC], F32) as gcols, \
         nc.sbuf_tensor("gbuf", [128, 2 * OC], F32) as gbuf, \
         nc.sbuf_tensor("tbuf", [128, 2 * OC], F32) as tbuf, \
         nc.sbuf_tensor("obuf", [128, 2 * OC], F32) as obuf, \
         nc.psum_tensor("sPS", [128, 2 * 512], F32) as sPS, \
         nc.Block() as blk:

        pe0, ve0, ac0, po0 = pe.v, ve.v, ac.v, po.v
        wl0 = (wl[0].v, wl[1].v)
        ll0 = (ll[0].v, ll[1].v)
        ol0 = (ol_[0].v, ol_[1].v)
        NK = NOC * NT

        def ld_tgt(k):
            return ll0[k % 2] + (k // 2 + 1) * 32

        def out_tgt(k):
            return ol0[k % 2] + (k // 2 + 1) * 16

        def _emit_wold(p, oc):
            if oc >= 2:
                p.wait_ge(pe.h, pe0 + (oc - 1) * NT)
            p.dma_start(
                woblk[:, (oc % 2) * NC_ * OC:(oc % 2 + 1) * NC_ * OC]
                    .rearrange("p (c o) -> p c o", o=OC),
                wo_d.rearrange("(c p) o -> p c o", p=128)[:, :, oc * OC:(oc + 1) * OC]
            ).then_inc(wl[oc % 2].h, 16)
            wl[oc % 2].v += 16

        @blk.sync
        def _(sync):
            for oc in range(NOC):
                for i in range(NT):
                    k = oc * NT + i
                    if k >= 2:
                        sync.wait_ge(po.h, po0 + k - 1)
                        sync.wait_ge(ac.h, ac0 + k - 1)
                    sync.dma_start(
                        xcols[:, (k % 2) * OC:(k % 2 + 1) * OC],
                        x_d.rearrange("(n p) d -> n p d", p=128)[i][:, oc * OC:(oc + 1) * OC]
                    ).then_inc(ll[k % 2].h, 16)
                    sync.dma_start(
                        gcols[:, (k % 2) * OC:(k % 2 + 1) * OC],
                        gs_d2.rearrange("(n p) d -> n p d", p=128)[i][:, oc * OC:(oc + 1) * OC]
                    ).then_inc(ll[k % 2].h, 16)
                    ll[k % 2].v += 32
                    if k >= 2:
                        kk = k - 2
                        oc2, i2 = kk // NT, kk % NT
                        sync.wait_ge(po.h, po0 + kk + 1)
                        sync.dma_start(
                            out_d.rearrange("(n p) d -> n p d", p=128)[i2]
                            [:, oc2 * OC:(oc2 + 1) * OC],
                            obuf[:, (kk % 2) * OC:(kk % 2 + 1) * OC]
                        ).then_inc(ol_[kk % 2].h, 16)
                        ol_[kk % 2].v += 16
            for kk in (NK - 2, NK - 1):
                oc2, i2 = kk // NT, kk % NT
                sync.wait_ge(po.h, po0 + kk + 1)
                sync.dma_start(
                    out_d.rearrange("(n p) d -> n p d", p=128)[i2]
                    [:, oc2 * OC:(oc2 + 1) * OC],
                    obuf[:, (kk % 2) * OC:(kk % 2 + 1) * OC]
                ).then_inc(ol_[kk % 2].h, 16)
                ol_[kk % 2].v += 16

        @blk.tensor
        def _(t):
            for oc in range(NOC):
                t.wait_ge(wl[oc % 2].h, wl0[oc % 2] + (oc // 2 + 1) * 16)
                for i in range(NT):
                    k = oc * NT + i
                    if k >= 2:
                        t.wait_ge(ve.h, ve0 + k - 1)
                    mm = None
                    for c in range(NC_):
                        mm = t.matmul(sPS[:, (k % 2) * 512:(k % 2) * 512 + OC],
                                      ctx[:, c * T + i * 128: c * T + (i + 1) * 128],
                                      woblk[:, (oc % 2) * NC_ * OC + c * OC:
                                            (oc % 2) * NC_ * OC + (c + 1) * OC],
                                      start=(c == 0), stop=(c == NC_ - 1))
                    pe.inc(mm)

        @blk.scalar
        def _(s):
            for oc in range(NOC):
                for i in range(NT):
                    k = oc * NT + i
                    s.wait_ge(ll[k % 2].h, ld_tgt(k))
                    if k >= 2:
                        s.wait_ge(ve.h, ve0 + k - 1)
                    ac.inc(s.activation(gbuf[:, (k % 2) * OC:(k % 2 + 1) * OC],
                                        gcols[:, (k % 2) * OC:(k % 2 + 1) * OC],
                                        AF.Silu))

        @blk.vector
        def _(v):
            for oc in range(NOC):
                for i in range(NT):
                    k = oc * NT + i
                    v.wait_ge(pe.h, pe0 + k + 1)
                    v.wait_ge(ac.h, ac0 + k + 1)
                    if k >= 2:
                        v.wait_ge(po.h, po0 + k - 1)
                    ve.inc(v.tensor_tensor(tbuf[:, (k % 2) * OC:(k % 2 + 1) * OC],
                                           sPS[:, (k % 2) * 512:(k % 2) * 512 + OC],
                                           gbuf[:, (k % 2) * OC:(k % 2 + 1) * OC],
                                           OP.mult))

        @blk.gpsimd
        def _(p):
            _emit_wold(p, 0)
            _emit_wold(p, 1)
            for oc in range(NOC):
                for i in range(NT):
                    k = oc * NT + i
                    p.wait_ge(ve.h, ve0 + k + 1)
                    p.wait_ge(ll[k % 2].h, ld_tgt(k))
                    if k >= 2:
                        p.wait_ge(ol_[k % 2].h, out_tgt(k - 2))
                    po.inc(p.tensor_tensor(obuf[:, (k % 2) * OC:(k % 2 + 1) * OC],
                                           tbuf[:, (k % 2) * OC:(k % 2 + 1) * OC],
                                           xcols[:, (k % 2) * OC:(k % 2 + 1) * OC],
                                           OP.add))
                if oc + 2 < NOC:
                    _emit_wold(p, oc + 2)

        assert pe.v == pe0 + NK
        assert ve.v == ve0 + NK
        assert ac.v == ac0 + NK
        assert po.v == po0 + NK


def kernel(L0_attn_resid, L0_attn_Wq, L0_attn_Wk, L0_attn_Wv, L0_attn_Wo,
           L0_ffn_W_res, L0_ffn_W_pre, L0_ffn_W_post, L0_ffn_H_res_base,
           L0_ffn_H_pre_base, L0_ffn_H_post_base, L0_ffn_gate, L0_ffn_W_branch):
    x = np.ascontiguousarray(np.asarray(L0_attn_resid, dtype=np.float32))
    cos_fm, sin_fm = _rope_tables()
    wrppT = np.ascontiguousarray(
        np.concatenate([np.asarray(L0_ffn_W_res), np.asarray(L0_ffn_W_pre),
                        np.asarray(L0_ffn_W_post)], axis=0).T.astype(np.float32))
    base24 = np.concatenate([
        np.asarray(L0_ffn_H_res_base, dtype=np.float32).reshape(16),
        np.asarray(L0_ffn_H_pre_base, dtype=np.float32).reshape(4),
        np.asarray(L0_ffn_H_post_base, dtype=np.float32).reshape(4)]).reshape(1, 24)
    delta16 = np.eye(4, dtype=np.float32).reshape(1, 16)
    gate = np.asarray(L0_ffn_gate, dtype=np.float32).reshape(1, 1)

    common = {
        "wq": np.ascontiguousarray(L0_attn_Wq, dtype=np.float32),
        "wk": np.ascontiguousarray(L0_attn_Wk, dtype=np.float32),
        "wv": np.ascontiguousarray(L0_attn_Wv, dtype=np.float32),
        "wo": np.ascontiguousarray(L0_attn_Wo, dtype=np.float32),
        "wb": np.ascontiguousarray(L0_ffn_W_branch, dtype=np.float32),
        "wrppT": wrppT, "base24": base24, "delta16": delta16, "gate": gate,
        "cos_fm": cos_fm, "sin_fm": sin_fm,
        "ident": np.eye(128, dtype=ml_dtypes.bfloat16),
        "rmat": _rot_lhsT(), "tri": _tri_mask(),
    }
    in_maps = [dict(common, x=np.ascontiguousarray(x[b])) for b in range(B)]

    nc = build_nc()
    res = run_bass_kernel_spmd(nc, in_maps, core_ids=list(range(B)))
    return np.stack([np.asarray(r["out"], dtype=np.float32) for r in res.results], axis=0)


# version 13
# speedup vs baseline: 1.0902x; 1.0205x over previous
import math
import numpy as np
import ml_dtypes
from contextlib import ExitStack

import concourse.bass as bass
import concourse.mybir as mybir
from concourse.bass_utils import run_bass_kernel_spmd

F32 = mybir.dt.float32
BF16 = mybir.dt.bfloat16
OP = mybir.AluOpType
AF = mybir.ActivationFunctionType
AX = mybir.AxisListType

B, T, D = 8, 2048, 4096
NH, NKV, DH = 32, 8, 128
NT = T // 128
NC_ = D // 128
INV_SQRT_DH = 1.0 / math.sqrt(DH)
EXP_SHIFT = -12.0
LN_EPS = 1e-5
RMS_EPS = 1.1920929e-07
TAU_INV = 10.0

_DEV = False


def _rope_tables():
    idx = np.arange(0, DH, 2, dtype=np.float64) / DH
    freq = 1.0 / (2.0 * math.pi) * (500000.0 ** (-idx))
    factor, lo, hi, L0 = 32.0, 1.0, 4.0, 8192.0
    freq_low, freq_high = lo / L0, hi / L0
    freq_scaled = np.where(freq < freq_low, freq / factor, freq)
    smooth = np.clip((L0 * freq - lo) / (hi - lo), 0.0, 1.0)
    freq_smooth = (1.0 - smooth) * (freq / factor) + smooth * freq
    is_mid = (freq >= freq_low) & (freq <= freq_high)
    freq = np.where(is_mid, freq_smooth, freq_scaled)
    phase = 2.0 * math.pi * np.arange(T, dtype=np.float64)[:, None] * freq[None, :]
    emb = np.concatenate([phase, phase], axis=-1)
    cos = np.ascontiguousarray(np.cos(emb).astype(np.float32).T)
    sin = np.ascontiguousarray(np.sin(emb).astype(np.float32).T)
    return cos, sin


def _rot_lhsT():
    r = np.zeros((DH, DH), dtype=np.float32)
    for m in range(64):
        r[m + 64, m] = -1.0
    for m in range(64, 128):
        r[m - 64, m] = 1.0
    return r.astype(ml_dtypes.bfloat16)


def _tri_mask():
    m = (np.arange(128)[:, None] <= np.arange(128)[None, :]).astype(np.float32)
    return m.astype(ml_dtypes.bfloat16)


class Ctr:

    def __init__(self, stack, nc, name):
        self.h = stack.enter_context(nc.semaphore(name))
        self.v = 0

    def inc(self, instr, n=1):
        instr.then_inc(self.h, n)
        self.v += n
        return self.v


def _cells(off):
    if off < 512:
        return [(off, 512), (512, 1024)]
    return [(off, 1024)]


def build_nc():
    nc = bass.Bass("TRN2", target_bir_lowering=False, debug=False)

    x_d = nc.dram_tensor("x", [T, D], F32, kind="ExternalInput").ap()
    wq_d = nc.dram_tensor("wq", [D, D], F32, kind="ExternalInput").ap()
    wk_d = nc.dram_tensor("wk", [D, NKV * DH], F32, kind="ExternalInput").ap()
    wv_d = nc.dram_tensor("wv", [D, NKV * DH], F32, kind="ExternalInput").ap()
    wo_d = nc.dram_tensor("wo", [D, D], F32, kind="ExternalInput").ap()
    wb_d = nc.dram_tensor("wb", [1024, 1024], F32, kind="ExternalInput").ap()
    wrppT_d = nc.dram_tensor("wrppT", [D, 24], F32, kind="ExternalInput").ap()
    base24_d = nc.dram_tensor("base24", [1, 24], F32, kind="ExternalInput").ap()
    delta16_d = nc.dram_tensor("delta16", [1, 16], F32, kind="ExternalInput").ap()
    gate_d = nc.dram_tensor("gate", [1, 1], F32, kind="ExternalInput").ap()
    cos_d = nc.dram_tensor("cos_fm", [DH, T], F32, kind="ExternalInput").ap()
    sin_d = nc.dram_tensor("sin_fm", [DH, T], F32, kind="ExternalInput").ap()
    ident_d = nc.dram_tensor("ident", [128, 128], BF16, kind="ExternalInput").ap()
    rmat_d = nc.dram_tensor("rmat", [128, 128], BF16, kind="ExternalInput").ap()
    tri_d = nc.dram_tensor("tri", [128, 128], BF16, kind="ExternalInput").ap()

    out_d = nc.dram_tensor("out", [T, D], F32, kind="ExternalOutput").ap()

    scratch_kind = "ExternalOutput" if _DEV else "Internal"
    qs_d = nc.dram_tensor("qs", [NH * DH, T], BF16, kind=scratch_kind).ap()
    ks_d = nc.dram_tensor("ks", [NKV * DH, T], BF16, kind=scratch_kind).ap()
    vs_d = nc.dram_tensor("vs", [T, NKV * DH], BF16, kind=scratch_kind).ap()
    ss_d = nc.dram_tensor("ss", [T, D], F32, kind=scratch_kind).ap()
    xns_d = nc.dram_tensor("xns", [T, D], BF16, kind=scratch_kind).ap()
    gs_d2 = nc.dram_tensor("gsp", [T, D], F32, kind=scratch_kind).ap()
    ys_d = nc.dram_tensor("ys", [T, 1024], BF16, kind=scratch_kind).ap()

    with ExitStack() as st:
        e = st.enter_context

        env = {}
        env.update(x_d=x_d, wq_d=wq_d, wk_d=wk_d, wv_d=wv_d, wo_d=wo_d, wb_d=wb_d,
                  wrppT_d=wrppT_d, base24_d=base24_d, delta16_d=delta16_d,
                  gate_d=gate_d, cos_d=cos_d, sin_d=sin_d, ident_d=ident_d,
                  rmat_d=rmat_d, tri_d=tri_d, out_d=out_d, qs_d=qs_d, ks_d=ks_d,
                  vs_d=vs_d, ss_d=ss_d, xns_d=xns_d, gs_d2=gs_d2, ys_d=ys_d)

        env["ident"] = e(nc.sbuf_tensor("identb", [128, 128], BF16))
        env["rmat"] = e(nc.sbuf_tensor("rmatb", [128, 128], BF16))
        env["tri"] = e(nc.sbuf_tensor("trib", [128, 128], BF16))
        env["wrppT"] = e(nc.sbuf_tensor("wrppTb", [128, NC_ * 24], BF16))
        env["base_bc"] = e(nc.sbuf_tensor("base_bc", [128, 24], F32))
        env["i1mg"] = e(nc.sbuf_tensor("i1mg", [128, 16], F32))
        env["gcol"] = e(nc.sbuf_tensor("gcol", [128, 1], F32))
        env["ones_bf"] = e(nc.sbuf_tensor("ones_bf", [128, 1], BF16))
        env["ones_f1"] = e(nc.sbuf_tensor("ones_f1", [1, 128], F32))
        env["shiftc"] = e(nc.sbuf_tensor("shiftc", [128, 1], F32))
        env["m_st"] = e(nc.sbuf_tensor("m_st", [128, NT], F32))
        env["rstd_st"] = e(nc.sbuf_tensor("rstd_st", [128, NT], F32))
        env["rstd2_st"] = e(nc.sbuf_tensor("rstd2_st", [128, NT], F32))
        env["tmp_st"] = e(nc.sbuf_tensor("tmp_st", [128, 8 * NT], F32))
        env["lbuf"] = e(nc.sbuf_tensor("lbuf", [128, NT * 24], F32))
        env["hres"] = e(nc.sbuf_tensor("hres", [128, NT * 16], F32))
        env["hpre"] = e(nc.sbuf_tensor("hpre", [128, NT * 4], F32))
        env["hpost"] = e(nc.sbuf_tensor("hpost", [128, NT * 4], F32))

        for nm in ("ds", "gs", "sp", "pe", "ve", "ac", "po", "os_", "sf",
                   "x0", "x1", "w0", "w1", "k0", "k1", "q0", "q1",
                   "l0", "l1", "o0", "o1", "sg", "s0", "s1", "g0", "g1", "sv"):
            env[nm] = Ctr(st, nc, nm)

        with nc.sbuf_tensor("xn_fm", [128, NC_ * T], BF16) as xn_fm, \
             nc.sbuf_tensor("cosb", [128, T], F32) as cosb, \
             nc.sbuf_tensor("sinb", [128, T], F32) as sinb, \
             nc.sbuf_tensor("psink", [128, NT * 16], F32) as psink, \
             nc.sbuf_tensor("mxs", [128, NT], F32) as mxs, \
             nc.sbuf_tensor("avs", [128, NT * 4], F32) as avs, \
             nc.sbuf_tensor("bvs", [128, NT * 4], F32) as bvs, \
             nc.sbuf_tensor("tts", [128, NT * 16], F32) as tts:
            env["xn_fm"] = xn_fm
            env["cosb"] = cosb
            env["sinb"] = sinb
            env.update(psink=psink, mxs=mxs, avs=avs, bvs=bvs, tts=tts)
            _p0(nc, env)
            _p1(nc, env)
            _p1b_setup(nc, env)
            _p1b_iters(nc, env)
            _p2(nc, env)
            _p3(nc, env)
            _p1b_final(nc, env)
        _p35(nc, env)
        with nc.sbuf_tensor("ctx", [128, NH * T], BF16) as ctx:
            env["ctx"] = ctx
            _p4(nc, env)
            _p5(nc, env)

        with nc.Block() as blk:
            @blk.sync
            def _(sync):
                for nm in ("o0", "o1", "sp", "ds", "gs", "x0", "x1", "w0", "w1",
                           "k0", "k1", "q0", "q1", "l0", "l1", "s0", "s1", "g0", "g1"):
                    c = env[nm]
                    sync.wait_ge(c.h, c.v)

    return nc


def _p0(nc, env):
    g = env.__getitem__
    ds, gs, pe, ve, ac = g("ds"), g("gs"), g("pe"), g("ve"), g("ac")

    with nc.psum_tensor("p0ps", [128, 512], F32) as p0ps, \
         nc.sbuf_tensor("g11", [1, 2], F32) as g11, \
         nc.sbuf_tensor("gate_sb", [1, 1], F32) as gate_sb, \
         nc.sbuf_tensor("b24row", [1, 24], F32) as b24row, \
         nc.sbuf_tensor("d16row", [1, 16], F32) as d16row, \
         nc.sbuf_tensor("omg", [128, 1], F32) as omg, \
         nc.Block() as blk:

        @blk.sync
        def _(sync):
            for dst, src in ((g("cosb")[:], g("cos_d")), (g("sinb")[:], g("sin_d")),
                             (g("ident")[:], g("ident_d")), (g("rmat")[:], g("rmat_d")),
                             (g("tri")[:], g("tri_d")), (gate_sb[:], g("gate_d")),
                             (b24row[:], g("base24_d")), (d16row[:], g("delta16_d"))):
                sync.dma_start(dst, src).then_inc(ds.h, 16)
            ds.v += 8 * 16
            sync.wait_ge(ds.h, ds.v)

        @blk.gpsimd
        def _(p):
            p.dma_start(g("wrppT")[:].rearrange("p (c e) -> p c e", e=24),
                        g("wrppT_d").rearrange("(c p) e -> p c e", p=128)).then_inc(gs.h, 16)
            gs.v += 16
            p.wait_ge(gs.h, gs.v)

        @blk.vector
        def _(v):
            v.memset(g("ones_bf")[:], 1.0)
            v.memset(g("shiftc")[:], EXP_SHIFT)
            ve.inc(v.memset(g("ones_f1")[:], 1.0))

        @blk.scalar
        def _(s):
            s.wait_ge(ds.h, ds.v)
            s.activation(g11[:, 0:1], gate_sb[:], AF.Sigmoid)
            ac.inc(s.activation(g11[:, 1:2], gate_sb[:], AF.Sigmoid, scale=-1.0))

        @blk.tensor
        def _(t):
            t.wait_ge(ac.h, ac.v)
            t.wait_ge(ve.h, ve.v)
            t.matmul(p0ps[:, 0:24], g("ones_f1")[:], b24row[:], start=True, stop=True)
            t.matmul(p0ps[:, 24:40], g("ones_f1")[:], d16row[:], start=True, stop=True)
            t.matmul(p0ps[:, 40:41], g("ones_f1")[:], g11[:, 0:1], start=True, stop=True)
            pe.inc(t.matmul(p0ps[:, 41:42], g("ones_f1")[:], g11[:, 1:2],
                            start=True, stop=True))

        @blk.vector
        def _(v):
            sf = g("sf")
            v.wait_ge(pe.h, pe.v)
            v.tensor_copy(g("base_bc")[:], p0ps[:, 0:24])
            v.tensor_copy(g("gcol")[:], p0ps[:, 40:41])
            sf.inc(v.tensor_copy(omg[:], p0ps[:, 41:42]))
            v.wait_ge(sf.h, sf.v)
            ve.inc(v.tensor_scalar(g("i1mg")[:], p0ps[:, 24:40], omg[:], None, OP.mult))


def _p1(nc, env):
    g = env.__getitem__
    x_d, xn_fm, wrppT = g("x_d"), g("xn_fm"), g("wrppT")
    ident, base_bc = g("ident"), g("base_bc")
    m_st, rstd_st, rstd2_st, tmp_st, lbuf = (g("m_st"), g("rstd_st"), g("rstd2_st"),
                                             g("tmp_st"), g("lbuf"))
    ds, pe, ve, ac, sf = g("ds"), g("pe"), g("ve"), g("ac"), g("sf")
    xl = (g("x0"), g("x1"))
    xsp = (g("s0"), g("s1"))
    xns_d = g("xns_d")

    VE_T, AC_T, PE_T = 19, 18, 33

    with nc.sbuf_tensor("xb", [128, 2 * D], F32) as xb, \
         nc.sbuf_tensor("xnb", [128, 2 * D], BF16) as xnb, \
         nc.sbuf_tensor("sqdump", [128, 2048], BF16) as sqdump, \
         nc.psum_tensor("ptr", [128, 2 * 1024], BF16) as ptr, \
         nc.psum_tensor("plog", [128, 512], F32) as plog, \
         nc.Block() as blk:

        ds0, ve0, ac0, pe0 = ds.v, ve.v, ac.v, pe.v
        xl0 = (xl[0].v, xl[1].v)
        xsp0 = (xsp[0].v, xsp[1].v)

        def xl_tgt(i):
            return xl0[i % 2] + (i // 2 + 1) * 16

        def ve_evac(i, c):
            return ve0 + i * VE_T + 2 + (c + 1) // 2

        def ac_evac(i, c):
            return ac0 + i * AC_T + 2 + c // 2 + 1

        @blk.sync
        def _(sync):
            for i in range(NT):
                if i >= 2:
                    sync.wait_ge(ve.h, ve0 + (i - 2) * VE_T + 2)
                    sync.wait_ge(ac.h, ac0 + (i - 2) * AC_T + 1)
                sync.dma_start(xb[:, (i % 2) * D:(i % 2 + 1) * D],
                               x_d.rearrange("(n p) d -> n p d", p=128)[i]
                               ).then_inc(xl[i % 2].h, 16)
                xl[i % 2].v += 16

        @blk.vector
        def _(v):
            for i in range(NT):
                xt = xb[:, (i % 2) * D:(i % 2 + 1) * D]
                c0 = 8 * i
                sums = tmp_st[:, c0:c0 + 1]
                varp = tmp_st[:, c0 + 1:c0 + 2]
                varpe = tmp_st[:, c0 + 2:c0 + 3]
                t1 = tmp_st[:, c0 + 3:c0 + 4]
                sumsq = tmp_st[:, c0 + 4:c0 + 5]
                ue = tmp_st[:, c0 + 5:c0 + 6]
                def _e(instr):
                    sf.inc(instr)
                    v.wait_ge(sf.h, sf.v)
                v.wait_ge(xl[i % 2].h, xl_tgt(i))
                _e(v.reduce_sum(sums, xt, axis=AX.X))
                v.wait_ge(ac.h, ac0 + i * AC_T + 1)
                _e(v.tensor_scalar(m_st[:, i:i + 1], sums, 1.0 / D, None, OP.mult))
                _e(v.tensor_scalar(t1, m_st[:, i:i + 1], m_st[:, i:i + 1], -1.0,
                                   OP.mult, OP.mult))
                _e(v.scalar_tensor_tensor(varp, tmp_st[:, c0 + 6:c0 + 7], 1.0 / D, t1,
                                          OP.mult, OP.add))
                _e(v.scalar_tensor_tensor(varp, sumsq, 1.0 / D, varp, OP.mult, OP.add))
                _e(v.tensor_scalar(varpe, varp, LN_EPS, None, OP.add))
                _e(v.reciprocal(t1, varpe))
                _e(v.scalar_tensor_tensor(ue, varp, 1.0, t1, OP.mult, OP.mult))
                ve.inc(v.tensor_scalar(ue, ue, RMS_EPS, None, OP.add))
                v.wait_ge(ac.h, ac0 + i * AC_T + 2)
                _e(v.reciprocal(rstd_st[:, i:i + 1], t1))
                _e(v.reciprocal(rstd2_st[:, i:i + 1], ue))
                if i >= 2:
                    v.wait_ge(pe.h, pe0 + (i - 2) * PE_T + NC_)
                    v.wait_ge(xsp[i % 2].h, xsp0[i % 2] + ((i - 2) // 2 + 1) * 16)
                ve.inc(v.tensor_scalar(xnb[:, (i % 2) * D:(i % 2 + 1) * D], xt,
                                       m_st[:, i:i + 1], rstd_st[:, i:i + 1],
                                       OP.subtract, OP.mult))
                for c in range(1, NC_, 2):
                    v.wait_ge(pe.h, pe0 + i * PE_T + c + 1)
                    ve.inc(v.tensor_copy(
                        xn_fm[:, c * T + i * 128: c * T + (i + 1) * 128],
                        ptr[:, (c % 2) * 1024:(c % 2) * 1024 + 128]))
                v.wait_ge(pe.h, pe0 + (i + 1) * PE_T)
                ve.inc(v.scalar_tensor_tensor(lbuf[:, i * 24:(i + 1) * 24], plog[:, 0:24],
                                              rstd2_st[:, i:i + 1], base_bc[:],
                                              OP.mult, OP.add))

        @blk.scalar
        def _(s):
            for i in range(NT):
                xt = xb[:, (i % 2) * D:(i % 2 + 1) * D]
                c0 = 8 * i
                s.wait_ge(xl[i % 2].h, xl_tgt(i))
                s.activation(sqdump[:], xt[:, 0:2048], AF.Square,
                             accum_out=tmp_st[:, c0 + 6:c0 + 7])
                ac.inc(s.activation(sqdump[:], xt[:, 2048:4096], AF.Square,
                                    accum_out=tmp_st[:, c0 + 4:c0 + 5]))
                s.wait_ge(ve.h, ve0 + i * VE_T + 1)
                s.activation(tmp_st[:, c0 + 3:c0 + 4], tmp_st[:, c0 + 2:c0 + 3], AF.Sqrt)
                ac.inc(s.activation(tmp_st[:, c0 + 5:c0 + 6],
                                    tmp_st[:, c0 + 5:c0 + 6], AF.Sqrt))
                for c in range(0, NC_, 2):
                    s.wait_ge(pe.h, pe0 + i * PE_T + c + 1)
                    ac.inc(s.copy(
                        xn_fm[:, c * T + i * 128: c * T + (i + 1) * 128],
                        ptr[:, (c % 2) * 1024:(c % 2) * 1024 + 128]))
                s.dma_start(xns_d.rearrange("(n p) d -> n p d", p=128)[i],
                            xnb[:, (i % 2) * D:(i % 2 + 1) * D]).then_inc(xsp[i % 2].h, 16)
                xsp[i % 2].v += 16

        @blk.tensor
        def _(t):
            for i in range(NT):
                t.wait_ge(ve.h, ve0 + i * VE_T + 2)
                for c in range(NC_):
                    if c >= 2:
                        pc = c - 2
                        if pc % 2 == 1:
                            t.wait_ge(ve.h, ve_evac(i, pc))
                        else:
                            t.wait_ge(ac.h, ac_evac(i, pc))
                    elif i > 0:
                        pc = NC_ - 2 + c
                        if pc % 2 == 1:
                            t.wait_ge(ve.h, ve_evac(i - 1, pc))
                        else:
                            t.wait_ge(ac.h, ac_evac(i - 1, pc))
                    pe.inc(t.transpose(ptr[:, (c % 2) * 1024:(c % 2) * 1024 + 128],
                                       xnb[:, (i % 2) * D + c * 128:
                                           (i % 2) * D + (c + 1) * 128],
                                       ident[:]))
                t.wait_ge(ve.h, ve0 + i * VE_T + 18)
                t.wait_ge(ac.h, ac0 + i * AC_T + 18)
                if i > 0:
                    t.wait_ge(ve.h, ve0 + i * VE_T)
                mm = None
                for c in range(NC_):
                    mm = t.matmul(plog[:, 0:24],
                                  xn_fm[:, c * T + i * 128: c * T + (i + 1) * 128],
                                  wrppT[:, c * 24:(c + 1) * 24],
                                  start=(c == 0), stop=(c == NC_ - 1))
                pe.inc(mm)

        assert ve.v == ve0 + NT * VE_T, (ve.v, ve0)
        assert ac.v == ac0 + NT * AC_T
        assert pe.v == pe0 + NT * PE_T


def _sink_views(env):
    G = NT
    psink, tts = env["psink"], env["tts"]
    avs, bvs = env["avs"], env["bvs"]
    return dict(
        pv4=psink[:].rearrange("p (g i j) -> p g i j", i=4, j=4),
        ttv=tts[:].rearrange("p (g i j) -> p g i j", i=4, j=4),
        ttv_t=tts[:].rearrange("p (g i j) -> p g j i", i=4, j=4),
        bRep=bvs[:].rearrange("p (g j) -> p g j", j=4).unsqueeze(2)
                   .broadcast_to([128, G, 4, 4]),
        aRep=avs[:].rearrange("p (g i) -> p g i", i=4).unsqueeze(3)
                   .broadcast_to([128, G, 4, 4]),
        av_v=avs[:].rearrange("p (g i) -> p g i", i=4),
        bv_v=bvs[:].rearrange("p (g j) -> p g j", j=4),
    )


def _p1b_setup(nc, env):
    g = env.__getitem__
    lbuf = g("lbuf")
    ve, ac, sf, po, sv = g("ve"), g("ac"), g("sf"), g("po"), g("sv")
    psink, mxs, bvs = g("psink"), g("mxs"), g("bvs")
    V = _sink_views(env)
    G = NT

    with nc.sbuf_tensor("z10", [128, NT * 16], F32) as z10, \
         nc.Block() as blk:
        z10v = z10[:].rearrange("p (g e) -> p g e", e=16)
        res_view = lbuf[:].rearrange("p (g e) -> p g e", e=24)[:, :, 0:16]

        @blk.vector
        def _(v):
            sf.inc(v.tensor_scalar(z10v, res_view, TAU_INV, None, OP.mult))
            v.wait_ge(sf.h, sf.v)
            sf.inc(v.reduce_max(mxs[:], z10v, axis=AX.X))
            v.wait_ge(sf.h, sf.v)
            ve.inc(v.tensor_tensor(
                z10v, z10v,
                mxs[:].rearrange("p (g o) -> p g o", o=1).broadcast_to([128, G, 16]),
                OP.subtract))

        @blk.scalar
        def _(s):
            s.wait_ge(ve.h, ve.v)
            ac.inc(s.activation(psink[:], z10[:], AF.Exp))

        @blk.gpsimd
        def _(p):
            sg = g("sg")
            sg.inc(p.memset(bvs[:], 1.0))
            p.wait_ge(sg.h, sg.v)
            p.wait_ge(ac.h, ac.v)
            po.inc(p.engine_nop())

    env["sk_po0"] = po.v
    env["sk_sv0"] = sv.v
    env["sk_dve_it"] = 0
    env["sk_pool_it"] = 0


def _sink_pool_emit(p, env, n_iters):
    g = env.__getitem__
    po, sv = g("po"), g("sv")
    V = _sink_views(env)
    po0, sv0 = env["sk_po0"], env["sk_sv0"]
    for _ in range(n_iters):
        it = env["sk_pool_it"]
        if it >= 50:
            return
        env["sk_pool_it"] = it + 1
        if it > 0:
            p.wait_ge(sv.h, sv0 + 2 * it)
        po.inc(p.tensor_tensor(V["ttv"], V["pv4"], V["bRep"], OP.mult))
        p.wait_ge(sv.h, sv0 + 2 * it + 1)
        po.inc(p.tensor_tensor(V["ttv"], V["pv4"], V["aRep"], OP.mult))


def _sink_dve_emit(v, env, n_iters):
    g = env.__getitem__
    po, sv, sf = g("po"), g("sv"), g("sf")
    avs, bvs = g("avs"), g("bvs")
    V = _sink_views(env)
    po0, sv0 = env["sk_po0"], env["sk_sv0"]
    for _ in range(n_iters):
        it = env["sk_dve_it"]
        if it >= 50:
            return
        env["sk_dve_it"] = it + 1
        v.wait_ge(po.h, po0 + 2 * it + 1)
        sf.inc(v.reduce_sum(V["av_v"], V["ttv"], axis=AX.X))
        v.wait_ge(sf.h, sf.v)
        sv.inc(v.reciprocal(avs[:], avs[:]))
        v.wait_ge(po.h, po0 + 2 * it + 2)
        sf.inc(v.reduce_sum(V["bv_v"], V["ttv_t"], axis=AX.X))
        v.wait_ge(sf.h, sf.v)
        sv.inc(v.reciprocal(bvs[:], bvs[:]))


def _p1b_iters(nc, env):
    with nc.Block() as blk:
        @blk.gpsimd
        def _(p):
            _sink_pool_emit(p, env, 50)

        @blk.vector
        def _(v):
            _sink_dve_emit(v, env, 50)


def _p1b_final(nc, env):
    g = env.__getitem__
    lbuf, hres, hpre, hpost = g("lbuf"), g("hres"), g("hpre"), g("hpost")
    i1mg, gcol = g("i1mg"), g("gcol")
    ve, ac, po, sv, sf = g("ve"), g("ac"), g("po"), g("sv"), g("sf")
    tts = g("tts")
    V = _sink_views(env)
    G = NT
    assert env["sk_dve_it"] == 50 and env["sk_pool_it"] == 50, (
        env["sk_dve_it"], env["sk_pool_it"])

    with nc.sbuf_tensor("sm", [128, G * 2], F32) as sm, \
         nc.sbuf_tensor("pexp", [128, G * 8], F32) as pexp, \
         nc.Block() as blk:

        ppv = lbuf[:].rearrange("p (g e) -> p g e", e=24)[:, :, 16:24] \
                     .rearrange("p g (h f) -> p g h f", h=2, f=4)
        pexpv = pexp[:].rearrange("p (g h f) -> p g h f", h=2, f=4)
        smv = sm[:].rearrange("p (g h) -> p g h", h=2)

        @blk.vector
        def _(v):
            v.wait_ge(po.h, po.v)
            v.wait_ge(sv.h, sv.v)
            sf.inc(v.tensor_tensor(V["ttv"], V["ttv"], V["bRep"], OP.mult))
            v.wait_ge(sf.h, sf.v)
            ve.inc(v.scalar_tensor_tensor(
                hres[:].rearrange("p (g q) -> p g q", q=16),
                tts[:].rearrange("p (g q) -> p g q", q=16), gcol[:],
                i1mg[:].unsqueeze(1).broadcast_to([128, G, 16]),
                OP.mult, OP.add))
            sf.inc(v.reduce_max(smv, ppv, axis=AX.X))
            v.wait_ge(sf.h, sf.v)
            ve.inc(v.tensor_tensor(pexpv, ppv,
                                   smv.unsqueeze(3).broadcast_to([128, G, 2, 4]),
                                   OP.subtract))

        @blk.scalar
        def _(s):
            s.wait_ge(ve.h, ve.v)
            ac.inc(s.activation(pexp[:], pexp[:], AF.Exp))

        @blk.vector
        def _(v):
            v.wait_ge(ac.h, ac.v)
            sf.inc(v.reduce_sum(smv, pexpv, axis=AX.X))
            v.wait_ge(sf.h, sf.v)
            sf.inc(v.reciprocal(sm[:], sm[:]))
            v.wait_ge(sf.h, sf.v)
            sf.inc(v.tensor_tensor(pexpv, pexpv,
                                   smv.unsqueeze(3).broadcast_to([128, G, 2, 4]),
                                   OP.mult))
            v.wait_ge(sf.h, sf.v)
            v.tensor_copy(hpre[:].rearrange("p (g f) -> p g f", f=4), pexpv[:, :, 0, :])
            ve.inc(v.tensor_copy(hpost[:].rearrange("p (g f) -> p g f", f=4),
                                 pexpv[:, :, 1, :]))


def _kv_matmuls(t, pe, psum, wslab, xn_fm):
    mm = None
    for c in range(NC_):
        for nb in range(4):
            mm = t.matmul(psum[:, nb * 512:(nb + 1) * 512],
                          wslab[:, c * 128:(c + 1) * 128],
                          xn_fm[:, c * T + nb * 512: c * T + (nb + 1) * 512],
                          start=(c == 0), stop=(c == NC_ - 1))
    return pe.inc(mm)


def _proj_rope_phase(nc, env, n_units, w_d, out_dram, tag):
    g = env.__getitem__
    xn_fm, cosb, sinb, rmat = g("xn_fm"), g("cosb"), g("sinb"), g("rmat")
    gs, sp, pe, ve, ac = g("gs"), g("sp"), g("pe"), g("ve"), g("ac")
    wl = (g("w0"), g("w1"))

    with nc.sbuf_tensor(f"wslab_{tag}", [128, 2 * NC_ * 128], BF16) as wslab, \
         nc.sbuf_tensor(f"raw_{tag}", [128, T], BF16) as raw, \
         nc.sbuf_tensor(f"c_{tag}", [128, T], F32) as cbuf, \
         nc.sbuf_tensor(f"t2_{tag}", [128, T], F32) as t2buf, \
         nc.sbuf_tensor(f"rb_{tag}", [128, T], BF16) as rb, \
         nc.psum_tensor(f"psM_{tag}", [128, T], F32) as psM, \
         nc.psum_tensor(f"psR_{tag}", [128, T], F32) as psR, \
         nc.Block() as blk:

        gs0, pe0, ve0, ac0, sp0 = gs.v, pe.v, ve.v, ac.v, sp.v
        wl0 = (wl[0].v, wl[1].v)

        @blk.gpsimd
        def _(p):
            for u in range(n_units):
                if u >= 2:
                    p.wait_ge(pe.h, pe0 + (u - 2) * 2 + 1)
                p.dma_start(
                    wslab[:, (u % 2) * NC_ * 128:(u % 2 + 1) * NC_ * 128]
                        .rearrange("p (c o) -> p c o", o=128),
                    w_d.rearrange("(c p) o -> p c o", p=128)[:, :, u * 128:(u + 1) * 128]
                ).then_inc(wl[u % 2].h, 16)
                wl[u % 2].v += 16

        @blk.tensor
        def _(t):
            for u in range(n_units):
                t.wait_ge(wl[u % 2].h, wl0[u % 2] + (u // 2 + 1) * 16)
                if u > 0:
                    t.wait_ge(ve.h, ve0 + (u - 1) * 3 + 1)
                    t.wait_ge(ac.h, ac0 + u)
                _kv_matmuls(t, pe, psM, wslab[:, (u % 2) * NC_ * 128:
                                              (u % 2 + 1) * NC_ * 128], xn_fm)
                t.wait_ge(ac.h, ac0 + u + 1)
                if u > 0:
                    t.wait_ge(ve.h, ve0 + (u - 1) * 3 + 2)
                mm = None
                for nb in range(4):
                    mm = t.matmul(psR[:, nb * 512:(nb + 1) * 512], rmat[:],
                                  raw[:, nb * 512:(nb + 1) * 512], start=True, stop=True)
                pe.inc(mm)

        @blk.scalar
        def _(s):
            for u in range(n_units):
                s.wait_ge(pe.h, pe0 + u * 2 + 1)
                ac.inc(s.copy(raw[:], psM[:]))
                if u > 0:
                    s.wait_ge(ve.h, ve0 + u * 3)
                    s.dma_start(out_dram.rearrange("(a p) t -> a p t", p=128)[u - 1],
                                rb[:]).then_inc(sp.h, 16)
            s.wait_ge(ve.h, ve0 + n_units * 3)
            s.dma_start(out_dram.rearrange("(a p) t -> a p t", p=128)[n_units - 1],
                        rb[:]).then_inc(sp.h, 16)
            sp.v += n_units * 16
            s.wait_ge(sp.h, sp.v)

        @blk.vector
        def _(v):
            for u in range(n_units):
                v.wait_ge(pe.h, pe0 + u * 2 + 1)
                if u > 0:
                    v.wait_ge(sp.h, sp0 + u * 16)
                ve.inc(v.tensor_tensor(cbuf[:], psM[:], cosb[:], OP.mult))
                v.wait_ge(pe.h, pe0 + u * 2 + 2)
                ve.inc(v.tensor_tensor(t2buf[:], psR[:], sinb[:], OP.mult))
                v.wait_ge(ve.h, ve.v)
                ve.inc(v.tensor_tensor(rb[:], cbuf[:], t2buf[:], OP.add))

        assert pe.v == pe0 + n_units * 2
        assert ac.v == ac0 + n_units
        assert ve.v == ve0 + n_units * 3


def _p2(nc, env):
    import os
    g = env.__getitem__
    if not os.environ.get("SKIP_K"):
        _proj_rope_phase(nc, env, NKV, g("wk_d"), g("ks_d"), "k")
    if os.environ.get("SKIP_V"):
        return

    xn_fm, ident = g("xn_fm"), g("ident")
    wv_d, vs_d = g("wv_d"), g("vs_d")
    gs, sp, pe, ve, ac = g("gs"), g("sp"), g("pe"), g("ve"), g("ac")
    wl = (g("w0"), g("w1"))

    with nc.sbuf_tensor("wvslab", [128, 2 * NC_ * 128], BF16) as wvslab, \
         nc.sbuf_tensor("vfm", [128, T], BF16) as vfm, \
         nc.sbuf_tensor("vstage", [128, NT * 128], BF16) as vstage, \
         nc.psum_tensor("psV", [128, T], F32) as psV, \
         nc.psum_tensor("psT", [128, 2 * 1024], BF16) as psT, \
         nc.Block() as blk:

        gs0, pe0, ve0, ac0, sp0 = gs.v, pe.v, ve.v, ac.v, sp.v
        wl0 = (wl[0].v, wl[1].v)
        PE_V, VE_V = 17, 16

        @blk.gpsimd
        def _(p):
            for vd in range(NKV):
                if vd >= 2:
                    p.wait_ge(pe.h, pe0 + (vd - 2) * PE_V + 1)
                p.dma_start(
                    wvslab[:, (vd % 2) * NC_ * 128:(vd % 2 + 1) * NC_ * 128]
                        .rearrange("p (c o) -> p c o", o=128),
                    wv_d.rearrange("(c p) o -> p c o", p=128)[:, :, vd * 128:(vd + 1) * 128]
                ).then_inc(wl[vd % 2].h, 16)
                wl[vd % 2].v += 16

        @blk.tensor
        def _(t):
            for vd in range(NKV):
                t.wait_ge(wl[vd % 2].h, wl0[vd % 2] + (vd // 2 + 1) * 16)
                if vd > 0:
                    t.wait_ge(ac.h, ac0 + vd)
                _kv_matmuls(t, pe, psV, wvslab[:, (vd % 2) * NC_ * 128:
                                               (vd % 2 + 1) * NC_ * 128], xn_fm)
                t.wait_ge(ac.h, ac0 + vd + 1)
                for kt in range(NT):
                    if kt >= 2:
                        t.wait_ge(ve.h, ve0 + vd * VE_V + kt - 1)
                    elif vd > 0:
                        t.wait_ge(ve.h, ve0 + vd * VE_V - 2 + kt + 1)
                    pe.inc(t.transpose(psT[:, (kt % 2) * 1024:(kt % 2) * 1024 + 128],
                                       vfm[:, kt * 128:(kt + 1) * 128], ident[:]))

        @blk.scalar
        def _(s):
            for vd in range(NKV):
                s.wait_ge(pe.h, pe0 + vd * PE_V + 1)
                ac.inc(s.copy(vfm[:], psV[:]))
                if vd > 0:
                    s.wait_ge(ve.h, ve0 + vd * VE_V)
                    s.dma_start(
                        vs_d.rearrange("(n p) o -> p n o", p=128)[:, :,
                                                                 (vd - 1) * 128:vd * 128],
                        vstage[:].rearrange("p (n o) -> p n o", o=128)).then_inc(sp.h, 16)
            s.wait_ge(ve.h, ve0 + NKV * VE_V)
            s.dma_start(
                vs_d.rearrange("(n p) o -> p n o", p=128)[:, :, (NKV - 1) * 128:NKV * 128],
                vstage[:].rearrange("p (n o) -> p n o", o=128)).then_inc(sp.h, 16)
            sp.v += NKV * 16
            s.wait_ge(sp.h, sp.v)

        @blk.vector
        def _(v):
            for vd in range(NKV):
                if vd > 0:
                    v.wait_ge(sp.h, sp0 + vd * 16)
                for kt in range(NT):
                    v.wait_ge(pe.h, pe0 + vd * PE_V + 1 + kt + 1)
                    ve.inc(v.tensor_copy(vstage[:, kt * 128:(kt + 1) * 128],
                                         psT[:, (kt % 2) * 1024:(kt % 2) * 1024 + 128]))

        assert pe.v == pe0 + NKV * PE_V
        assert ac.v == ac0 + NKV
        assert ve.v == ve0 + NKV * VE_V


def _p3(nc, env):
    g = env.__getitem__
    _proj_rope_phase(nc, env, NH, g("wq_d"), g("qs_d"), "q")


def _p35(nc, env):
    g = env.__getitem__
    xns_d, ys_d, wb_d = g("xns_d"), g("ys_d"), g("wb_d")
    ident, hpre = g("ident"), g("hpre")
    gs, sp, pe, ve, ac, sf = g("gs"), g("sp"), g("pe"), g("ve"), g("ac"), g("sf")
    ll = (g("l0"), g("l1"))

    with nc.sbuf_tensor("wbs", [128, 8 * 1024], BF16) as wbs, \
         nc.sbuf_tensor("xnb35", [128, 2 * D], BF16) as xnb35, \
         nc.sbuf_tensor("xp35", [128, 1024], F32) as xp, \
         nc.sbuf_tensor("xpa35", [128, 1024], F32) as xpa, \
         nc.sbuf_tensor("xpb35", [128, 2 * 1024], BF16) as xpb, \
         nc.sbuf_tensor("xpT35", [128, 2 * 8 * 128], BF16) as xpT, \
         nc.sbuf_tensor("yt35", [128, 2 * 1024], BF16) as yt, \
         nc.psum_tensor("pT35", [128, 2 * 1024], BF16) as pT35, \
         nc.psum_tensor("psY35", [128, 2 * 1024], F32) as psY, \
         nc.Block() as blk:

        gs0, pe0, ve0, ac0, sp0 = gs.v, pe.v, ve.v, ac.v, sp.v
        ll0 = (ll[0].v, ll[1].v)

        def xpb_done(i):
            return ve0 + i * 10 + 1

        def evT(i, cc):
            return ve0 + i * 10 + 2 + cc

        def yt_done(i):
            return ve0 + (i + 1) * 10

        def tr_done(i, cc):
            return pe0 + i * 9 + cc + 1

        def y_done(i):
            return pe0 + (i + 1) * 9

        @blk.gpsimd
        def _(p):
            p.dma_start(wbs[:].rearrange("p (c o) -> p c o", o=1024),
                        wb_d.rearrange("(c p) o -> p c o", p=128)).then_inc(gs.h, 16)
            gs.v += 16
            p.wait_ge(gs.h, gs.v)

        @blk.sync
        def _(sync):
            for i in range(NT):
                if i >= 2:
                    sync.wait_ge(ve.h, xpb_done(i - 2) + 4)
                sync.dma_start(xnb35[:, (i % 2) * D:(i % 2 + 1) * D],
                               xns_d.rearrange("(n p) d -> n p d", p=128)[i]
                               ).then_inc(ll[i % 2].h, 16)
                ll[i % 2].v += 16

        @blk.vector
        def _(v):
            for i in range(NT):
                xt = xnb35[:, (i % 2) * D:(i % 2 + 1) * D]

                def _e(instr):
                    sf.inc(instr)
                    v.wait_ge(sf.h, sf.v)
                v.wait_ge(ll[i % 2].h, ll0[i % 2] + (i // 2 + 1) * 16)
                _e(v.tensor_scalar(xp[:], xt[:, 0:1024], hpre[:, i * 4:i * 4 + 1],
                                   None, OP.mult))
                _e(v.scalar_tensor_tensor(xpa[:], xt[:, 1024:2048],
                                          hpre[:, i * 4 + 1:i * 4 + 2], xp[:],
                                          OP.mult, OP.add))
                _e(v.scalar_tensor_tensor(xp[:], xt[:, 2048:3072],
                                          hpre[:, i * 4 + 2:i * 4 + 3], xpa[:],
                                          OP.mult, OP.add))
                if i >= 2:
                    v.wait_ge(pe.h, tr_done(i - 2, 7))
                ve.inc(v.scalar_tensor_tensor(xpb[:, (i % 2) * 1024:(i % 2 + 1) * 1024],
                                              xt[:, 3072:4096],
                                              hpre[:, i * 4 + 3:i * 4 + 4], xp[:],
                                              OP.mult, OP.add))
                for cc in range(8):
                    v.wait_ge(pe.h, tr_done(i, cc))
                    ve.inc(v.tensor_copy(
                        xpT[:, (i % 2) * 1024 + cc * 128:(i % 2) * 1024 + (cc + 1) * 128],
                        pT35[:, (cc % 2) * 1024:(cc % 2) * 1024 + 128]))
                v.wait_ge(pe.h, y_done(i))
                if i >= 2:
                    v.wait_ge(sp.h, sp0 + (i - 1) * 16)
                ve.inc(v.tensor_copy(yt[:, (i % 2) * 1024:(i % 2 + 1) * 1024], psY[
                    :, (i % 2) * 1024:(i % 2 + 1) * 1024]))

        @blk.tensor
        def _(t):
            for i in range(NT):
                t.wait_ge(ve.h, xpb_done(i))
                if i == 0:
                    t.wait_ge(gs.h, gs.v)
                for cc in range(8):
                    if cc >= 2:
                        t.wait_ge(ve.h, evT(i, cc - 2))
                    elif i > 0:
                        t.wait_ge(ve.h, evT(i - 1, 6 + cc))
                    pe.inc(t.transpose(pT35[:, (cc % 2) * 1024:(cc % 2) * 1024 + 128],
                                       xpb[:, (i % 2) * 1024 + cc * 128:
                                           (i % 2) * 1024 + (cc + 1) * 128],
                                       ident[:]))
                t.wait_ge(ve.h, evT(i, 7))
                if i > 0:
                    t.wait_ge(ve.h, yt_done(i - 1))
                mm = None
                for cc in range(8):
                    for nb in range(2):
                        mm = t.matmul(
                            psY[:, (i % 2) * 1024 + nb * 512:(i % 2) * 1024 + (nb + 1) * 512],
                            xpT[:, (i % 2) * 1024 + cc * 128:(i % 2) * 1024 + (cc + 1) * 128],
                            wbs[:, cc * 1024 + nb * 512: cc * 1024 + (nb + 1) * 512],
                            start=(cc == 0), stop=(cc == 7))
                pe.inc(mm)

        @blk.scalar
        def _(s):
            for i in range(NT):
                s.wait_ge(ve.h, yt_done(i))
                s.dma_start(ys_d.rearrange("(n p) c -> n p c", p=128)[i],
                            yt[:, (i % 2) * 1024:(i % 2 + 1) * 1024]).then_inc(sp.h, 16)
            sp.v += NT * 16
            s.wait_ge(sp.h, sp.v)

        assert pe.v == pe0 + NT * 9
        assert ve.v == ve0 + NT * 10


def _p4(nc, env):
    g = env.__getitem__
    qs_d, ks_d, vs_d = g("qs_d"), g("ks_d"), g("vs_d")
    xns_d, ys_d, gs_d2 = g("xns_d"), g("ys_d"), g("gs_d2")
    ctx, tri, ones_bf, ones_f1 = g("ctx"), g("tri"), g("ones_bf"), g("ones_f1")
    shiftc = g("shiftc")
    hres, hpost = g("hres"), g("hpost")
    pe, ve, ac, po, sf = g("pe"), g("ve"), g("ac"), g("po"), g("sf")
    kl = (g("k0"), g("k1"))
    ql = (g("q0"), g("q1"))
    ol = (g("o0"), g("o1"))
    yl = (g("l0"), g("l1"))
    gsp = (g("g0"), g("g1"))

    steps = [(hh, qc) for hh in range(NH) for qc in range(4)]
    NS = len(steps)

    with nc.sbuf_tensor("ksl", [128, 2 * T], BF16) as ksl, \
         nc.sbuf_tensor("vsl", [128, 2 * NT * 128], BF16) as vsl, \
         nc.sbuf_tensor("qh", [128, 2 * 512], BF16) as qh, \
         nc.sbuf_tensor("pbuf", [128, 6 * 512], BF16) as pbuf, \
         nc.sbuf_tensor("invs", [1, 2 * 512], F32) as invs, \
         nc.sbuf_tensor("invbc", [128, 2 * 512], F32) as invbc, \
         nc.sbuf_tensor("xnb4", [128, 2 * 2048], BF16) as xnb4, \
         nc.sbuf_tensor("yt4", [128, 2 * 1024], BF16) as yt4, \
         nc.sbuf_tensor("gpre", [128, D], F32) as gpre, \
         nc.psum_tensor("scp", [128, 3 * 512], F32) as scp, \
         nc.psum_tensor("sbp", [128, 2 * 512], F32) as sbp, \
         nc.psum_tensor("ctxPS", [128, 2 * 512], F32) as ctxPS, \
         nc.Block() as blk:

        pe0, ve0, ac0, po0 = pe.v, ve.v, ac.v, po.v
        kl0 = (kl[0].v, kl[1].v)
        ql0 = (ql[0].v, ql[1].v)
        ol0 = (ol[0].v, ol[1].v)
        yl0 = (yl[0].v, yl[1].v)
        gsp0 = (gsp[0].v, gsp[1].v)
        gv0 = 0

        def kv_tgt(kvh):
            return kl0[kvh % 2] + (kvh // 2 + 1) * 32

        def q_tgt(s_i):
            return ql0[s_i % 2] + (s_i // 2 + 1) * 16

        def yt_tgt(i):
            return yl0[i % 2] + (i // 2 + 1) * 16

        def xa_tgt(i):
            return ol0[0] + (i + 1) * 16

        def xb_tgt(i):
            return ol0[1] + (i + 1) * 16

        def gspill_tgt(i):
            return gsp0[i % 2] + (i // 2 + 1) * 16

        pe_c, ac_c, ve_c, po_c = pe0, ac0, ve0, po0
        scores_done, exp_done, mask_done = {}, {}, {}
        svkt_done, pv_done, bcast_done = {}, {}, {}
        recip_done, evac_done = {}, {}
        g_t0, g_t1, g_t2 = {}, {}, {}

        def g_slots(s_i):
            if s_i % 8 in (0, 1, 2, 3, 4) and s_i // 8 < NT:
                return (s_i // 8, s_i % 8)
            return None

        for s_i, (hh, qc) in enumerate(steps):
            Q0 = qc * 512
            ktmax = 4 * (qc + 1)
            for kt in range(ktmax):
                ac_c += 1
                exp_done[(s_i, kt)] = ac_c
                if kt * 128 >= Q0:
                    ve_c += 1
                    mask_done[(s_i, kt)] = ve_c
            bc_kt = 1 if qc == 0 else 3
            for kt in range(ktmax):
                pe_c += 1
                scores_done[(s_i, kt)] = pe_c
                if s_i > 0 and kt == bc_kt:
                    pe_c += 1
                    bcast_done[s_i - 1] = pe_c
                if kt >= 2:
                    pe_c += 2
                    svkt_done[(s_i, kt - 2)] = pe_c
            for kt in (ktmax - 2, ktmax - 1):
                pe_c += 2
                svkt_done[(s_i, kt)] = pe_c
            pv_done[s_i] = pe_c
            ve_c += 1
            recip_done[s_i] = ve_c
            ve_c += 2
            evac_done[s_i] = ve_c
            gsl = g_slots(s_i)
            if gsl is not None:
                i, piece = gsl
                ve_c += 1
                if piece == 0:
                    g_t0[i] = ve_c
                elif piece == 2:
                    g_t1[i] = ve_c
                elif piece == 4:
                    g_t2[i] = ve_c
        pe_c += 1
        bcast_done[NS - 1] = pe_c

        @blk.sync
        def _(sync):
            for s_i, (hh, qc) in enumerate(steps):
                kvh = hh // 4
                if qc == 0 and hh % 4 == 0:
                    if kvh >= 2:
                        sync.wait_ge(pe.h, pv_done[((kvh - 2) * 4 + 3) * 4 + 3])
                    sync.dma_start(ksl[:, (kvh % 2) * T:(kvh % 2 + 1) * T],
                                   ks_d.rearrange("(a p) t -> a p t", p=128)[kvh]
                                   ).then_inc(kl[kvh % 2].h, 16)
                    sync.dma_start(
                        vsl[:, (kvh % 2) * NT * 128:(kvh % 2 + 1) * NT * 128]
                            .rearrange("p (n o) -> p n o", o=128),
                        vs_d.rearrange("(n p) o -> p n o", p=128)[:, :,
                                                                 kvh * 128:(kvh + 1) * 128]
                    ).then_inc(kl[kvh % 2].h, 16)
                    kl[kvh % 2].v += 32
                if s_i >= 2:
                    sync.wait_ge(pe.h, pv_done[s_i - 2])
                sync.dma_start(qh[:, (s_i % 2) * 512:(s_i % 2 + 1) * 512],
                               qs_d.rearrange("(a p) t -> a p t", p=128)[hh]
                               [:, qc * 512:(qc + 1) * 512]).then_inc(ql[s_i % 2].h, 16)
                ql[s_i % 2].v += 16
                if s_i % 8 == 0 and s_i // 8 < NT:
                    i = s_i // 8
                    if i >= 2:
                        sync.wait_ge(ve.h, g_t0[i - 2])
                    sync.dma_start(yt4[:, (i % 2) * 1024:(i % 2 + 1) * 1024],
                                   ys_d.rearrange("(n p) c -> n p c", p=128)[i]
                                   ).then_inc(yl[i % 2].h, 16)
                    yl[i % 2].v += 16
                    if i >= 1:
                        sync.wait_ge(ve.h, g_t1[i - 1])
                    sync.dma_start(
                        xnb4[:, 0:2048],
                        xns_d.rearrange("(n p) d -> n p d", p=128)[i][:, 0:2048]
                    ).then_inc(ol[0].h, 16)
                    ol[0].v += 16
                    if i >= 1:
                        sync.wait_ge(ve.h, g_t2[i - 1])
                    sync.dma_start(
                        xnb4[:, 2048:4096],
                        xns_d.rearrange("(n p) d -> n p d", p=128)[i][:, 2048:4096]
                    ).then_inc(ol[1].h, 16)
                    ol[1].v += 16

        @blk.tensor
        def _(t):
            def emit_sv(s_i, kt, Q0, ktmax, kvh):
                qlo = max(Q0, kt * 128)
                off = qlo - Q0
                preg_kt = ((s_i % 2) * 3 + (kt % 3)) * 512
                vsl_o = (kvh % 2) * NT * 128
                if (s_i, kt) in mask_done:
                    t.wait_ge(ve.h, mask_done[(s_i, kt)])
                else:
                    t.wait_ge(ac.h, exp_done[(s_i, kt)])
                if kt == 0 and s_i > 1:
                    t.wait_ge(ve.h, evac_done[s_i - 2])
                mm = t.matmul(sbp[0:1, (s_i % 2) * 512 + off:(s_i % 2) * 512 + 512],
                              ones_bf[:], pbuf[:, preg_kt + off: preg_kt + 512],
                              start=(kt == 0), stop=(kt == ktmax - 1))
                pe.inc(mm)
                mm = t.matmul(ctxPS[:, (s_i % 2) * 512 + off:(s_i % 2) * 512 + 512],
                              vsl[:, vsl_o + kt * 128: vsl_o + (kt + 1) * 128],
                              pbuf[:, preg_kt + off: preg_kt + 512],
                              start=(kt == 0), stop=(kt == ktmax - 1))
                pe.inc(mm)

            for s_i, (hh, qc) in enumerate(steps):
                kvh = hh // 4
                Q0 = qc * 512
                ktmax = 4 * (qc + 1)
                ksl_o = (kvh % 2) * T
                t.wait_ge(ql[s_i % 2].h, q_tgt(s_i))
                if qc == 0 and hh % 4 == 0:
                    t.wait_ge(kl[kvh % 2].h, kv_tgt(kvh))
                bc_kt = 1 if qc == 0 else 3
                for kt in range(ktmax):
                    qlo = max(Q0, kt * 128)
                    off = qlo - Q0
                    w = 512 - off
                    if kt >= 3:
                        t.wait_ge(ac.h, exp_done[(s_i, kt - 3)])
                    elif s_i > 0:
                        pmax = 4 * (steps[s_i - 1][1] + 1)
                        t.wait_ge(ac.h, exp_done[(s_i - 1, pmax - 1)])
                    mm = t.matmul(scp[:, (kt % 3) * 512:(kt % 3) * 512 + w],
                                  ksl[:, ksl_o + kt * 128: ksl_o + (kt + 1) * 128],
                                  qh[:, (s_i % 2) * 512 + off:(s_i % 2 + 1) * 512],
                                  start=True, stop=True)
                    pe.inc(mm)
                    if s_i > 0 and kt == bc_kt:
                        ps = s_i - 1
                        t.wait_ge(ve.h, recip_done[ps])
                        mm = t.matmul(sbp[:, (ps % 2) * 512:(ps % 2 + 1) * 512],
                                      ones_f1[:], invs[:, (ps % 2) * 512:(ps % 2 + 1) * 512],
                                      start=True, stop=True)
                        pe.inc(mm)
                    if kt >= 2:
                        emit_sv(s_i, kt - 2, Q0, ktmax, kvh)
                emit_sv(s_i, ktmax - 2, Q0, ktmax, kvh)
                emit_sv(s_i, ktmax - 1, Q0, ktmax, kvh)
            t.wait_ge(ve.h, recip_done[NS - 1])
            mm = t.matmul(sbp[:, ((NS - 1) % 2) * 512:((NS - 1) % 2 + 1) * 512],
                          ones_f1[:], invs[:, ((NS - 1) % 2) * 512:((NS - 1) % 2 + 1) * 512],
                          start=True, stop=True)
            pe.inc(mm)

        @blk.scalar
        def _(s):
            for s_i, (hh, qc) in enumerate(steps):
                Q0 = qc * 512
                ktmax = 4 * (qc + 1)
                for kt in range(ktmax):
                    qlo = max(Q0, kt * 128)
                    off = qlo - Q0
                    w = 512 - off
                    preg_kt = ((s_i % 2) * 3 + (kt % 3)) * 512
                    s.wait_ge(pe.h, scores_done[(s_i, kt)])
                    if kt >= 3:
                        s.wait_ge(pe.h, svkt_done[(s_i, kt - 3)])
                    elif s_i >= 2:
                        s.wait_ge(pe.h, pv_done[s_i - 2])
                    ac.inc(s.activation(
                        pbuf[:, preg_kt + off: preg_kt + off + w],
                        scp[:, (kt % 3) * 512:(kt % 3) * 512 + w],
                        AF.Exp, bias=shiftc[:], scale=INV_SQRT_DH))

        @blk.vector
        def _(v):
            for s_i, (hh, qc) in enumerate(steps):
                Q0 = qc * 512
                ktmax = 4 * (qc + 1)
                for kt in range(4 * qc, ktmax):
                    off = kt * 128 - Q0
                    preg_kt = ((s_i % 2) * 3 + (kt % 3)) * 512
                    v.wait_ge(ac.h, exp_done[(s_i, kt)])
                    ve.inc(v.tensor_tensor(
                        pbuf[:, preg_kt + off: preg_kt + off + 128],
                        pbuf[:, preg_kt + off: preg_kt + off + 128],
                        tri[:], OP.mult))
                v.wait_ge(pe.h, pv_done[s_i])
                ve.inc(v.reciprocal(invs[:, (s_i % 2) * 512:(s_i % 2 + 1) * 512],
                                    sbp[0:1, (s_i % 2) * 512:(s_i % 2 + 1) * 512]))
                v.wait_ge(pe.h, bcast_done[s_i])
                ve.inc(v.tensor_copy(invbc[:, (s_i % 2) * 512:(s_i % 2 + 1) * 512],
                                     sbp[:, (s_i % 2) * 512:(s_i % 2 + 1) * 512]))
                v.wait_ge(ve.h, ve.v)
                ve.inc(v.tensor_tensor(
                    ctx[:, hh * T + qc * 512: hh * T + (qc + 1) * 512],
                    ctxPS[:, (s_i % 2) * 512:(s_i % 2 + 1) * 512],
                    invbc[:, (s_i % 2) * 512:(s_i % 2 + 1) * 512], OP.mult))
                gsl = g_slots(s_i)
                if gsl is not None:
                    i, piece = gsl
                    if piece == 0:
                        if i >= 1:
                            v.wait_ge(gsp[(i - 1) % 2].h, gspill_tgt(i - 1))
                        v.wait_ge(yl[i % 2].h, yt_tgt(i))
                        for istr in range(4):
                            tsp = v.tensor_scalar(
                                gpre[:, istr * 1024:(istr + 1) * 1024],
                                yt4[:, (i % 2) * 1024:(i % 2 + 1) * 1024],
                                hpost[:, i * 4 + istr:i * 4 + istr + 1], None, OP.mult)
                        ve.inc(tsp)
                    else:
                        j = piece - 1
                        if j == 0:
                            v.wait_ge(ol[0].h, xa_tgt(i))
                        elif j == 2:
                            v.wait_ge(ol[1].h, xb_tgt(i))
                        for istr in range(4):
                            stt = v.scalar_tensor_tensor(
                                gpre[:, istr * 1024:(istr + 1) * 1024],
                                xnb4[:, j * 1024:(j + 1) * 1024],
                                hres[:, i * 16 + istr * 4 + j:
                                     i * 16 + istr * 4 + j + 1],
                                gpre[:, istr * 1024:(istr + 1) * 1024],
                                OP.mult, OP.add)
                        ve.inc(stt)

        @blk.gpsimd
        def _(p):
            for i in range(NT):
                p.wait_ge(ve.h, g_t2[i])
                p.dma_start(gs_d2.rearrange("(n p) d -> n p d", p=128)[i],
                            gpre[:]).then_inc(gsp[i % 2].h, 16)
                gsp[i % 2].v += 16
            p.wait_ge(gsp[0].h, gsp[0].v)
            p.wait_ge(gsp[1].h, gsp[1].v)

        assert pe.v == pe_c, (pe.v, pe_c)
        assert ac.v == ac_c
        assert ve.v == ve_c


def _p5(nc, env):
    g = env.__getitem__
    wo_d, ctx, x_d, out_d, gs_d2 = g("wo_d"), g("ctx"), g("x_d"), g("out_d"), g("gs_d2")
    gs, pe, ve, ac, po = g("gs"), g("pe"), g("ve"), g("ac"), g("po")
    wl = (g("w0"), g("w1"))
    ll = (g("l0"), g("l1"))
    ol_ = (g("o0"), g("o1"))

    NOC = 16
    OC = 256

    with nc.sbuf_tensor("woblk", [128, 2 * NC_ * OC], BF16) as woblk, \
         nc.sbuf_tensor("xcols", [128, 2 * OC], F32) as xcols, \
         nc.sbuf_tensor("gcols", [128, 2 * OC], F32) as gcols, \
         nc.sbuf_tensor("gbuf", [128, 2 * OC], F32) as gbuf, \
         nc.sbuf_tensor("tbuf", [128, 2 * OC], F32) as tbuf, \
         nc.sbuf_tensor("obuf", [128, 2 * OC], F32) as obuf, \
         nc.psum_tensor("sPS", [128, 2 * 512], F32) as sPS, \
         nc.Block() as blk:

        pe0, ve0, ac0, po0 = pe.v, ve.v, ac.v, po.v
        wl0 = (wl[0].v, wl[1].v)
        ll0 = (ll[0].v, ll[1].v)
        ol0 = (ol_[0].v, ol_[1].v)
        NK = NOC * NT

        def ld_tgt(k):
            return ll0[k % 2] + (k // 2 + 1) * 32

        def out_tgt(k):
            return ol0[k % 2] + (k // 2 + 1) * 16

        def _emit_wold(p, oc):
            if oc >= 2:
                p.wait_ge(pe.h, pe0 + (oc - 1) * NT)
            p.dma_start(
                woblk[:, (oc % 2) * NC_ * OC:(oc % 2 + 1) * NC_ * OC]
                    .rearrange("p (c o) -> p c o", o=OC),
                wo_d.rearrange("(c p) o -> p c o", p=128)[:, :, oc * OC:(oc + 1) * OC]
            ).then_inc(wl[oc % 2].h, 16)
            wl[oc % 2].v += 16

        @blk.sync
        def _(sync):
            for oc in range(NOC):
                for i in range(NT):
                    k = oc * NT + i
                    if k >= 2:
                        sync.wait_ge(po.h, po0 + k - 1)
                        sync.wait_ge(ac.h, ac0 + k - 1)
                    sync.dma_start(
                        xcols[:, (k % 2) * OC:(k % 2 + 1) * OC],
                        x_d.rearrange("(n p) d -> n p d", p=128)[i][:, oc * OC:(oc + 1) * OC]
                    ).then_inc(ll[k % 2].h, 16)
                    sync.dma_start(
                        gcols[:, (k % 2) * OC:(k % 2 + 1) * OC],
                        gs_d2.rearrange("(n p) d -> n p d", p=128)[i][:, oc * OC:(oc + 1) * OC]
                    ).then_inc(ll[k % 2].h, 16)
                    ll[k % 2].v += 32
                    if k >= 2:
                        kk = k - 2
                        oc2, i2 = kk // NT, kk % NT
                        sync.wait_ge(po.h, po0 + kk + 1)
                        sync.dma_start(
                            out_d.rearrange("(n p) d -> n p d", p=128)[i2]
                            [:, oc2 * OC:(oc2 + 1) * OC],
                            obuf[:, (kk % 2) * OC:(kk % 2 + 1) * OC]
                        ).then_inc(ol_[kk % 2].h, 16)
                        ol_[kk % 2].v += 16
            for kk in (NK - 2, NK - 1):
                oc2, i2 = kk // NT, kk % NT
                sync.wait_ge(po.h, po0 + kk + 1)
                sync.dma_start(
                    out_d.rearrange("(n p) d -> n p d", p=128)[i2]
                    [:, oc2 * OC:(oc2 + 1) * OC],
                    obuf[:, (kk % 2) * OC:(kk % 2 + 1) * OC]
                ).then_inc(ol_[kk % 2].h, 16)
                ol_[kk % 2].v += 16

        @blk.tensor
        def _(t):
            for oc in range(NOC):
                t.wait_ge(wl[oc % 2].h, wl0[oc % 2] + (oc // 2 + 1) * 16)
                for i in range(NT):
                    k = oc * NT + i
                    if k >= 2:
                        t.wait_ge(ve.h, ve0 + k - 1)
                    mm = None
                    for c in range(NC_):
                        mm = t.matmul(sPS[:, (k % 2) * 512:(k % 2) * 512 + OC],
                                      ctx[:, c * T + i * 128: c * T + (i + 1) * 128],
                                      woblk[:, (oc % 2) * NC_ * OC + c * OC:
                                            (oc % 2) * NC_ * OC + (c + 1) * OC],
                                      start=(c == 0), stop=(c == NC_ - 1))
                    pe.inc(mm)

        @blk.scalar
        def _(s):
            for oc in range(NOC):
                for i in range(NT):
                    k = oc * NT + i
                    s.wait_ge(ll[k % 2].h, ld_tgt(k))
                    if k >= 2:
                        s.wait_ge(ve.h, ve0 + k - 1)
                    ac.inc(s.activation(gbuf[:, (k % 2) * OC:(k % 2 + 1) * OC],
                                        gcols[:, (k % 2) * OC:(k % 2 + 1) * OC],
                                        AF.Silu))

        @blk.vector
        def _(v):
            for oc in range(NOC):
                for i in range(NT):
                    k = oc * NT + i
                    v.wait_ge(pe.h, pe0 + k + 1)
                    v.wait_ge(ac.h, ac0 + k + 1)
                    if k >= 2:
                        v.wait_ge(po.h, po0 + k - 1)
                    ve.inc(v.tensor_tensor(tbuf[:, (k % 2) * OC:(k % 2 + 1) * OC],
                                           sPS[:, (k % 2) * 512:(k % 2) * 512 + OC],
                                           gbuf[:, (k % 2) * OC:(k % 2 + 1) * OC],
                                           OP.mult))

        @blk.gpsimd
        def _(p):
            _emit_wold(p, 0)
            _emit_wold(p, 1)
            for oc in range(NOC):
                for i in range(NT):
                    k = oc * NT + i
                    p.wait_ge(ve.h, ve0 + k + 1)
                    p.wait_ge(ll[k % 2].h, ld_tgt(k))
                    if k >= 2:
                        p.wait_ge(ol_[k % 2].h, out_tgt(k - 2))
                    po.inc(p.tensor_tensor(obuf[:, (k % 2) * OC:(k % 2 + 1) * OC],
                                           tbuf[:, (k % 2) * OC:(k % 2 + 1) * OC],
                                           xcols[:, (k % 2) * OC:(k % 2 + 1) * OC],
                                           OP.add))
                if oc + 2 < NOC:
                    _emit_wold(p, oc + 2)

        assert pe.v == pe0 + NK
        assert ve.v == ve0 + NK
        assert ac.v == ac0 + NK
        assert po.v == po0 + NK


def kernel(L0_attn_resid, L0_attn_Wq, L0_attn_Wk, L0_attn_Wv, L0_attn_Wo,
           L0_ffn_W_res, L0_ffn_W_pre, L0_ffn_W_post, L0_ffn_H_res_base,
           L0_ffn_H_pre_base, L0_ffn_H_post_base, L0_ffn_gate, L0_ffn_W_branch):
    x = np.ascontiguousarray(np.asarray(L0_attn_resid, dtype=np.float32))
    cos_fm, sin_fm = _rope_tables()
    wrppT = np.ascontiguousarray(
        np.concatenate([np.asarray(L0_ffn_W_res), np.asarray(L0_ffn_W_pre),
                        np.asarray(L0_ffn_W_post)], axis=0).T.astype(np.float32))
    base24 = np.concatenate([
        np.asarray(L0_ffn_H_res_base, dtype=np.float32).reshape(16),
        np.asarray(L0_ffn_H_pre_base, dtype=np.float32).reshape(4),
        np.asarray(L0_ffn_H_post_base, dtype=np.float32).reshape(4)]).reshape(1, 24)
    delta16 = np.eye(4, dtype=np.float32).reshape(1, 16)
    gate = np.asarray(L0_ffn_gate, dtype=np.float32).reshape(1, 1)

    common = {
        "wq": np.ascontiguousarray(L0_attn_Wq, dtype=np.float32),
        "wk": np.ascontiguousarray(L0_attn_Wk, dtype=np.float32),
        "wv": np.ascontiguousarray(L0_attn_Wv, dtype=np.float32),
        "wo": np.ascontiguousarray(L0_attn_Wo, dtype=np.float32),
        "wb": np.ascontiguousarray(L0_ffn_W_branch, dtype=np.float32),
        "wrppT": wrppT, "base24": base24, "delta16": delta16, "gate": gate,
        "cos_fm": cos_fm, "sin_fm": sin_fm,
        "ident": np.eye(128, dtype=ml_dtypes.bfloat16),
        "rmat": _rot_lhsT(), "tri": _tri_mask(),
    }
    in_maps = [dict(common, x=np.ascontiguousarray(x[b])) for b in range(B)]

    nc = build_nc()
    res = run_bass_kernel_spmd(nc, in_maps, core_ids=list(range(B)))
    return np.stack([np.asarray(r["out"], dtype=np.float32) for r in res.results], axis=0)


# version 14
# speedup vs baseline: 1.1090x; 1.0173x over previous
import math
import numpy as np
import ml_dtypes
from contextlib import ExitStack

import concourse.bass as bass
import concourse.mybir as mybir
from concourse.bass_utils import run_bass_kernel_spmd

F32 = mybir.dt.float32
BF16 = mybir.dt.bfloat16
OP = mybir.AluOpType
AF = mybir.ActivationFunctionType
AX = mybir.AxisListType

B, T, D = 8, 2048, 4096
NH, NKV, DH = 32, 8, 128
NT = T // 128
NC_ = D // 128
INV_SQRT_DH = 1.0 / math.sqrt(DH)
EXP_SHIFT = -12.0
LN_EPS = 1e-5
RMS_EPS = 1.1920929e-07
TAU_INV = 10.0

_DEV = False


def _rope_tables():
    idx = np.arange(0, DH, 2, dtype=np.float64) / DH
    freq = 1.0 / (2.0 * math.pi) * (500000.0 ** (-idx))
    factor, lo, hi, L0 = 32.0, 1.0, 4.0, 8192.0
    freq_low, freq_high = lo / L0, hi / L0
    freq_scaled = np.where(freq < freq_low, freq / factor, freq)
    smooth = np.clip((L0 * freq - lo) / (hi - lo), 0.0, 1.0)
    freq_smooth = (1.0 - smooth) * (freq / factor) + smooth * freq
    is_mid = (freq >= freq_low) & (freq <= freq_high)
    freq = np.where(is_mid, freq_smooth, freq_scaled)
    phase = 2.0 * math.pi * np.arange(T, dtype=np.float64)[:, None] * freq[None, :]
    emb = np.concatenate([phase, phase], axis=-1)
    cos = np.ascontiguousarray(np.cos(emb).astype(np.float32).T)
    sin = np.ascontiguousarray(np.sin(emb).astype(np.float32).T)
    return cos, sin


def _rot_lhsT():
    r = np.zeros((DH, DH), dtype=np.float32)
    for m in range(64):
        r[m + 64, m] = -1.0
    for m in range(64, 128):
        r[m - 64, m] = 1.0
    return r.astype(ml_dtypes.bfloat16)


def _tri_mask():
    m = (np.arange(128)[:, None] <= np.arange(128)[None, :]).astype(np.float32)
    return m.astype(ml_dtypes.bfloat16)


class Ctr:

    def __init__(self, stack, nc, name):
        self.h = stack.enter_context(nc.semaphore(name))
        self.v = 0

    def inc(self, instr, n=1):
        instr.then_inc(self.h, n)
        self.v += n
        return self.v


def _cells(off):
    if off < 512:
        return [(off, 512), (512, 1024)]
    return [(off, 1024)]


def build_nc():
    nc = bass.Bass("TRN2", target_bir_lowering=False, debug=False)

    x_d = nc.dram_tensor("x", [T, D], F32, kind="ExternalInput").ap()
    wq_d = nc.dram_tensor("wq", [D, D], F32, kind="ExternalInput").ap()
    wk_d = nc.dram_tensor("wk", [D, NKV * DH], F32, kind="ExternalInput").ap()
    wv_d = nc.dram_tensor("wv", [D, NKV * DH], F32, kind="ExternalInput").ap()
    wo_d = nc.dram_tensor("wo", [D, D], F32, kind="ExternalInput").ap()
    wb_d = nc.dram_tensor("wb", [1024, 1024], F32, kind="ExternalInput").ap()
    wrppT_d = nc.dram_tensor("wrppT", [D, 24], F32, kind="ExternalInput").ap()
    base24_d = nc.dram_tensor("base24", [1, 24], F32, kind="ExternalInput").ap()
    delta16_d = nc.dram_tensor("delta16", [1, 16], F32, kind="ExternalInput").ap()
    gate_d = nc.dram_tensor("gate", [1, 1], F32, kind="ExternalInput").ap()
    cos_d = nc.dram_tensor("cos_fm", [DH, T], F32, kind="ExternalInput").ap()
    sin_d = nc.dram_tensor("sin_fm", [DH, T], F32, kind="ExternalInput").ap()
    ident_d = nc.dram_tensor("ident", [128, 128], BF16, kind="ExternalInput").ap()
    rmat_d = nc.dram_tensor("rmat", [128, 128], BF16, kind="ExternalInput").ap()
    tri_d = nc.dram_tensor("tri", [128, 128], BF16, kind="ExternalInput").ap()

    out_d = nc.dram_tensor("out", [T, D], F32, kind="ExternalOutput").ap()

    scratch_kind = "ExternalOutput" if _DEV else "Internal"
    qs_d = nc.dram_tensor("qs", [NH * DH, T], BF16, kind=scratch_kind).ap()
    ks_d = nc.dram_tensor("ks", [NKV * DH, T], BF16, kind=scratch_kind).ap()
    vs_d = nc.dram_tensor("vs", [T, NKV * DH], BF16, kind=scratch_kind).ap()
    ss_d = nc.dram_tensor("ss", [T, D], F32, kind=scratch_kind).ap()
    xns_d = nc.dram_tensor("xns", [T, D], BF16, kind=scratch_kind).ap()
    gs_d2 = nc.dram_tensor("gsp", [T, D], F32, kind=scratch_kind).ap()
    ys_d = nc.dram_tensor("ys", [T, 1024], BF16, kind=scratch_kind).ap()

    with ExitStack() as st:
        e = st.enter_context

        env = {}
        env.update(x_d=x_d, wq_d=wq_d, wk_d=wk_d, wv_d=wv_d, wo_d=wo_d, wb_d=wb_d,
                  wrppT_d=wrppT_d, base24_d=base24_d, delta16_d=delta16_d,
                  gate_d=gate_d, cos_d=cos_d, sin_d=sin_d, ident_d=ident_d,
                  rmat_d=rmat_d, tri_d=tri_d, out_d=out_d, qs_d=qs_d, ks_d=ks_d,
                  vs_d=vs_d, ss_d=ss_d, xns_d=xns_d, gs_d2=gs_d2, ys_d=ys_d)

        env["ident"] = e(nc.sbuf_tensor("identb", [128, 128], BF16))
        env["rmat"] = e(nc.sbuf_tensor("rmatb", [128, 128], BF16))
        env["tri"] = e(nc.sbuf_tensor("trib", [128, 128], BF16))
        env["wrppT"] = e(nc.sbuf_tensor("wrppTb", [128, NC_ * 24], BF16))
        env["base_bc"] = e(nc.sbuf_tensor("base_bc", [128, 24], F32))
        env["i1mg"] = e(nc.sbuf_tensor("i1mg", [128, 16], F32))
        env["gcol"] = e(nc.sbuf_tensor("gcol", [128, 1], F32))
        env["ones_bf"] = e(nc.sbuf_tensor("ones_bf", [128, 1], BF16))
        env["ones_f1"] = e(nc.sbuf_tensor("ones_f1", [1, 128], F32))
        env["shiftc"] = e(nc.sbuf_tensor("shiftc", [128, 1], F32))
        env["m_st"] = e(nc.sbuf_tensor("m_st", [128, NT], F32))
        env["rstd_st"] = e(nc.sbuf_tensor("rstd_st", [128, NT], F32))
        env["rstd2_st"] = e(nc.sbuf_tensor("rstd2_st", [128, NT], F32))
        env["tmp_st"] = e(nc.sbuf_tensor("tmp_st", [128, 8 * NT], F32))
        env["lbuf"] = e(nc.sbuf_tensor("lbuf", [128, NT * 24], F32))
        env["hres"] = e(nc.sbuf_tensor("hres", [128, NT * 16], F32))
        env["hpre"] = e(nc.sbuf_tensor("hpre", [128, NT * 4], F32))
        env["hpost"] = e(nc.sbuf_tensor("hpost", [128, NT * 4], F32))

        for nm in ("ds", "gs", "sp", "pe", "ve", "ac", "po", "os_", "sf",
                   "x0", "x1", "w0", "w1", "k0", "k1", "q0", "q1",
                   "l0", "l1", "o0", "o1", "sg", "s0", "s1", "g0", "g1", "sv"):
            env[nm] = Ctr(st, nc, nm)

        with nc.sbuf_tensor("xn_fm", [128, NC_ * T], BF16) as xn_fm, \
             nc.sbuf_tensor("cosb", [128, T], F32) as cosb, \
             nc.sbuf_tensor("sinb", [128, T], F32) as sinb, \
             nc.sbuf_tensor("psink", [128, NT * 16], F32) as psink, \
             nc.sbuf_tensor("mxs", [128, NT], F32) as mxs, \
             nc.sbuf_tensor("avs", [128, NT * 4], F32) as avs, \
             nc.sbuf_tensor("bvs", [128, NT * 4], F32) as bvs, \
             nc.sbuf_tensor("tts", [128, NT * 16], F32) as tts:
            env["xn_fm"] = xn_fm
            env["cosb"] = cosb
            env["sinb"] = sinb
            env.update(psink=psink, mxs=mxs, avs=avs, bvs=bvs, tts=tts)
            _p0(nc, env)
            _p1(nc, env)
            _p1b_setup(nc, env)
            _p1b_iters(nc, env)
            _p2(nc, env)
            _p3(nc, env)
            _p1b_final(nc, env)
        _p35(nc, env)
        with nc.sbuf_tensor("ctx", [128, NH * T], BF16) as ctx:
            env["ctx"] = ctx
            _p4(nc, env)
            _p5(nc, env)

        with nc.Block() as blk:
            @blk.sync
            def _(sync):
                for nm in ("o0", "o1", "sp", "ds", "gs", "x0", "x1", "w0", "w1",
                           "k0", "k1", "q0", "q1", "l0", "l1", "s0", "s1", "g0", "g1"):
                    c = env[nm]
                    sync.wait_ge(c.h, c.v)

    return nc


def _p0(nc, env):
    g = env.__getitem__
    ds, gs, pe, ve, ac = g("ds"), g("gs"), g("pe"), g("ve"), g("ac")

    with nc.psum_tensor("p0ps", [128, 512], F32) as p0ps, \
         nc.sbuf_tensor("g11", [1, 2], F32) as g11, \
         nc.sbuf_tensor("gate_sb", [1, 1], F32) as gate_sb, \
         nc.sbuf_tensor("b24row", [1, 24], F32) as b24row, \
         nc.sbuf_tensor("d16row", [1, 16], F32) as d16row, \
         nc.sbuf_tensor("omg", [128, 1], F32) as omg, \
         nc.Block() as blk:

        @blk.sync
        def _(sync):
            for dst, src in ((g("cosb")[:], g("cos_d")), (g("sinb")[:], g("sin_d")),
                             (g("ident")[:], g("ident_d")), (g("rmat")[:], g("rmat_d")),
                             (g("tri")[:], g("tri_d")), (gate_sb[:], g("gate_d")),
                             (b24row[:], g("base24_d")), (d16row[:], g("delta16_d"))):
                sync.dma_start(dst, src).then_inc(ds.h, 16)
            ds.v += 8 * 16
            sync.wait_ge(ds.h, ds.v)

        @blk.gpsimd
        def _(p):
            p.dma_start(g("wrppT")[:].rearrange("p (c e) -> p c e", e=24),
                        g("wrppT_d").rearrange("(c p) e -> p c e", p=128)).then_inc(gs.h, 16)
            gs.v += 16
            p.wait_ge(gs.h, gs.v)

        @blk.vector
        def _(v):
            v.memset(g("ones_bf")[:], 1.0)
            v.memset(g("shiftc")[:], EXP_SHIFT)
            ve.inc(v.memset(g("ones_f1")[:], 1.0))

        @blk.scalar
        def _(s):
            s.wait_ge(ds.h, ds.v)
            s.activation(g11[:, 0:1], gate_sb[:], AF.Sigmoid)
            ac.inc(s.activation(g11[:, 1:2], gate_sb[:], AF.Sigmoid, scale=-1.0))

        @blk.tensor
        def _(t):
            t.wait_ge(ac.h, ac.v)
            t.wait_ge(ve.h, ve.v)
            t.matmul(p0ps[:, 0:24], g("ones_f1")[:], b24row[:], start=True, stop=True)
            t.matmul(p0ps[:, 24:40], g("ones_f1")[:], d16row[:], start=True, stop=True)
            t.matmul(p0ps[:, 40:41], g("ones_f1")[:], g11[:, 0:1], start=True, stop=True)
            pe.inc(t.matmul(p0ps[:, 41:42], g("ones_f1")[:], g11[:, 1:2],
                            start=True, stop=True))

        @blk.vector
        def _(v):
            sf = g("sf")
            v.wait_ge(pe.h, pe.v)
            v.tensor_copy(g("base_bc")[:], p0ps[:, 0:24])
            v.tensor_copy(g("gcol")[:], p0ps[:, 40:41])
            sf.inc(v.tensor_copy(omg[:], p0ps[:, 41:42]))
            v.wait_ge(sf.h, sf.v)
            ve.inc(v.tensor_scalar(g("i1mg")[:], p0ps[:, 24:40], omg[:], None, OP.mult))


def _p1(nc, env):
    g = env.__getitem__
    x_d, xn_fm, wrppT = g("x_d"), g("xn_fm"), g("wrppT")
    ident, base_bc = g("ident"), g("base_bc")
    m_st, rstd_st, rstd2_st, tmp_st, lbuf = (g("m_st"), g("rstd_st"), g("rstd2_st"),
                                             g("tmp_st"), g("lbuf"))
    ds, pe, ve, ac, sf = g("ds"), g("pe"), g("ve"), g("ac"), g("sf")
    xl = (g("x0"), g("x1"))
    xsp = (g("s0"), g("s1"))
    xns_d = g("xns_d")

    VE_T, AC_T, PE_T = 19, 18, 33

    with nc.sbuf_tensor("xb", [128, 2 * D], F32) as xb, \
         nc.sbuf_tensor("xnb", [128, 2 * D], BF16) as xnb, \
         nc.sbuf_tensor("sqdump", [128, 2048], BF16) as sqdump, \
         nc.psum_tensor("ptr", [128, 2 * 1024], BF16) as ptr, \
         nc.psum_tensor("plog", [128, 512], F32) as plog, \
         nc.Block() as blk:

        ds0, ve0, ac0, pe0 = ds.v, ve.v, ac.v, pe.v
        xl0 = (xl[0].v, xl[1].v)
        xsp0 = (xsp[0].v, xsp[1].v)

        def xl_tgt(i):
            return xl0[i % 2] + (i // 2 + 1) * 16

        def ve_evac(i, c):
            return ve0 + i * VE_T + 2 + (c + 1) // 2

        def ac_evac(i, c):
            return ac0 + i * AC_T + 2 + c // 2 + 1

        @blk.sync
        def _(sync):
            for i in range(NT):
                if i >= 2:
                    sync.wait_ge(ve.h, ve0 + (i - 2) * VE_T + 2)
                    sync.wait_ge(ac.h, ac0 + (i - 2) * AC_T + 1)
                sync.dma_start(xb[:, (i % 2) * D:(i % 2 + 1) * D],
                               x_d.rearrange("(n p) d -> n p d", p=128)[i]
                               ).then_inc(xl[i % 2].h, 16)
                xl[i % 2].v += 16

        @blk.vector
        def _(v):
            for i in range(NT):
                xt = xb[:, (i % 2) * D:(i % 2 + 1) * D]
                c0 = 8 * i
                sums = tmp_st[:, c0:c0 + 1]
                varp = tmp_st[:, c0 + 1:c0 + 2]
                varpe = tmp_st[:, c0 + 2:c0 + 3]
                t1 = tmp_st[:, c0 + 3:c0 + 4]
                sumsq = tmp_st[:, c0 + 4:c0 + 5]
                ue = tmp_st[:, c0 + 5:c0 + 6]
                def _e(instr):
                    sf.inc(instr)
                    v.wait_ge(sf.h, sf.v)
                v.wait_ge(xl[i % 2].h, xl_tgt(i))
                _e(v.reduce_sum(sums, xt, axis=AX.X))
                v.wait_ge(ac.h, ac0 + i * AC_T + 1)
                _e(v.tensor_scalar(m_st[:, i:i + 1], sums, 1.0 / D, None, OP.mult))
                _e(v.tensor_scalar(t1, m_st[:, i:i + 1], m_st[:, i:i + 1], -1.0,
                                   OP.mult, OP.mult))
                _e(v.scalar_tensor_tensor(varp, tmp_st[:, c0 + 6:c0 + 7], 1.0 / D, t1,
                                          OP.mult, OP.add))
                _e(v.scalar_tensor_tensor(varp, sumsq, 1.0 / D, varp, OP.mult, OP.add))
                _e(v.tensor_scalar(varpe, varp, LN_EPS, None, OP.add))
                _e(v.reciprocal(t1, varpe))
                _e(v.scalar_tensor_tensor(ue, varp, 1.0, t1, OP.mult, OP.mult))
                ve.inc(v.tensor_scalar(ue, ue, RMS_EPS, None, OP.add))
                v.wait_ge(ac.h, ac0 + i * AC_T + 2)
                _e(v.reciprocal(rstd_st[:, i:i + 1], t1))
                _e(v.reciprocal(rstd2_st[:, i:i + 1], ue))
                if i >= 2:
                    v.wait_ge(pe.h, pe0 + (i - 2) * PE_T + NC_)
                    v.wait_ge(xsp[i % 2].h, xsp0[i % 2] + ((i - 2) // 2 + 1) * 16)
                ve.inc(v.tensor_scalar(xnb[:, (i % 2) * D:(i % 2 + 1) * D], xt,
                                       m_st[:, i:i + 1], rstd_st[:, i:i + 1],
                                       OP.subtract, OP.mult))
                for c in range(1, NC_, 2):
                    v.wait_ge(pe.h, pe0 + i * PE_T + c + 1)
                    ve.inc(v.tensor_copy(
                        xn_fm[:, c * T + i * 128: c * T + (i + 1) * 128],
                        ptr[:, (c % 2) * 1024:(c % 2) * 1024 + 128]))
                v.wait_ge(pe.h, pe0 + (i + 1) * PE_T)
                ve.inc(v.scalar_tensor_tensor(lbuf[:, i * 24:(i + 1) * 24], plog[:, 0:24],
                                              rstd2_st[:, i:i + 1], base_bc[:],
                                              OP.mult, OP.add))

        @blk.scalar
        def _(s):
            for i in range(NT):
                xt = xb[:, (i % 2) * D:(i % 2 + 1) * D]
                c0 = 8 * i
                s.wait_ge(xl[i % 2].h, xl_tgt(i))
                s.activation(sqdump[:], xt[:, 0:2048], AF.Square,
                             accum_out=tmp_st[:, c0 + 6:c0 + 7])
                ac.inc(s.activation(sqdump[:], xt[:, 2048:4096], AF.Square,
                                    accum_out=tmp_st[:, c0 + 4:c0 + 5]))
                s.wait_ge(ve.h, ve0 + i * VE_T + 1)
                s.activation(tmp_st[:, c0 + 3:c0 + 4], tmp_st[:, c0 + 2:c0 + 3], AF.Sqrt)
                ac.inc(s.activation(tmp_st[:, c0 + 5:c0 + 6],
                                    tmp_st[:, c0 + 5:c0 + 6], AF.Sqrt))
                for c in range(0, NC_, 2):
                    s.wait_ge(pe.h, pe0 + i * PE_T + c + 1)
                    ac.inc(s.copy(
                        xn_fm[:, c * T + i * 128: c * T + (i + 1) * 128],
                        ptr[:, (c % 2) * 1024:(c % 2) * 1024 + 128]))
                s.dma_start(xns_d.rearrange("(n p) d -> n p d", p=128)[i],
                            xnb[:, (i % 2) * D:(i % 2 + 1) * D]).then_inc(xsp[i % 2].h, 16)
                xsp[i % 2].v += 16

        @blk.tensor
        def _(t):
            for i in range(NT):
                t.wait_ge(ve.h, ve0 + i * VE_T + 2)
                for c in range(NC_):
                    if c >= 2:
                        pc = c - 2
                        if pc % 2 == 1:
                            t.wait_ge(ve.h, ve_evac(i, pc))
                        else:
                            t.wait_ge(ac.h, ac_evac(i, pc))
                    elif i > 0:
                        pc = NC_ - 2 + c
                        if pc % 2 == 1:
                            t.wait_ge(ve.h, ve_evac(i - 1, pc))
                        else:
                            t.wait_ge(ac.h, ac_evac(i - 1, pc))
                    pe.inc(t.transpose(ptr[:, (c % 2) * 1024:(c % 2) * 1024 + 128],
                                       xnb[:, (i % 2) * D + c * 128:
                                           (i % 2) * D + (c + 1) * 128],
                                       ident[:]))
                t.wait_ge(ve.h, ve0 + i * VE_T + 18)
                t.wait_ge(ac.h, ac0 + i * AC_T + 18)
                if i > 0:
                    t.wait_ge(ve.h, ve0 + i * VE_T)
                mm = None
                for c in range(NC_):
                    mm = t.matmul(plog[:, 0:24],
                                  xn_fm[:, c * T + i * 128: c * T + (i + 1) * 128],
                                  wrppT[:, c * 24:(c + 1) * 24],
                                  start=(c == 0), stop=(c == NC_ - 1))
                pe.inc(mm)

        assert ve.v == ve0 + NT * VE_T, (ve.v, ve0)
        assert ac.v == ac0 + NT * AC_T
        assert pe.v == pe0 + NT * PE_T


def _sink_views(env):
    G = NT
    psink, tts = env["psink"], env["tts"]
    avs, bvs = env["avs"], env["bvs"]
    return dict(
        pv4=psink[:].rearrange("p (g i j) -> p g i j", i=4, j=4),
        ttv=tts[:].rearrange("p (g i j) -> p g i j", i=4, j=4),
        ttv_t=tts[:].rearrange("p (g i j) -> p g j i", i=4, j=4),
        bRep=bvs[:].rearrange("p (g j) -> p g j", j=4).unsqueeze(2)
                   .broadcast_to([128, G, 4, 4]),
        aRep=avs[:].rearrange("p (g i) -> p g i", i=4).unsqueeze(3)
                   .broadcast_to([128, G, 4, 4]),
        av_v=avs[:].rearrange("p (g i) -> p g i", i=4),
        bv_v=bvs[:].rearrange("p (g j) -> p g j", j=4),
    )


def _p1b_setup(nc, env):
    g = env.__getitem__
    lbuf = g("lbuf")
    ve, ac, sf, po, sv = g("ve"), g("ac"), g("sf"), g("po"), g("sv")
    psink, mxs, bvs = g("psink"), g("mxs"), g("bvs")
    V = _sink_views(env)
    G = NT

    with nc.sbuf_tensor("z10", [128, NT * 16], F32) as z10, \
         nc.Block() as blk:
        z10v = z10[:].rearrange("p (g e) -> p g e", e=16)
        res_view = lbuf[:].rearrange("p (g e) -> p g e", e=24)[:, :, 0:16]

        @blk.vector
        def _(v):
            sf.inc(v.tensor_scalar(z10v, res_view, TAU_INV, None, OP.mult))
            v.wait_ge(sf.h, sf.v)
            sf.inc(v.reduce_max(mxs[:], z10v, axis=AX.X))
            v.wait_ge(sf.h, sf.v)
            ve.inc(v.tensor_tensor(
                z10v, z10v,
                mxs[:].rearrange("p (g o) -> p g o", o=1).broadcast_to([128, G, 16]),
                OP.subtract))

        @blk.scalar
        def _(s):
            s.wait_ge(ve.h, ve.v)
            ac.inc(s.activation(psink[:], z10[:], AF.Exp))

        @blk.gpsimd
        def _(p):
            sg = g("sg")
            sg.inc(p.memset(bvs[:], 1.0))
            p.wait_ge(sg.h, sg.v)
            p.wait_ge(ac.h, ac.v)
            po.inc(p.engine_nop())

    env["sk_po0"] = po.v
    env["sk_sv0"] = sv.v
    env["sk_dve_it"] = 0
    env["sk_pool_it"] = 0


def _sink_pool_emit(p, env, n_iters):
    g = env.__getitem__
    po, sv = g("po"), g("sv")
    V = _sink_views(env)
    po0, sv0 = env["sk_po0"], env["sk_sv0"]
    for _ in range(n_iters):
        it = env["sk_pool_it"]
        if it >= 50:
            return
        env["sk_pool_it"] = it + 1
        if it > 0:
            p.wait_ge(sv.h, sv0 + 2 * it)
        po.inc(p.tensor_tensor(V["ttv"], V["pv4"], V["bRep"], OP.mult))
        p.wait_ge(sv.h, sv0 + 2 * it + 1)
        po.inc(p.tensor_tensor(V["ttv"], V["pv4"], V["aRep"], OP.mult))


def _sink_dve_emit(v, env, n_iters):
    g = env.__getitem__
    po, sv, sf = g("po"), g("sv"), g("sf")
    avs, bvs = g("avs"), g("bvs")
    V = _sink_views(env)
    po0, sv0 = env["sk_po0"], env["sk_sv0"]
    for _ in range(n_iters):
        it = env["sk_dve_it"]
        if it >= 50:
            return
        env["sk_dve_it"] = it + 1
        v.wait_ge(po.h, po0 + 2 * it + 1)
        sf.inc(v.reduce_sum(V["av_v"], V["ttv"], axis=AX.X))
        v.wait_ge(sf.h, sf.v)
        sv.inc(v.reciprocal(avs[:], avs[:]))
        v.wait_ge(po.h, po0 + 2 * it + 2)
        sf.inc(v.reduce_sum(V["bv_v"], V["ttv_t"], axis=AX.X))
        v.wait_ge(sf.h, sf.v)
        sv.inc(v.reciprocal(bvs[:], bvs[:]))


def _p1b_iters(nc, env):
    with nc.Block() as blk:
        @blk.gpsimd
        def _(p):
            _sink_pool_emit(p, env, 50)

        @blk.vector
        def _(v):
            _sink_dve_emit(v, env, 50)


def _p1b_final(nc, env):
    g = env.__getitem__
    lbuf, hres, hpre, hpost = g("lbuf"), g("hres"), g("hpre"), g("hpost")
    i1mg, gcol = g("i1mg"), g("gcol")
    ve, ac, po, sv, sf = g("ve"), g("ac"), g("po"), g("sv"), g("sf")
    tts = g("tts")
    V = _sink_views(env)
    G = NT
    assert env["sk_dve_it"] == 50 and env["sk_pool_it"] == 50, (
        env["sk_dve_it"], env["sk_pool_it"])

    with nc.sbuf_tensor("sm", [128, G * 2], F32) as sm, \
         nc.sbuf_tensor("pexp", [128, G * 8], F32) as pexp, \
         nc.Block() as blk:

        ppv = lbuf[:].rearrange("p (g e) -> p g e", e=24)[:, :, 16:24] \
                     .rearrange("p g (h f) -> p g h f", h=2, f=4)
        pexpv = pexp[:].rearrange("p (g h f) -> p g h f", h=2, f=4)
        smv = sm[:].rearrange("p (g h) -> p g h", h=2)

        @blk.vector
        def _(v):
            v.wait_ge(po.h, po.v)
            v.wait_ge(sv.h, sv.v)
            sf.inc(v.tensor_tensor(V["ttv"], V["ttv"], V["bRep"], OP.mult))
            v.wait_ge(sf.h, sf.v)
            ve.inc(v.scalar_tensor_tensor(
                hres[:].rearrange("p (g q) -> p g q", q=16),
                tts[:].rearrange("p (g q) -> p g q", q=16), gcol[:],
                i1mg[:].unsqueeze(1).broadcast_to([128, G, 16]),
                OP.mult, OP.add))
            sf.inc(v.reduce_max(smv, ppv, axis=AX.X))
            v.wait_ge(sf.h, sf.v)
            ve.inc(v.tensor_tensor(pexpv, ppv,
                                   smv.unsqueeze(3).broadcast_to([128, G, 2, 4]),
                                   OP.subtract))

        @blk.scalar
        def _(s):
            s.wait_ge(ve.h, ve.v)
            ac.inc(s.activation(pexp[:], pexp[:], AF.Exp))

        @blk.vector
        def _(v):
            v.wait_ge(ac.h, ac.v)
            sf.inc(v.reduce_sum(smv, pexpv, axis=AX.X))
            v.wait_ge(sf.h, sf.v)
            sf.inc(v.reciprocal(sm[:], sm[:]))
            v.wait_ge(sf.h, sf.v)
            sf.inc(v.tensor_tensor(pexpv, pexpv,
                                   smv.unsqueeze(3).broadcast_to([128, G, 2, 4]),
                                   OP.mult))
            v.wait_ge(sf.h, sf.v)
            v.tensor_copy(hpre[:].rearrange("p (g f) -> p g f", f=4), pexpv[:, :, 0, :])
            ve.inc(v.tensor_copy(hpost[:].rearrange("p (g f) -> p g f", f=4),
                                 pexpv[:, :, 1, :]))


def _kv_matmuls(t, pe, psum, wslab, xn_fm):
    mm = None
    for c in range(NC_):
        for nb in range(4):
            mm = t.matmul(psum[:, nb * 512:(nb + 1) * 512],
                          wslab[:, c * 128:(c + 1) * 128],
                          xn_fm[:, c * T + nb * 512: c * T + (nb + 1) * 512],
                          start=(c == 0), stop=(c == NC_ - 1))
    return pe.inc(mm)


def _proj_rope_phase(nc, env, n_units, w_d, out_dram, tag):
    g = env.__getitem__
    xn_fm, cosb, sinb, rmat = g("xn_fm"), g("cosb"), g("sinb"), g("rmat")
    gs, sp, pe, ve, ac = g("gs"), g("sp"), g("pe"), g("ve"), g("ac")
    wl = (g("w0"), g("w1"))

    with nc.sbuf_tensor(f"wslab_{tag}", [128, 3 * NC_ * 128], BF16) as wslab, \
         nc.sbuf_tensor(f"raw_{tag}", [128, T], BF16) as raw, \
         nc.sbuf_tensor(f"c_{tag}", [128, T], F32) as cbuf, \
         nc.sbuf_tensor(f"t2_{tag}", [128, T], F32) as t2buf, \
         nc.sbuf_tensor(f"rb_{tag}", [128, T], BF16) as rb, \
         nc.psum_tensor(f"psM_{tag}", [128, T], F32) as psM, \
         nc.psum_tensor(f"psR_{tag}", [128, T], F32) as psR, \
         nc.Block() as blk:

        gs0, pe0, ve0, ac0, sp0 = gs.v, pe.v, ve.v, ac.v, sp.v
        wl0 = (wl[0].v, wl[1].v)

        @blk.gpsimd
        def _(p):
            for u in range(n_units):
                if u >= 3:
                    p.wait_ge(pe.h, pe0 + (u - 3) * 2 + 1)
                p.dma_start(
                    wslab[:, (u % 3) * NC_ * 128:(u % 3 + 1) * NC_ * 128]
                        .rearrange("p (c o) -> p c o", o=128),
                    w_d.rearrange("(c p) o -> p c o", p=128)[:, :, u * 128:(u + 1) * 128]
                ).then_inc(wl[u % 2].h, 16)
                wl[u % 2].v += 16

        @blk.tensor
        def _(t):
            for u in range(n_units):
                t.wait_ge(wl[u % 2].h, wl0[u % 2] + (u // 2 + 1) * 16)
                if u > 0:
                    t.wait_ge(ve.h, ve0 + (u - 1) * 3 + 1)
                    t.wait_ge(ac.h, ac0 + u)
                _kv_matmuls(t, pe, psM, wslab[:, (u % 3) * NC_ * 128:
                                              (u % 3 + 1) * NC_ * 128], xn_fm)
                t.wait_ge(ac.h, ac0 + u + 1)
                if u > 0:
                    t.wait_ge(ve.h, ve0 + (u - 1) * 3 + 2)
                mm = None
                for nb in range(4):
                    mm = t.matmul(psR[:, nb * 512:(nb + 1) * 512], rmat[:],
                                  raw[:, nb * 512:(nb + 1) * 512], start=True, stop=True)
                pe.inc(mm)

        @blk.scalar
        def _(s):
            for u in range(n_units):
                s.wait_ge(pe.h, pe0 + u * 2 + 1)
                ac.inc(s.copy(raw[:], psM[:]))
                if u > 0:
                    s.wait_ge(ve.h, ve0 + u * 3)
                    s.dma_start(out_dram.rearrange("(a p) t -> a p t", p=128)[u - 1],
                                rb[:]).then_inc(sp.h, 16)
            s.wait_ge(ve.h, ve0 + n_units * 3)
            s.dma_start(out_dram.rearrange("(a p) t -> a p t", p=128)[n_units - 1],
                        rb[:]).then_inc(sp.h, 16)
            sp.v += n_units * 16
            s.wait_ge(sp.h, sp.v)

        @blk.vector
        def _(v):
            for u in range(n_units):
                v.wait_ge(pe.h, pe0 + u * 2 + 1)
                if u > 0:
                    v.wait_ge(sp.h, sp0 + u * 16)
                ve.inc(v.tensor_tensor(cbuf[:], psM[:], cosb[:], OP.mult))
                v.wait_ge(pe.h, pe0 + u * 2 + 2)
                ve.inc(v.tensor_tensor(t2buf[:], psR[:], sinb[:], OP.mult))
                v.wait_ge(ve.h, ve.v)
                ve.inc(v.tensor_tensor(rb[:], cbuf[:], t2buf[:], OP.add))

        assert pe.v == pe0 + n_units * 2
        assert ac.v == ac0 + n_units
        assert ve.v == ve0 + n_units * 3


def _p2(nc, env):
    import os
    g = env.__getitem__
    if not os.environ.get("SKIP_K"):
        _proj_rope_phase(nc, env, NKV, g("wk_d"), g("ks_d"), "k")
    if os.environ.get("SKIP_V"):
        return

    xn_fm, ident = g("xn_fm"), g("ident")
    wv_d, vs_d = g("wv_d"), g("vs_d")
    gs, sp, pe, ve, ac = g("gs"), g("sp"), g("pe"), g("ve"), g("ac")
    wl = (g("w0"), g("w1"))

    with nc.sbuf_tensor("wvslab", [128, 2 * NC_ * 128], BF16) as wvslab, \
         nc.sbuf_tensor("vfm", [128, T], BF16) as vfm, \
         nc.sbuf_tensor("vstage", [128, NT * 128], BF16) as vstage, \
         nc.psum_tensor("psV", [128, T], F32) as psV, \
         nc.psum_tensor("psT", [128, 2 * 1024], BF16) as psT, \
         nc.Block() as blk:

        gs0, pe0, ve0, ac0, sp0 = gs.v, pe.v, ve.v, ac.v, sp.v
        wl0 = (wl[0].v, wl[1].v)
        PE_V, VE_V = 17, 16

        @blk.gpsimd
        def _(p):
            for vd in range(NKV):
                if vd >= 2:
                    p.wait_ge(pe.h, pe0 + (vd - 2) * PE_V + 1)
                p.dma_start(
                    wvslab[:, (vd % 2) * NC_ * 128:(vd % 2 + 1) * NC_ * 128]
                        .rearrange("p (c o) -> p c o", o=128),
                    wv_d.rearrange("(c p) o -> p c o", p=128)[:, :, vd * 128:(vd + 1) * 128]
                ).then_inc(wl[vd % 2].h, 16)
                wl[vd % 2].v += 16

        @blk.tensor
        def _(t):
            for vd in range(NKV):
                t.wait_ge(wl[vd % 2].h, wl0[vd % 2] + (vd // 2 + 1) * 16)
                if vd > 0:
                    t.wait_ge(ac.h, ac0 + vd)
                _kv_matmuls(t, pe, psV, wvslab[:, (vd % 2) * NC_ * 128:
                                               (vd % 2 + 1) * NC_ * 128], xn_fm)
                t.wait_ge(ac.h, ac0 + vd + 1)
                for kt in range(NT):
                    if kt >= 2:
                        t.wait_ge(ve.h, ve0 + vd * VE_V + kt - 1)
                    elif vd > 0:
                        t.wait_ge(ve.h, ve0 + vd * VE_V - 2 + kt + 1)
                    pe.inc(t.transpose(psT[:, (kt % 2) * 1024:(kt % 2) * 1024 + 128],
                                       vfm[:, kt * 128:(kt + 1) * 128], ident[:]))

        @blk.scalar
        def _(s):
            for vd in range(NKV):
                s.wait_ge(pe.h, pe0 + vd * PE_V + 1)
                ac.inc(s.copy(vfm[:], psV[:]))
                if vd > 0:
                    s.wait_ge(ve.h, ve0 + vd * VE_V)
                    s.dma_start(
                        vs_d.rearrange("(n p) o -> p n o", p=128)[:, :,
                                                                 (vd - 1) * 128:vd * 128],
                        vstage[:].rearrange("p (n o) -> p n o", o=128)).then_inc(sp.h, 16)
            s.wait_ge(ve.h, ve0 + NKV * VE_V)
            s.dma_start(
                vs_d.rearrange("(n p) o -> p n o", p=128)[:, :, (NKV - 1) * 128:NKV * 128],
                vstage[:].rearrange("p (n o) -> p n o", o=128)).then_inc(sp.h, 16)
            sp.v += NKV * 16
            s.wait_ge(sp.h, sp.v)

        @blk.vector
        def _(v):
            for vd in range(NKV):
                if vd > 0:
                    v.wait_ge(sp.h, sp0 + vd * 16)
                for kt in range(NT):
                    v.wait_ge(pe.h, pe0 + vd * PE_V + 1 + kt + 1)
                    ve.inc(v.tensor_copy(vstage[:, kt * 128:(kt + 1) * 128],
                                         psT[:, (kt % 2) * 1024:(kt % 2) * 1024 + 128]))

        assert pe.v == pe0 + NKV * PE_V
        assert ac.v == ac0 + NKV
        assert ve.v == ve0 + NKV * VE_V


def _p3(nc, env):
    g = env.__getitem__
    _proj_rope_phase(nc, env, NH, g("wq_d"), g("qs_d"), "q")


def _p35(nc, env):
    g = env.__getitem__
    xns_d, ys_d, wb_d = g("xns_d"), g("ys_d"), g("wb_d")
    ident, hpre = g("ident"), g("hpre")
    gs, sp, pe, ve, ac, sf = g("gs"), g("sp"), g("pe"), g("ve"), g("ac"), g("sf")
    ll = (g("l0"), g("l1"))

    with nc.sbuf_tensor("wbs", [128, 8 * 1024], BF16) as wbs, \
         nc.sbuf_tensor("xnb35", [128, 2 * D], BF16) as xnb35, \
         nc.sbuf_tensor("xp35", [128, 1024], F32) as xp, \
         nc.sbuf_tensor("xpa35", [128, 1024], F32) as xpa, \
         nc.sbuf_tensor("xpb35", [128, 2 * 1024], BF16) as xpb, \
         nc.sbuf_tensor("xpT35", [128, 2 * 8 * 128], BF16) as xpT, \
         nc.sbuf_tensor("yt35", [128, 2 * 1024], BF16) as yt, \
         nc.psum_tensor("pT35", [128, 2 * 1024], BF16) as pT35, \
         nc.psum_tensor("psY35", [128, 2 * 1024], F32) as psY, \
         nc.Block() as blk:

        gs0, pe0, ve0, ac0, sp0 = gs.v, pe.v, ve.v, ac.v, sp.v
        ll0 = (ll[0].v, ll[1].v)

        def xpb_done(i):
            return ve0 + i * 10 + 1

        def evT(i, cc):
            return ve0 + i * 10 + 2 + cc

        def yt_done(i):
            return ve0 + (i + 1) * 10

        def tr_done(i, cc):
            return pe0 + i * 9 + cc + 1

        def y_done(i):
            return pe0 + (i + 1) * 9

        @blk.gpsimd
        def _(p):
            p.dma_start(wbs[:].rearrange("p (c o) -> p c o", o=1024),
                        wb_d.rearrange("(c p) o -> p c o", p=128)).then_inc(gs.h, 16)
            gs.v += 16
            p.wait_ge(gs.h, gs.v)

        @blk.sync
        def _(sync):
            for i in range(NT):
                if i >= 2:
                    sync.wait_ge(ve.h, xpb_done(i - 2) + 4)
                sync.dma_start(xnb35[:, (i % 2) * D:(i % 2 + 1) * D],
                               xns_d.rearrange("(n p) d -> n p d", p=128)[i]
                               ).then_inc(ll[i % 2].h, 16)
                ll[i % 2].v += 16

        @blk.vector
        def _(v):
            for i in range(NT):
                xt = xnb35[:, (i % 2) * D:(i % 2 + 1) * D]

                def _e(instr):
                    sf.inc(instr)
                    v.wait_ge(sf.h, sf.v)
                v.wait_ge(ll[i % 2].h, ll0[i % 2] + (i // 2 + 1) * 16)
                _e(v.tensor_scalar(xp[:], xt[:, 0:1024], hpre[:, i * 4:i * 4 + 1],
                                   None, OP.mult))
                _e(v.scalar_tensor_tensor(xpa[:], xt[:, 1024:2048],
                                          hpre[:, i * 4 + 1:i * 4 + 2], xp[:],
                                          OP.mult, OP.add))
                _e(v.scalar_tensor_tensor(xp[:], xt[:, 2048:3072],
                                          hpre[:, i * 4 + 2:i * 4 + 3], xpa[:],
                                          OP.mult, OP.add))
                if i >= 2:
                    v.wait_ge(pe.h, tr_done(i - 2, 7))
                ve.inc(v.scalar_tensor_tensor(xpb[:, (i % 2) * 1024:(i % 2 + 1) * 1024],
                                              xt[:, 3072:4096],
                                              hpre[:, i * 4 + 3:i * 4 + 4], xp[:],
                                              OP.mult, OP.add))
                for cc in range(8):
                    v.wait_ge(pe.h, tr_done(i, cc))
                    ve.inc(v.tensor_copy(
                        xpT[:, (i % 2) * 1024 + cc * 128:(i % 2) * 1024 + (cc + 1) * 128],
                        pT35[:, (cc % 2) * 1024:(cc % 2) * 1024 + 128]))
                v.wait_ge(pe.h, y_done(i))
                if i >= 2:
                    v.wait_ge(sp.h, sp0 + (i - 1) * 16)
                ve.inc(v.tensor_copy(yt[:, (i % 2) * 1024:(i % 2 + 1) * 1024], psY[
                    :, (i % 2) * 1024:(i % 2 + 1) * 1024]))

        @blk.tensor
        def _(t):
            for i in range(NT):
                t.wait_ge(ve.h, xpb_done(i))
                if i == 0:
                    t.wait_ge(gs.h, gs.v)
                for cc in range(8):
                    if cc >= 2:
                        t.wait_ge(ve.h, evT(i, cc - 2))
                    elif i > 0:
                        t.wait_ge(ve.h, evT(i - 1, 6 + cc))
                    pe.inc(t.transpose(pT35[:, (cc % 2) * 1024:(cc % 2) * 1024 + 128],
                                       xpb[:, (i % 2) * 1024 + cc * 128:
                                           (i % 2) * 1024 + (cc + 1) * 128],
                                       ident[:]))
                t.wait_ge(ve.h, evT(i, 7))
                if i > 0:
                    t.wait_ge(ve.h, yt_done(i - 1))
                mm = None
                for cc in range(8):
                    for nb in range(2):
                        mm = t.matmul(
                            psY[:, (i % 2) * 1024 + nb * 512:(i % 2) * 1024 + (nb + 1) * 512],
                            xpT[:, (i % 2) * 1024 + cc * 128:(i % 2) * 1024 + (cc + 1) * 128],
                            wbs[:, cc * 1024 + nb * 512: cc * 1024 + (nb + 1) * 512],
                            start=(cc == 0), stop=(cc == 7))
                pe.inc(mm)

        @blk.scalar
        def _(s):
            for i in range(NT):
                s.wait_ge(ve.h, yt_done(i))
                s.dma_start(ys_d.rearrange("(n p) c -> n p c", p=128)[i],
                            yt[:, (i % 2) * 1024:(i % 2 + 1) * 1024]).then_inc(sp.h, 16)
            sp.v += NT * 16
            s.wait_ge(sp.h, sp.v)

        assert pe.v == pe0 + NT * 9
        assert ve.v == ve0 + NT * 10


def _p4(nc, env):
    g = env.__getitem__
    qs_d, ks_d, vs_d = g("qs_d"), g("ks_d"), g("vs_d")
    xns_d, ys_d, gs_d2 = g("xns_d"), g("ys_d"), g("gs_d2")
    ctx, tri, ones_bf, ones_f1 = g("ctx"), g("tri"), g("ones_bf"), g("ones_f1")
    shiftc = g("shiftc")
    hres, hpost = g("hres"), g("hpost")
    pe, ve, ac, po, sf = g("pe"), g("ve"), g("ac"), g("po"), g("sf")
    kl = (g("k0"), g("k1"))
    ql = (g("q0"), g("q1"))
    ol = (g("o0"), g("o1"))
    yl = (g("l0"), g("l1"))
    gsp = (g("g0"), g("g1"))

    steps = [(hh, qc) for hh in range(NH) for qc in range(4)]
    NS = len(steps)

    with nc.sbuf_tensor("ksl", [128, 2 * T], BF16) as ksl, \
         nc.sbuf_tensor("vsl", [128, 2 * NT * 128], BF16) as vsl, \
         nc.sbuf_tensor("qh", [128, 2 * 512], BF16) as qh, \
         nc.sbuf_tensor("pbuf", [128, 6 * 512], BF16) as pbuf, \
         nc.sbuf_tensor("invs", [1, 2 * 512], F32) as invs, \
         nc.sbuf_tensor("invbc", [128, 2 * 512], F32) as invbc, \
         nc.sbuf_tensor("xnb4", [128, 2 * 2048], BF16) as xnb4, \
         nc.sbuf_tensor("yt4", [128, 2 * 1024], BF16) as yt4, \
         nc.sbuf_tensor("gpre", [128, D], F32) as gpre, \
         nc.psum_tensor("scp", [128, 3 * 512], F32) as scp, \
         nc.psum_tensor("sbp", [128, 2 * 512], F32) as sbp, \
         nc.psum_tensor("ctxPS", [128, 2 * 512], F32) as ctxPS, \
         nc.Block() as blk:

        pe0, ve0, ac0, po0 = pe.v, ve.v, ac.v, po.v
        kl0 = (kl[0].v, kl[1].v)
        ql0 = (ql[0].v, ql[1].v)
        ol0 = (ol[0].v, ol[1].v)
        yl0 = (yl[0].v, yl[1].v)
        gsp0 = (gsp[0].v, gsp[1].v)
        gv0 = 0

        def kv_tgt(kvh):
            return kl0[kvh % 2] + (kvh // 2 + 1) * 32

        def q_tgt(s_i):
            return ql0[s_i % 2] + (s_i // 2 + 1) * 16

        def yt_tgt(i):
            return yl0[i % 2] + (i // 2 + 1) * 16

        def xa_tgt(i):
            return ol0[0] + (i + 1) * 16

        def xb_tgt(i):
            return ol0[1] + (i + 1) * 16

        def gspill_tgt(i):
            return gsp0[i % 2] + (i // 2 + 1) * 16

        pe_c, ac_c, ve_c, po_c = pe0, ac0, ve0, po0
        scores_done, exp_done, mask_done = {}, {}, {}
        svkt_done, pv_done, bcast_done = {}, {}, {}
        recip_done, evac_done = {}, {}
        g_t0, g_t1, g_t2 = {}, {}, {}

        def g_slots(s_i):
            if s_i % 8 in (0, 1, 2, 3, 4) and s_i // 8 < NT:
                return (s_i // 8, s_i % 8)
            return None

        for s_i, (hh, qc) in enumerate(steps):
            Q0 = qc * 512
            ktmax = 4 * (qc + 1)
            for kt in range(ktmax):
                ac_c += 1
                exp_done[(s_i, kt)] = ac_c
                if kt * 128 >= Q0:
                    po_c += 1
                    mask_done[(s_i, kt)] = po_c
            bc_kt = 3
            for kt in range(ktmax):
                pe_c += 1
                scores_done[(s_i, kt)] = pe_c
                if s_i > 0 and kt == bc_kt:
                    pe_c += 1
                    bcast_done[s_i - 1] = pe_c
                if kt >= 2:
                    pe_c += 2
                    svkt_done[(s_i, kt - 2)] = pe_c
            for kt in (ktmax - 2, ktmax - 1):
                pe_c += 2
                svkt_done[(s_i, kt)] = pe_c
            pv_done[s_i] = pe_c
            ve_c += 1
            recip_done[s_i] = ve_c
            ve_c += 2
            evac_done[s_i] = ve_c
            gsl = g_slots(s_i)
            if gsl is not None:
                i, piece = gsl
                ve_c += 1
                if piece == 0:
                    g_t0[i] = ve_c
                elif piece == 2:
                    g_t1[i] = ve_c
                elif piece == 4:
                    g_t2[i] = ve_c
        pe_c += 1
        bcast_done[NS - 1] = pe_c

        @blk.sync
        def _(sync):
            for s_i, (hh, qc) in enumerate(steps):
                kvh = hh // 4
                if qc == 0 and hh % 4 == 0:
                    if kvh >= 2:
                        sync.wait_ge(pe.h, pv_done[((kvh - 2) * 4 + 3) * 4 + 3])
                    sync.dma_start(ksl[:, (kvh % 2) * T:(kvh % 2 + 1) * T],
                                   ks_d.rearrange("(a p) t -> a p t", p=128)[kvh]
                                   ).then_inc(kl[kvh % 2].h, 16)
                    sync.dma_start(
                        vsl[:, (kvh % 2) * NT * 128:(kvh % 2 + 1) * NT * 128]
                            .rearrange("p (n o) -> p n o", o=128),
                        vs_d.rearrange("(n p) o -> p n o", p=128)[:, :,
                                                                 kvh * 128:(kvh + 1) * 128]
                    ).then_inc(kl[kvh % 2].h, 16)
                    kl[kvh % 2].v += 32
                if s_i >= 2:
                    sync.wait_ge(pe.h, pv_done[s_i - 2])
                sync.dma_start(qh[:, (s_i % 2) * 512:(s_i % 2 + 1) * 512],
                               qs_d.rearrange("(a p) t -> a p t", p=128)[hh]
                               [:, qc * 512:(qc + 1) * 512]).then_inc(ql[s_i % 2].h, 16)
                ql[s_i % 2].v += 16
                if s_i % 8 == 0 and s_i // 8 < NT:
                    i = s_i // 8
                    if i >= 2:
                        sync.wait_ge(ve.h, g_t0[i - 2])
                    sync.dma_start(yt4[:, (i % 2) * 1024:(i % 2 + 1) * 1024],
                                   ys_d.rearrange("(n p) c -> n p c", p=128)[i]
                                   ).then_inc(yl[i % 2].h, 16)
                    yl[i % 2].v += 16
                    if i >= 1:
                        sync.wait_ge(ve.h, g_t1[i - 1])
                    sync.dma_start(
                        xnb4[:, 0:2048],
                        xns_d.rearrange("(n p) d -> n p d", p=128)[i][:, 0:2048]
                    ).then_inc(ol[0].h, 16)
                    ol[0].v += 16
                    if i >= 1:
                        sync.wait_ge(ve.h, g_t2[i - 1])
                    sync.dma_start(
                        xnb4[:, 2048:4096],
                        xns_d.rearrange("(n p) d -> n p d", p=128)[i][:, 2048:4096]
                    ).then_inc(ol[1].h, 16)
                    ol[1].v += 16

        @blk.tensor
        def _(t):
            def emit_sv(s_i, kt, Q0, ktmax, kvh):
                qlo = max(Q0, kt * 128)
                off = qlo - Q0
                preg_kt = ((s_i % 2) * 3 + (kt % 3)) * 512
                vsl_o = (kvh % 2) * NT * 128
                if (s_i, kt) in mask_done:
                    t.wait_ge(po.h, mask_done[(s_i, kt)])
                else:
                    t.wait_ge(ac.h, exp_done[(s_i, kt)])
                if kt == 0 and s_i > 1:
                    t.wait_ge(ve.h, evac_done[s_i - 2])
                mm = t.matmul(sbp[0:1, (s_i % 2) * 512 + off:(s_i % 2) * 512 + 512],
                              ones_bf[:], pbuf[:, preg_kt + off: preg_kt + 512],
                              start=(kt == 0), stop=(kt == ktmax - 1))
                pe.inc(mm)
                mm = t.matmul(ctxPS[:, (s_i % 2) * 512 + off:(s_i % 2) * 512 + 512],
                              vsl[:, vsl_o + kt * 128: vsl_o + (kt + 1) * 128],
                              pbuf[:, preg_kt + off: preg_kt + 512],
                              start=(kt == 0), stop=(kt == ktmax - 1))
                pe.inc(mm)

            for s_i, (hh, qc) in enumerate(steps):
                kvh = hh // 4
                Q0 = qc * 512
                ktmax = 4 * (qc + 1)
                ksl_o = (kvh % 2) * T
                t.wait_ge(ql[s_i % 2].h, q_tgt(s_i))
                if qc == 0 and hh % 4 == 0:
                    t.wait_ge(kl[kvh % 2].h, kv_tgt(kvh))
                bc_kt = 3
                for kt in range(ktmax):
                    qlo = max(Q0, kt * 128)
                    off = qlo - Q0
                    w = 512 - off
                    if kt >= 3:
                        t.wait_ge(ac.h, exp_done[(s_i, kt - 3)])
                    elif s_i > 0:
                        pmax = 4 * (steps[s_i - 1][1] + 1)
                        t.wait_ge(ac.h, exp_done[(s_i - 1, pmax - 1)])
                    mm = t.matmul(scp[:, (kt % 3) * 512:(kt % 3) * 512 + w],
                                  ksl[:, ksl_o + kt * 128: ksl_o + (kt + 1) * 128],
                                  qh[:, (s_i % 2) * 512 + off:(s_i % 2 + 1) * 512],
                                  start=True, stop=True)
                    pe.inc(mm)
                    if s_i > 0 and kt == bc_kt:
                        ps = s_i - 1
                        t.wait_ge(ve.h, recip_done[ps])
                        mm = t.matmul(sbp[:, (ps % 2) * 512:(ps % 2 + 1) * 512],
                                      ones_f1[:], invs[:, (ps % 2) * 512:(ps % 2 + 1) * 512],
                                      start=True, stop=True)
                        pe.inc(mm)
                    if kt >= 2:
                        emit_sv(s_i, kt - 2, Q0, ktmax, kvh)
                emit_sv(s_i, ktmax - 2, Q0, ktmax, kvh)
                emit_sv(s_i, ktmax - 1, Q0, ktmax, kvh)
            t.wait_ge(ve.h, recip_done[NS - 1])
            mm = t.matmul(sbp[:, ((NS - 1) % 2) * 512:((NS - 1) % 2 + 1) * 512],
                          ones_f1[:], invs[:, ((NS - 1) % 2) * 512:((NS - 1) % 2 + 1) * 512],
                          start=True, stop=True)
            pe.inc(mm)

        @blk.scalar
        def _(s):
            for s_i, (hh, qc) in enumerate(steps):
                Q0 = qc * 512
                ktmax = 4 * (qc + 1)
                for kt in range(ktmax):
                    qlo = max(Q0, kt * 128)
                    off = qlo - Q0
                    w = 512 - off
                    preg_kt = ((s_i % 2) * 3 + (kt % 3)) * 512
                    s.wait_ge(pe.h, scores_done[(s_i, kt)])
                    if kt >= 3:
                        s.wait_ge(pe.h, svkt_done[(s_i, kt - 3)])
                    elif s_i >= 2:
                        s.wait_ge(pe.h, pv_done[s_i - 2])
                    ac.inc(s.activation(
                        pbuf[:, preg_kt + off: preg_kt + off + w],
                        scp[:, (kt % 3) * 512:(kt % 3) * 512 + w],
                        AF.Exp, bias=shiftc[:], scale=INV_SQRT_DH))

        @blk.vector
        def _(v):
            for s_i, (hh, qc) in enumerate(steps):
                Q0 = qc * 512
                ktmax = 4 * (qc + 1)
                v.wait_ge(pe.h, pv_done[s_i])
                ve.inc(v.reciprocal(invs[:, (s_i % 2) * 512:(s_i % 2 + 1) * 512],
                                    sbp[0:1, (s_i % 2) * 512:(s_i % 2 + 1) * 512]))
                v.wait_ge(pe.h, bcast_done[s_i])
                ve.inc(v.tensor_copy(invbc[:, (s_i % 2) * 512:(s_i % 2 + 1) * 512],
                                     sbp[:, (s_i % 2) * 512:(s_i % 2 + 1) * 512]))
                v.wait_ge(ve.h, ve.v)
                ve.inc(v.tensor_tensor(
                    ctx[:, hh * T + qc * 512: hh * T + (qc + 1) * 512],
                    ctxPS[:, (s_i % 2) * 512:(s_i % 2 + 1) * 512],
                    invbc[:, (s_i % 2) * 512:(s_i % 2 + 1) * 512], OP.mult))
                gsl = g_slots(s_i)
                if gsl is not None:
                    i, piece = gsl
                    if piece == 0:
                        if i >= 1:
                            v.wait_ge(gsp[(i - 1) % 2].h, gspill_tgt(i - 1))
                        v.wait_ge(yl[i % 2].h, yt_tgt(i))
                        for istr in range(4):
                            tsp = v.tensor_scalar(
                                gpre[:, istr * 1024:(istr + 1) * 1024],
                                yt4[:, (i % 2) * 1024:(i % 2 + 1) * 1024],
                                hpost[:, i * 4 + istr:i * 4 + istr + 1], None, OP.mult)
                        ve.inc(tsp)
                    else:
                        j = piece - 1
                        if j == 0:
                            v.wait_ge(ol[0].h, xa_tgt(i))
                        elif j == 2:
                            v.wait_ge(ol[1].h, xb_tgt(i))
                        for istr in range(4):
                            stt = v.scalar_tensor_tensor(
                                gpre[:, istr * 1024:(istr + 1) * 1024],
                                xnb4[:, j * 1024:(j + 1) * 1024],
                                hres[:, i * 16 + istr * 4 + j:
                                     i * 16 + istr * 4 + j + 1],
                                gpre[:, istr * 1024:(istr + 1) * 1024],
                                OP.mult, OP.add)
                        ve.inc(stt)

        @blk.gpsimd
        def _(p):
            for s_i, (hh, qc) in enumerate(steps):
                Q0 = qc * 512
                ktmax = 4 * (qc + 1)
                for kt in range(4 * qc, ktmax):
                    off = kt * 128 - Q0
                    preg_kt = ((s_i % 2) * 3 + (kt % 3)) * 512
                    p.wait_ge(ac.h, exp_done[(s_i, kt)])
                    po.inc(p.tensor_tensor(
                        pbuf[:, preg_kt + off: preg_kt + off + 128],
                        pbuf[:, preg_kt + off: preg_kt + off + 128],
                        tri[:], OP.mult))
                if s_i % 8 == 4 and s_i // 8 < NT:
                    i = s_i // 8
                    p.wait_ge(ve.h, g_t2[i])
                    p.dma_start(gs_d2.rearrange("(n p) d -> n p d", p=128)[i],
                                gpre[:]).then_inc(gsp[i % 2].h, 16)
                    gsp[i % 2].v += 16
            p.wait_ge(gsp[0].h, gsp[0].v)
            p.wait_ge(gsp[1].h, gsp[1].v)

        assert pe.v == pe_c, (pe.v, pe_c)
        assert ac.h is not None and ac.v == ac_c
        assert ve.v == ve_c
        assert po.v == po_c, (po.v, po_c)


def _p5(nc, env):
    g = env.__getitem__
    wo_d, ctx, x_d, out_d, gs_d2 = g("wo_d"), g("ctx"), g("x_d"), g("out_d"), g("gs_d2")
    gs, pe, ve, ac, po = g("gs"), g("pe"), g("ve"), g("ac"), g("po")
    wl = (g("w0"), g("w1"))
    ll = (g("l0"), g("l1"))
    ol_ = (g("o0"), g("o1"))

    NOC = 16
    OC = 256

    with nc.sbuf_tensor("woblk", [128, 2 * NC_ * OC], BF16) as woblk, \
         nc.sbuf_tensor("xcols", [128, 2 * OC], F32) as xcols, \
         nc.sbuf_tensor("gcols", [128, 2 * OC], F32) as gcols, \
         nc.sbuf_tensor("gbuf", [128, 2 * OC], F32) as gbuf, \
         nc.sbuf_tensor("tbuf", [128, 2 * OC], F32) as tbuf, \
         nc.sbuf_tensor("obuf", [128, 2 * OC], F32) as obuf, \
         nc.psum_tensor("sPS", [128, 2 * 512], F32) as sPS, \
         nc.Block() as blk:

        pe0, ve0, ac0, po0 = pe.v, ve.v, ac.v, po.v
        wl0 = (wl[0].v, wl[1].v)
        ll0 = (ll[0].v, ll[1].v)
        ol0 = (ol_[0].v, ol_[1].v)
        NK = NOC * NT

        def ld_tgt(k):
            return ll0[k % 2] + (k // 2 + 1) * 32

        def out_tgt(k):
            return ol0[k % 2] + (k // 2 + 1) * 16

        def _emit_wold(p, oc):
            if oc >= 2:
                p.wait_ge(pe.h, pe0 + (oc - 1) * NT)
            p.dma_start(
                woblk[:, (oc % 2) * NC_ * OC:(oc % 2 + 1) * NC_ * OC]
                    .rearrange("p (c o) -> p c o", o=OC),
                wo_d.rearrange("(c p) o -> p c o", p=128)[:, :, oc * OC:(oc + 1) * OC]
            ).then_inc(wl[oc % 2].h, 16)
            wl[oc % 2].v += 16

        @blk.sync
        def _(sync):
            for oc in range(NOC):
                for i in range(NT):
                    k = oc * NT + i
                    if k >= 2:
                        sync.wait_ge(po.h, po0 + k - 1)
                        sync.wait_ge(ac.h, ac0 + k - 1)
                    sync.dma_start(
                        xcols[:, (k % 2) * OC:(k % 2 + 1) * OC],
                        x_d.rearrange("(n p) d -> n p d", p=128)[i][:, oc * OC:(oc + 1) * OC]
                    ).then_inc(ll[k % 2].h, 16)
                    sync.dma_start(
                        gcols[:, (k % 2) * OC:(k % 2 + 1) * OC],
                        gs_d2.rearrange("(n p) d -> n p d", p=128)[i][:, oc * OC:(oc + 1) * OC]
                    ).then_inc(ll[k % 2].h, 16)
                    ll[k % 2].v += 32
                    if k >= 2:
                        kk = k - 2
                        oc2, i2 = kk // NT, kk % NT
                        sync.wait_ge(po.h, po0 + kk + 1)
                        sync.dma_start(
                            out_d.rearrange("(n p) d -> n p d", p=128)[i2]
                            [:, oc2 * OC:(oc2 + 1) * OC],
                            obuf[:, (kk % 2) * OC:(kk % 2 + 1) * OC]
                        ).then_inc(ol_[kk % 2].h, 16)
                        ol_[kk % 2].v += 16
            for kk in (NK - 2, NK - 1):
                oc2, i2 = kk // NT, kk % NT
                sync.wait_ge(po.h, po0 + kk + 1)
                sync.dma_start(
                    out_d.rearrange("(n p) d -> n p d", p=128)[i2]
                    [:, oc2 * OC:(oc2 + 1) * OC],
                    obuf[:, (kk % 2) * OC:(kk % 2 + 1) * OC]
                ).then_inc(ol_[kk % 2].h, 16)
                ol_[kk % 2].v += 16

        @blk.tensor
        def _(t):
            for oc in range(NOC):
                t.wait_ge(wl[oc % 2].h, wl0[oc % 2] + (oc // 2 + 1) * 16)
                for i in range(NT):
                    k = oc * NT + i
                    if k >= 2:
                        t.wait_ge(ve.h, ve0 + k - 1)
                    mm = None
                    for c in range(NC_):
                        mm = t.matmul(sPS[:, (k % 2) * 512:(k % 2) * 512 + OC],
                                      ctx[:, c * T + i * 128: c * T + (i + 1) * 128],
                                      woblk[:, (oc % 2) * NC_ * OC + c * OC:
                                            (oc % 2) * NC_ * OC + (c + 1) * OC],
                                      start=(c == 0), stop=(c == NC_ - 1))
                    pe.inc(mm)

        @blk.scalar
        def _(s):
            for oc in range(NOC):
                for i in range(NT):
                    k = oc * NT + i
                    s.wait_ge(ll[k % 2].h, ld_tgt(k))
                    if k >= 2:
                        s.wait_ge(ve.h, ve0 + k - 1)
                    ac.inc(s.activation(gbuf[:, (k % 2) * OC:(k % 2 + 1) * OC],
                                        gcols[:, (k % 2) * OC:(k % 2 + 1) * OC],
                                        AF.Silu))

        @blk.vector
        def _(v):
            for oc in range(NOC):
                for i in range(NT):
                    k = oc * NT + i
                    v.wait_ge(pe.h, pe0 + k + 1)
                    v.wait_ge(ac.h, ac0 + k + 1)
                    if k >= 2:
                        v.wait_ge(po.h, po0 + k - 1)
                    ve.inc(v.tensor_tensor(tbuf[:, (k % 2) * OC:(k % 2 + 1) * OC],
                                           sPS[:, (k % 2) * 512:(k % 2) * 512 + OC],
                                           gbuf[:, (k % 2) * OC:(k % 2 + 1) * OC],
                                           OP.mult))

        @blk.gpsimd
        def _(p):
            _emit_wold(p, 0)
            _emit_wold(p, 1)
            for oc in range(NOC):
                for i in range(NT):
                    k = oc * NT + i
                    p.wait_ge(ve.h, ve0 + k + 1)
                    p.wait_ge(ll[k % 2].h, ld_tgt(k))
                    if k >= 2:
                        p.wait_ge(ol_[k % 2].h, out_tgt(k - 2))
                    po.inc(p.tensor_tensor(obuf[:, (k % 2) * OC:(k % 2 + 1) * OC],
                                           tbuf[:, (k % 2) * OC:(k % 2 + 1) * OC],
                                           xcols[:, (k % 2) * OC:(k % 2 + 1) * OC],
                                           OP.add))
                if oc + 2 < NOC:
                    _emit_wold(p, oc + 2)

        assert pe.v == pe0 + NK
        assert ve.v == ve0 + NK
        assert ac.v == ac0 + NK
        assert po.v == po0 + NK


def kernel(L0_attn_resid, L0_attn_Wq, L0_attn_Wk, L0_attn_Wv, L0_attn_Wo,
           L0_ffn_W_res, L0_ffn_W_pre, L0_ffn_W_post, L0_ffn_H_res_base,
           L0_ffn_H_pre_base, L0_ffn_H_post_base, L0_ffn_gate, L0_ffn_W_branch):
    x = np.ascontiguousarray(np.asarray(L0_attn_resid, dtype=np.float32))
    cos_fm, sin_fm = _rope_tables()
    wrppT = np.ascontiguousarray(
        np.concatenate([np.asarray(L0_ffn_W_res), np.asarray(L0_ffn_W_pre),
                        np.asarray(L0_ffn_W_post)], axis=0).T.astype(np.float32))
    base24 = np.concatenate([
        np.asarray(L0_ffn_H_res_base, dtype=np.float32).reshape(16),
        np.asarray(L0_ffn_H_pre_base, dtype=np.float32).reshape(4),
        np.asarray(L0_ffn_H_post_base, dtype=np.float32).reshape(4)]).reshape(1, 24)
    delta16 = np.eye(4, dtype=np.float32).reshape(1, 16)
    gate = np.asarray(L0_ffn_gate, dtype=np.float32).reshape(1, 1)

    common = {
        "wq": np.ascontiguousarray(L0_attn_Wq, dtype=np.float32),
        "wk": np.ascontiguousarray(L0_attn_Wk, dtype=np.float32),
        "wv": np.ascontiguousarray(L0_attn_Wv, dtype=np.float32),
        "wo": np.ascontiguousarray(L0_attn_Wo, dtype=np.float32),
        "wb": np.ascontiguousarray(L0_ffn_W_branch, dtype=np.float32),
        "wrppT": wrppT, "base24": base24, "delta16": delta16, "gate": gate,
        "cos_fm": cos_fm, "sin_fm": sin_fm,
        "ident": np.eye(128, dtype=ml_dtypes.bfloat16),
        "rmat": _rot_lhsT(), "tri": _tri_mask(),
    }
    in_maps = [dict(common, x=np.ascontiguousarray(x[b])) for b in range(B)]

    nc = build_nc()
    res = run_bass_kernel_spmd(nc, in_maps, core_ids=list(range(B)))
    return np.stack([np.asarray(r["out"], dtype=np.float32) for r in res.results], axis=0)
